# version 29
# baseline (speedup 1.0000x reference)
import numpy as np
from contextlib import ExitStack
import concourse.bass as bass
import concourse.mybir as mybir
from concourse.bass_utils import run_bass_kernel_spmd

F32 = mybir.dt.float32
BF16 = mybir.dt.bfloat16
I32 = mybir.dt.int32
AF = mybir.ActivationFunctionType
OP = mybir.AluOpType
AX = mybir.AxisListType

NCORE = 8
P = 128
TOK = 2048
D = 1024
NG = 8
DI = 2816
IQ = 704
NQ = 4
ASL = 256
NASL = TOK // ASL
NTC = TOK // P
NPR = NTC // 2
NID = DI // P
WGU = 352
C6TH = float(np.float32(1.0 / 6.0))
f32 = lambda v: float(np.float32(v))
af32 = np.float32


def build_nc(dbg=False):
    dk = (lambda: {"kind": "ExternalOutput"}) if dbg else (lambda: {})
    nc = bass.Bass("TRN2", target_bir_lowering=False, debug=False)

    x_tm = nc.dram_tensor("x_tm", [TOK, D], F32, kind="ExternalInput").ap()
    xT = nc.dram_tensor("xT", [D, TOK], F32, kind="ExternalInput").ap()
    wg_sl = nc.dram_tensor("wg_sl", [WGU, D], F32, kind="ExternalInput").ap()
    wu_sl = nc.dram_tensor("wu_sl", [WGU, D], F32, kind="ExternalInput").ap()
    wd_sl = nc.dram_tensor("wd_sl", [P, DI], F32, kind="ExternalInput").ap()
    w_in = nc.dram_tensor("w_in", [NG, P, P], F32, kind="ExternalInput").ap()
    w_a = nc.dram_tensor("w_a", [NG, P, P], F32, kind="ExternalInput").ap()
    apg = nc.dram_tensor("apg", [P, NG], F32, kind="ExternalInput").ap()
    selcol = nc.dram_tensor("selcol", [P, NG], F32, kind="ExternalInput").ap()
    identf = nc.dram_tensor("identf", [P, P], F32, kind="ExternalInput").ap()
    out = nc.dram_tensor("out", [TOK, D], F32, kind="ExternalOutput").ap()

    xn16_d = nc.dram_tensor("xn16_d", [D, TOK], BF16, **dk()).ap()
    xq16_d = nc.dram_tensor("xq16_d", [D, TOK], BF16, **dk()).ap()
    wgq_sl_d = nc.dram_tensor("wgq_sl_d", [WGU, D], BF16)
    wuq_sl_d = nc.dram_tensor("wuq_sl_d", [WGU, D], BF16)
    wdq_sl_d = nc.dram_tensor("wdq_sl_d", [P, DI], BF16)
    wgq_d = nc.dram_tensor("wgq_d", [DI, D], BF16)
    wuq_d = nc.dram_tensor("wuq_d", [DI, D], BF16)
    wdq_d = nc.dram_tensor("wdq_d", [D, DI], BF16)
    y_d = nc.dram_tensor("y_d", [NG, P, TOK], F32, **dk()).ap()
    p_d = nc.dram_tensor("p_d", [NG, P, TOK], F32, **dk()).ap()
    hq_d = nc.dram_tensor("hq_d", [TOK, DI], BF16, **dk()).ap()
    dbgcc_d = nc.dram_tensor("dbgcc_d", [P, 160], F32, **dk()).ap() if True else None
    dbgwg_d = nc.dram_tensor("dbgwg_d", [DI, D], BF16, **dk()).ap() if dbg else None
    dbgwu_d = nc.dram_tensor("dbgwu_d", [DI, D], BF16, **dk()).ap() if dbg else None
    dbgh_d = nc.dram_tensor("dbgh_d", [NTC * NQ, P, IQ], F32, **dk()).ap() if dbg else None
    dbgar1_d = nc.dram_tensor("dbgar1_d", [P, 16 * DI], BF16, **dk()).ap() if dbg else None
    dbgar2_d = nc.dram_tensor("dbgar2_d", [P, 16 * DI], BF16, **dk()).ap() if dbg else None
    cc_in = nc.dram_tensor("cc_in", [P, 16], F32)
    cc_out = nc.dram_tensor("cc_out", [NCORE * P, 16], F32)

    ctx = ExitStack()
    with ctx:
        def sb(name, shape, dt):
            return ctx.enter_context(nc.sbuf_tensor(name, shape, dt))

        AR_W = sb("AR_W", [P, 16 * DI], BF16)
        xqblk = [sb("xqblk0", [P, NG * P], BF16), sb("xqblk1", [P, NG * P], BF16)]

        qv = sb("qv", [P, 1024], F32)
        qi = sb("qi", [P, 1024], I32)
        qi2 = sb("qi2", [P, IQ], I32)
        qc1 = sb("qc1", [P, 1024], BF16)
        qc2 = sb("qc2", [P, 1024], BF16)
        qc3 = sb("qc3", [P, 1024], BF16)
        qa1 = sb("qa1", [P, 1024], BF16)
        qsg = sb("qsg", [P, 1024], BF16)
        qrse = sb("qrse", [P, 1024], F32)
        qav = qrse
        qsse = sb("qsse", [P, 1024], BF16)
        qbm = sb("qbm", [P, 192], F32)
        qsc = sb("qsc", [P, 192], F32)
        qrs = sb("qrs", [P, 192], F32)
        qsb = sb("qsb", [P, 192], BF16)
        qsb2 = sb("qsb2", [P, 192], BF16)

        xb0 = sb("xb0", [P, 1024], F32)
        xb1 = sb("xb1", [P, 1024], F32)
        xnb = sb("xnb", [P, 1024], F32)
        xn16t = [sb("xn16t0", [P, 1024], BF16), sb("xn16t1", [P, 1024], BF16)]
        xqo = [sb("xqo0", [P, 1024], BF16), sb("xqo1", [P, 1024], BF16)]
        lnscr = sb("lnscr", [P, 1024], F32)
        lnsx = [[sb(f"lnsx{pp}{s}", [P, 1], F32) for s in range(2)] for pp in range(2)]
        lnsq = [[sb(f"lnsq{pp}{s}", [P, 1], F32) for s in range(2)] for pp in range(2)]
        lnm = [[sb(f"lnm{pp}{s}", [P, 1], F32) for s in range(2)] for pp in range(2)]
        lnr1 = [[sb(f"lnr1{pp}{s}", [P, 1], F32) for s in range(2)] for pp in range(2)]
        lnrs = [[sb(f"lnrs{pp}{s}", [P, 1], F32) for s in range(2)] for pp in range(2)]
        lnm2 = [sb(f"lnm2{s}", [P, 1], F32) for s in range(2)]
        lnv0 = [sb(f"lnv0{s}", [P, 1], F32) for s in range(2)]
        lnv1 = [sb(f"lnv1{s}", [P, 1], F32) for s in range(2)]

        wbuf = sb("wbuf", [P, 1024], F32)
        wqo = xqblk[1]

        logaM = sb("logaM", [P, NG * ASL], F32)
        aM = sb("aM", [P, NG * ASL], F32)
        gxM = sb("gxM", [P, NG * ASL], F32)
        multM = sb("multM", [P, NG * ASL], F32)
        xnslM = sb("xnslM", [P, NG * ASL], BF16)
        bb1 = [sb("bb1a", [P, ASL], F32), sb("bb1b", [P, ASL], F32)]
        bb2 = [sb("bb2a", [P, ASL], F32), sb("bb2b", [P, ASL], F32)]
        yslb = [sb(f"ysl{i}", [P, ASL], F32) for i in range(4)]
        pslb = [sb(f"psl{i}", [P, ASL], F32) for i in range(4)]
        zasl = sb("zasl", [P, ASL], F32)
        ycol = sb("ycol", [P, NG], F32)
        pcol = sb("pcol", [P, NG], F32)
        winb = sb("winb", [P, NG * P], BF16)
        wab = sb("wab", [P, NG * P], BF16)
        spu = sb("spu", [P, NG], F32)
        spt1 = sb("spt1", [P, NG], F32)
        spt2 = sb("spt2", [P, NG], F32)
        spneg = sb("spneg", [P, NG], F32)
        selc = sb("selc", [P, NG], F32)
        idf = sb("idf", [P, P], F32)
        ccs = sb("ccs", [P, 16], F32)
        carr = sb("carr", [P, NCORE * 16], F32)
        cc0 = sb("cc0", [P, NG], F32)
        cc1 = sb("cc1", [P, NG], F32)
        cct = sb("cct", [P, NG], F32)
        cs0 = sb("cs0", [P, NG], F32)
        cs1 = sb("cs1", [P, NG], F32)
        selq = sb("selq", [P, 1], F32)
        cmq = [sb("cmqa", [P, 1], F32), sb("cmqb", [P, 1], F32)]

        hsil = xnb
        hful = lnscr
        hqo = xqo

        otc = [lambda: lnscr[:, 0:512], lambda: lnscr[:, 512:1024]]
        res1 = [lambda: qrse[:, 0:512], lambda: qrse[:, 512:1024]]
        res2 = [lambda: qv[:, 0:512], lambda: qv[:, 512:1024]]

        PS = [ctx.enter_context(nc.psum_tensor(f"PS{i}", [P, 1024], F32)) for i in range(4)]

        sem_names = [
            "xld0", "xld1", "mld", "mld2", "wld", "spv", "tms", "tmv", "tms2", "tmv2",
            "xnst0", "xnst1",
            "wqv", "wqst", "ccw", "xcl0", "xcl1", "att", "ats", "atsB", "atsC", "atf",
            "atv", "atvB", "yb0", "yb1", "yb2", "yb3",
            "ccv", "ccc", "ccl", "ccc2", "crv", "xqcl0", "xqcl1", "wqtl", "wdl", "gut", "gus",
            "guh", "hqv", "hst0", "hst1", "dcl0", "dcl1", "rld", "dnt", "dnv", "trt", "otv",
            "ost", "qvs", "qss", "xnv", "xqv", "ttr", "tev", "vfn",
        ]
        sems = {n: ctx.enter_context(nc.semaphore(n)) for n in sem_names}

        ENG = ("sync", "vector", "scalar", "tensor", "gpsimd")
        sched = {e: [] for e in ENG}
        cnt = {n: 0 for n in sem_names}

        def op(eng, build, sem=None, k=1, drain=False, fence=False):
            if fence and sem is None:
                sem, k, drain = "vfn", 1, True
            if sem is not None:
                cnt[sem] += k

            def fn(E, build=build, sem=sem, k=k, drain=drain):
                ins = build(E)
                if drain:
                    d = E.drain()
                    if sem is not None:
                        d.then_inc(sems[sem], k)
                elif sem is not None:
                    ins.then_inc(sems[sem], k)
            sched[eng].append(fn)
            return None

        WAITLOG = []

        def wait(eng, sem, val=None):
            v = cnt[sem] if val is None else val
            if v <= 0:
                return
            WAITLOG.append((eng, sem, v, len(sched[eng])))
            sched[eng].append(lambda E, sem=sem, v=v: E.wait_ge(sems[sem], v))

        def cur(sem):
            return cnt[sem]

        def emit_quant2(rows, width, src, dst, dst_sem=None, dst_k=1, soff=True):
            nb = width // 16
            R = rows
            V = lambda fn, dr=False, sem=None, k=1: op("vector", fn, drain=dr, sem=sem, k=k)
            S = lambda fn, dr=False, sem=None: op("scalar", fn, drain=dr, sem=sem)
            V(lambda E: E.reduce_max(
                qbm[0:R, 0:nb].unsqueeze(2).unsqueeze(3),
                src(0, width).rearrange("p (a b c) -> p a b c", b=4, c=4),
                axis=AX.XY, apply_absolute_value=True), dr=True)
            V(lambda E: E.tensor_scalar(qsc[0:R, 0:nb], qbm[0:R, 0:nb], C6TH, 1e-12, OP.mult, OP.max), dr=True)
            V(lambda E: E.reciprocal(qrs[0:R, 0:nb], qsc[0:R, 0:nb]))
            V(lambda E: E.tensor_copy(qsb[0:R, 0:nb], qsc[0:R, 0:nb]), dr=True)
            V(lambda E: E.tensor_copy(
                qrse[0:R, 0:width].rearrange("p (a b) -> p a b", a=nb),
                qrs[0:R, 0:nb].unsqueeze(2).broadcast_to([R, nb, 16])))
            V(lambda E: E.tensor_copy(
                qsse[0:R, 0:width].rearrange("p (a b) -> p a b", a=nb),
                qsb[0:R, 0:nb].unsqueeze(2).broadcast_to([R, nb, 16])), dr=True)
            V(lambda E: E.tensor_tensor(qv[0:R, 0:width], src(0, width), qrse[0:R, 0:width], OP.mult),
              dr=True, sem="qvs")
            wait("scalar", "qvs")
            if soff:
                S(lambda E: E.activation(qav[0:R, 0:width], qv[0:R, 0:width], AF.Abs), dr=True)
                S(lambda E: E.activation(qsg[0:R, 0:width], qv[0:R, 0:width], AF.Sign))
                S(lambda E: E.activation(qi[0:R, 0:width], qav[0:R, 0:width], AF.Copy,
                                         scale=af32(2.0)), dr=True)
                S(lambda E: E.activation(qa1[0:R, 0:width], qi[0:R, 0:width], AF.Copy,
                                         scale=af32(0.5)), dr=True)
                S(lambda E: E.activation(qi[0:R, 0:width], qav[0:R, 0:width], AF.Copy,
                                         scale=af32(1.0)), dr=True)
                S(lambda E: E.activation(qc2[0:R, 0:width], qi[0:R, 0:width], AF.Copy,
                                         scale=af32(1.0)), dr=True)
                S(lambda E: E.activation(qi[0:R, 0:width], qav[0:R, 0:width], AF.Copy,
                                         scale=af32(0.125), bias=float(-0.125)), dr=True)
                S(lambda E: E.activation(qc3[0:R, 0:width], qi[0:R, 0:width], AF.Copy,
                                         scale=af32(2.0)), dr=True, sem="qss")
                wait("vector", "qss")
                V(lambda E: E.tensor_scalar(qc1[0:R, 0:width], qa1[0:R, 0:width], f32(2.0), None, OP.min), dr=True)
                V(lambda E: E.tensor_scalar(qa1[0:R, 0:width], qc2[0:R, 0:width], f32(2.0), f32(4.0), OP.max, OP.min), dr=True)
                V(lambda E: E.tensor_scalar(qc2[0:R, 0:width], qc3[0:R, 0:width], f32(-2.0), None, OP.add), dr=True)
                V(lambda E: E.tensor_tensor(qc3[0:R, 0:width], qc1[0:R, 0:width], qa1[0:R, 0:width], OP.add), dr=True)
                V(lambda E: E.tensor_tensor(qc1[0:R, 0:width], qc3[0:R, 0:width], qc2[0:R, 0:width], OP.add), dr=True)
                V(lambda E: E.tensor_tensor(qa1[0:R, 0:width], qc1[0:R, 0:width], qsg[0:R, 0:width], OP.mult), dr=True)
                V(lambda E: E.tensor_tensor(dst(0, width), qa1[0:R, 0:width], qsse[0:R, 0:width], OP.mult),
                  dr=True, sem=dst_sem, k=dst_k)
            else:
                S(lambda E: E.activation(qav[0:R, 0:width], qv[0:R, 0:width], AF.Abs))
                S(lambda E: E.activation(qsg[0:R, 0:width], qv[0:R, 0:width], AF.Sign),
                  dr=True, sem="qss")
                wait("vector", "qss")
                V(lambda E: E.tensor_scalar(qi[0:R, 0:width], qav[0:R, 0:width], f32(2.0), None, OP.mult), dr=True)
                V(lambda E: E.tensor_copy(qa1[0:R, 0:width], qi[0:R, 0:width]), dr=True)
                V(lambda E: E.tensor_scalar(qi[0:R, 0:width], qav[0:R, 0:width], f32(1.0), None, OP.mult))
                V(lambda E: E.tensor_scalar(qc1[0:R, 0:width], qa1[0:R, 0:width], f32(4.0), f32(0.5), OP.min, OP.mult), dr=True)
                V(lambda E: E.tensor_copy(qa1[0:R, 0:width], qi[0:R, 0:width]), dr=True)
                V(lambda E: E.tensor_scalar(qi[0:R, 0:width], qav[0:R, 0:width], f32(0.125), f32(-0.125), OP.mult, OP.add))
                V(lambda E: E.tensor_scalar(qc2[0:R, 0:width], qa1[0:R, 0:width], f32(2.0), f32(4.0), OP.max, OP.min), dr=True)
                V(lambda E: E.tensor_copy(qa1[0:R, 0:width], qi[0:R, 0:width]), dr=True)
                V(lambda E: E.tensor_scalar(qc3[0:R, 0:width], qa1[0:R, 0:width], f32(2.0), f32(-2.0), OP.mult, OP.add), dr=True)
                V(lambda E: E.tensor_tensor(qa1[0:R, 0:width], qc1[0:R, 0:width], qc2[0:R, 0:width], OP.add), dr=True)
                V(lambda E: E.tensor_tensor(qc1[0:R, 0:width], qa1[0:R, 0:width], qc3[0:R, 0:width], OP.add), dr=True)
                V(lambda E: E.tensor_tensor(qc2[0:R, 0:width], qc1[0:R, 0:width], qsg[0:R, 0:width], OP.mult), dr=True)
                V(lambda E: E.tensor_tensor(dst(0, width), qc2[0:R, 0:width], qsse[0:R, 0:width], OP.mult),
                  dr=True, sem=dst_sem, k=dst_k)

        def qp_head(T, R, W, srcf):
            V = lambda fn, dr=False, sem=None: op("vector", fn, drain=dr, sem=sem)
            nb = W // 16
            V(lambda E: E.reduce_max(
                qbm[0:R, 0:nb].unsqueeze(2).unsqueeze(3),
                srcf().rearrange("p (a b c) -> p a b c", b=4, c=4),
                axis=AX.XY, apply_absolute_value=True), dr=True)
            V(lambda E: E.tensor_scalar(qsc[0:R, 0:nb], qbm[0:R, 0:nb], C6TH, 1e-12, OP.mult, OP.max), dr=True)
            V(lambda E: E.reciprocal(qrs[0:R, 0:nb], qsc[0:R, 0:nb]))
            V(lambda E: E.tensor_copy(T["qsb"](R, nb), qsc[0:R, 0:nb]), dr=True)
            V(lambda E: E.tensor_copy(
                qrse[0:R, 0:W].rearrange("p (a b) -> p a b", a=nb),
                qrs[0:R, 0:nb].unsqueeze(2).broadcast_to([R, nb, 16])), dr=True)
            V(lambda E: E.tensor_tensor(T["qv"](R, W), srcf(), qrse[0:R, 0:W], OP.mult),
              dr=True, sem="qvs")
            return cur("qvs")

        def qp_schain_gu(T, R, W, vqs):
            S = lambda fn, dr=False, sem=None: op("scalar", fn, drain=dr, sem=sem)
            wait("scalar", "qvs", vqs)
            S(lambda E: E.activation(T["qav"](R, W), T["qv"](R, W), AF.Abs))
            S(lambda E: E.activation(T["qsg"](R, W), T["qv"](R, W), AF.Sign))
            S(lambda E: E.activation(qi[0:R, 0:W], T["qv"](R, W), AF.Abs, scale=af32(2.0)))
            S(lambda E: E.activation(qi2[0:R, 0:W], T["qv"](R, W), AF.Abs, scale=af32(1.0)), dr=True)
            S(lambda E: E.activation(T["qa1"](R, W), qi[0:R, 0:W], AF.Copy, scale=af32(0.5)))
            S(lambda E: E.activation(T["qc2"](R, W), qi2[0:R, 0:W], AF.Copy, scale=af32(1.0)), dr=True)
            S(lambda E: E.activation(qi[0:R, 0:W], T["qav"](R, W), AF.Copy,
                                     scale=af32(0.125), bias=float(-0.125)), dr=True)
            S(lambda E: E.activation(T["qc3"](R, W), qi[0:R, 0:W], AF.Copy, scale=af32(2.0)),
              dr=True, sem="qss")
            return cur("qss")

        def qp_schain(T, R, W, vqs):
            S = lambda fn, dr=False, sem=None: op("scalar", fn, drain=dr, sem=sem)
            wait("scalar", "qvs", vqs)
            S(lambda E: E.activation(T["qav"](R, W), T["qv"](R, W), AF.Abs), dr=True)
            S(lambda E: E.activation(T["qsg"](R, W), T["qv"](R, W), AF.Sign))
            S(lambda E: E.activation(qi[0:R, 0:W], T["qav"](R, W), AF.Copy, scale=af32(2.0)), dr=True)
            S(lambda E: E.activation(T["qa1"](R, W), qi[0:R, 0:W], AF.Copy, scale=af32(0.5)), dr=True)
            S(lambda E: E.activation(qi[0:R, 0:W], T["qav"](R, W), AF.Copy, scale=af32(1.0)), dr=True)
            S(lambda E: E.activation(T["qc2"](R, W), qi[0:R, 0:W], AF.Copy, scale=af32(1.0)), dr=True)
            S(lambda E: E.activation(qi[0:R, 0:W], T["qav"](R, W), AF.Copy,
                                     scale=af32(0.125), bias=float(-0.125)), dr=True)
            S(lambda E: E.activation(T["qc3"](R, W), qi[0:R, 0:W], AF.Copy, scale=af32(2.0)),
              dr=True, sem="qss")
            return cur("qss")

        def qp_tail(T, R, W, dstf, dst_sem, dst_k, sqs):
            V = lambda fn, dr=False, sem=None, k=1: op("vector", fn, drain=dr, sem=sem, k=k)
            nb = W // 16
            wait("vector", "qss", sqs)
            V(lambda E: E.tensor_scalar(qc1[0:R, 0:W], T["qa1"](R, W), f32(2.0), None, OP.min), dr=True)
            V(lambda E: E.tensor_scalar(T["qa1"](R, W), T["qc2"](R, W), f32(2.0), f32(4.0), OP.max, OP.min), dr=True)
            V(lambda E: E.tensor_scalar(T["qc2"](R, W), T["qc3"](R, W), f32(-2.0), None, OP.add), dr=True)
            V(lambda E: E.tensor_tensor(T["qc3"](R, W), qc1[0:R, 0:W], T["qa1"](R, W), OP.add), dr=True)
            V(lambda E: E.tensor_copy(
                qsse[0:R, 0:W].rearrange("p (a b) -> p a b", a=nb),
                T["qsb"](R, nb).unsqueeze(2).broadcast_to([R, nb, 16])))
            V(lambda E: E.tensor_tensor(qc1[0:R, 0:W], T["qc3"](R, W), T["qc2"](R, W), OP.add), dr=True)
            V(lambda E: E.tensor_tensor(T["qa1"](R, W), qc1[0:R, 0:W], T["qsg"](R, W), OP.mult), dr=True)
            V(lambda E: E.tensor_tensor(dstf(), T["qa1"](R, W), qsse[0:R, 0:W], OP.mult),
              dr=True, sem=dst_sem, k=dst_k)

        def qp_schain_lite(T, R, W, vqs):
            S = lambda fn, dr=False, sem=None: op("scalar", fn, drain=dr, sem=sem)
            wait("scalar", "qvs", vqs)
            S(lambda E: E.activation(T["qav"](R, W), T["qv"](R, W), AF.Abs))
            S(lambda E: E.activation(T["qsg"](R, W), T["qv"](R, W), AF.Sign),
              dr=True, sem="qss")
            return cur("qss")

        def qp_tail_v(T, R, W, dstf, dst_sem, dst_k, sqs):
            V = lambda fn, dr=False, sem=None, k=1: op("vector", fn, drain=dr, sem=sem, k=k)
            nb = W // 16
            wait("vector", "qss", sqs)
            V(lambda E: E.tensor_scalar(qi[0:R, 0:W], T["qav"](R, W), f32(2.0), None, OP.mult), dr=True)
            V(lambda E: E.tensor_copy(qa1[0:R, 0:W], qi[0:R, 0:W]), dr=True)
            V(lambda E: E.tensor_scalar(qi[0:R, 0:W], T["qav"](R, W), f32(1.0), None, OP.mult))
            V(lambda E: E.tensor_scalar(qc1[0:R, 0:W], qa1[0:R, 0:W], f32(4.0), f32(0.5), OP.min, OP.mult), dr=True)
            V(lambda E: E.tensor_copy(qa1[0:R, 0:W], qi[0:R, 0:W]), dr=True)
            V(lambda E: E.tensor_scalar(qi[0:R, 0:W], T["qav"](R, W), f32(0.125), f32(-0.125), OP.mult, OP.add))
            V(lambda E: E.tensor_scalar(qc2[0:R, 0:W], qa1[0:R, 0:W], f32(2.0), f32(4.0), OP.max, OP.min), dr=True)
            V(lambda E: E.tensor_copy(qa1[0:R, 0:W], qi[0:R, 0:W]), dr=True)
            V(lambda E: E.tensor_scalar(qc3[0:R, 0:W], qa1[0:R, 0:W], f32(2.0), f32(-2.0), OP.mult, OP.add), dr=True)
            V(lambda E: E.tensor_tensor(qa1[0:R, 0:W], qc1[0:R, 0:W], qc2[0:R, 0:W], OP.add), dr=True)
            V(lambda E: E.tensor_copy(
                qsse[0:R, 0:W].rearrange("p (a b) -> p a b", a=nb),
                T["qsb"](R, nb).unsqueeze(2).broadcast_to([R, nb, 16])))
            V(lambda E: E.tensor_tensor(qc1[0:R, 0:W], qa1[0:R, 0:W], qc3[0:R, 0:W], OP.add), dr=True)
            V(lambda E: E.tensor_tensor(qc2[0:R, 0:W], qc1[0:R, 0:W], T["qsg"](R, W), OP.mult), dr=True)
            V(lambda E: E.tensor_tensor(dstf(), qc2[0:R, 0:W], qsse[0:R, 0:W], OP.mult),
              dr=True, sem=dst_sem, k=dst_k)

        TL_TM = [
            {"qv": lambda R, W: qv[0:R, 0:W], "qav": lambda R, W: aM[0:R, 0:W],
             "qa1": lambda R, W: qa1[0:R, 0:W], "qc2": lambda R, W: qc2[0:R, 0:W],
             "qc3": lambda R, W: qc3[0:R, 0:W], "qsg": lambda R, W: qsg[0:R, 0:W],
             "qsb": lambda R, nb: qsb[0:R, 0:nb]},
            {"qv": lambda R, W: logaM[0:R, 0:W], "qav": lambda R, W: aM[0:R, 1024:1024 + W],
             "qa1": lambda R, W: xqblk[0][0:R, 0:W], "qc2": lambda R, W: xqblk[1][0:R, 0:W],
             "qc3": lambda R, W: xnslM[0:R, 1024:1024 + W], "qsg": lambda R, W: winb[0:R, 0:W],
             "qsb": lambda R, nb: qsb2[0:R, 0:nb]},
        ]
        TL_W = [
            {"qv": lambda R, W: qv[0:R, 0:W], "qav": lambda R, W: aM[0:R, 0:W],
             "qsg": lambda R, W: qsg[0:R, 0:W], "qsb": lambda R, nb: qsb[0:R, 0:nb]},
            {"qv": lambda R, W: xb0[0:R, 0:W], "qav": lambda R, W: xb1[0:R, 0:W],
             "qsg": lambda R, W: xqblk[0][0:R, 0:W], "qsb": lambda R, nb: qsb2[0:R, 0:nb]},
        ]
        TL_GU = [
            TL_TM[0],
            {"qv": lambda R, W: logaM[0:R, 0:W], "qav": lambda R, W: aM[0:R, 1024:1024 + W],
             "qa1": lambda R, W: winb[0:R, 0:W], "qc2": lambda R, W: wab[0:R, 0:W],
             "qc3": lambda R, W: xn16t[0][0:R, 0:W], "qsg": lambda R, W: xn16t[1][0:R, 0:W],
             "qsb": lambda R, nb: qsb2[0:R, 0:nb]},
        ]

        op("sync", lambda E: E.dma_start(out=idf[:], in_=identf), sem="mld", k=16)
        op("sync", lambda E: E.dma_start(out=selc[:], in_=selcol), sem="mld", k=16)
        op("sync", lambda E: E.dma_start(out=spu[:], in_=apg), sem="mld", k=16)
        def emit_mld2():
            wait("gpsimd", "xnst0", 256)
            wait("gpsimd", "xnst1", 256)
            op("gpsimd", lambda E: E.dma_start(
                out=winb[:].rearrange("d (h e) -> d h e", h=NG),
                in_=w_in.rearrange("h d e -> d h e")), sem="mld2", k=16)
            op("gpsimd", lambda E: E.dma_start(
                out=wab[:].rearrange("d (h e) -> d h e", h=NG),
                in_=w_a.rearrange("h d e -> d h e")), sem="mld2", k=16)
        def emit_xloads(pr):
            op("sync", lambda E, pr=pr: E.dma_start(
                out=xb0[:], in_=x_tm[2 * pr * P:(2 * pr + 1) * P, :]), sem="xld0", k=16)
            op("sync", lambda E, pr=pr: E.dma_start(
                out=xb1[:], in_=x_tm[(2 * pr + 1) * P:(2 * pr + 2) * P, :]), sem="xld1", k=16)
        emit_xloads(0)

        wait("scalar", "mld", 48)
        op("scalar", lambda E: E.activation(spt1[:], spu[:], AF.Exp), sem="spv", drain=True)
        wait("vector", "spv")
        op("vector", lambda E: E.tensor_scalar(spt2[:], spt1[:], f32(-0.25), f32(1.0 / 3.0), OP.mult, OP.add), fence=True)
        op("vector", lambda E: E.scalar_tensor_tensor(spneg[:], spt2[:], f32(-1.0), spt1[:], OP.mult, OP.mult), fence=True)
        op("vector", lambda E: E.tensor_scalar(spt2[:], spneg[:], f32(0.5), None, OP.add), fence=True)
        op("vector", lambda E: E.scalar_tensor_tensor(spneg[:], spt2[:], f32(-1.0), spt1[:], OP.mult, OP.mult), fence=True)
        op("vector", lambda E: E.tensor_scalar(spt2[:], spneg[:], f32(1.0), None, OP.add), fence=True)
        op("vector", lambda E: E.tensor_mul(spu[:], spt2[:], spt1[:]), fence=True)
        op("vector", lambda E: E.tensor_scalar(spneg[:], spu[:], f32(-8.0), None, OP.mult), fence=True)
        op("vector", lambda E: E.memset(zasl[:], 0.0), fence=True)
        op("vector", lambda E: E.memset(xnslM[:, 0:1024], 1.0), sem="spv", drain=True)

        XB = [xb0, xb1]
        SCR2 = [lnscr, xnb]
        SSC = [qv, qrse]
        QPEND = []
        TTRXN, TEVXN, TTRXQ, TEVXQ = {}, {}, {}, {}

        def pend_pop():
            ent = QPEND.pop(0)
            if ent[0] == "tm":
                _, tc2, s2, vqs2 = ent
                sqs2 = qp_schain(TL_TM[s2], P, D, vqs2)
                qp_tail(TL_TM[s2], P, D, lambda s2=s2: gxM[:, s2 * 1024:(s2 + 1) * 1024],
                        "xqv", 1, sqs2)
                wait("tensor", "xqv", tc2 + 1)
                if tc2 >= 1:
                    wait("tensor", "tev", TEVXQ[tc2 - 1])
                for b in range(8):
                    op("tensor", lambda E, s2=s2, b=b: E.transpose(
                        PS[3][:, b * P:(b + 1) * P], gxM[:, s2 * 1024 + b * P:s2 * 1024 + (b + 1) * P],
                        idf[:]), sem=("ttr" if b == 7 else None))
                TTRXQ[tc2] = cur("ttr")
                wait("scalar", "ttr", TTRXQ[tc2])
                wait("scalar", f"xnst{s2}", 32 * (tc2 // 2))
                op("scalar", lambda E, s2=s2: E.activation(
                    xqo[s2][:, 0:D], PS[3][:, 0:D], AF.Copy), sem="tev", drain=True)
                TEVXQ[tc2] = cur("tev")
                wait("sync", "tev", TEVXQ[tc2])
                op("sync", lambda E, tc2=tc2, s2=s2: E.dma_start(
                    out=xq16_d[:, tc2 * P:(tc2 + 1) * P].rearrange("(b d) t -> d b t", b=8),
                    in_=xqo[s2][:].rearrange("p (b t) -> p b t", b=8)), sem=f"xnst{s2}", k=16)
            else:
                _, k2, rr2, ww2, dst2, vqs2 = ent
                p2 = k2 % 2
                sqs2 = qp_schain_lite(TL_W[p2], rr2, ww2, vqs2)
                wait("vector", "wqst", 16 * k2)
                qp_tail_v(TL_W[p2], rr2, ww2,
                          lambda rr2=rr2, ww2=ww2: wqo[0:rr2, 0:ww2], "wqv", 1, sqs2)
                wait("gpsimd", "wqv")
                op("gpsimd", lambda E, dst2=dst2, rr2=rr2, ww2=ww2: E.dma_start(
                    out=dst2, in_=wqo[0:rr2, 0:ww2]), sem="wqst", k=16)

        for it in range(NPR):
            if True:
                pr = it
                pp = pr % 2
                wait("vector", "xld0", 16 * (pr + 1))
                wait("vector", "xld1", 16 * (pr + 1))
                if pr >= 2:
                    wait("vector", "tms2", pr - 1)
                for s in range(2):
                    op("vector", lambda E, s=s: E.tensor_tensor(
                        SCR2[s][:], XB[s][:], XB[s][:], OP.mult))
                for s in range(2):
                    op("vector", lambda E, s=s: E.tensor_tensor_scan(
                        SSC[s][:], xnslM[:, 0:1024], XB[s][:], 0.0, OP.mult, OP.add), drain=(s == 1))
                for s in range(2):
                    op("vector", lambda E, s=s, pp=pp: E.tensor_copy(
                        lnsx[pp][s][:], SSC[s][:, 1023:1024]), drain=(s == 1))
                for s in range(2):
                    op("vector", lambda E, s=s: E.tensor_tensor_scan(
                        SSC[s][:], xnslM[:, 0:1024], SCR2[s][:], 0.0, OP.mult, OP.add), drain=(s == 1))
                for s in range(2):
                    op("vector", lambda E, s=s, pp=pp: E.tensor_copy(
                        lnsq[pp][s][:], SSC[s][:, 1023:1024]))
                for s in range(2):
                    op("vector", lambda E, s=s, pp=pp: E.tensor_scalar(
                        lnm[pp][s][:], lnsx[pp][s][:], f32(1.0 / D), None, OP.mult), drain=(s == 1))
                for s in range(2):
                    op("vector", lambda E, s=s, pp=pp: E.tensor_mul(
                        lnm2[s][:], lnm[pp][s][:], lnm[pp][s][:]), drain=(s == 1))
                for s in range(2):
                    op("vector", lambda E, s=s, pp=pp: E.scalar_tensor_tensor(
                        lnv0[s][:], lnsq[pp][s][:], f32(1.0 / D), lnm2[s][:], OP.mult, OP.subtract), drain=(s == 1))
                for s in range(2):
                    op("vector", lambda E, s=s: E.tensor_scalar(
                        lnv1[s][:], lnv0[s][:], f32(1e-5), None, OP.add), drain=(s == 1))
                for s in range(2):
                    op("vector", lambda E, s=s, pp=pp: E.reciprocal(
                        lnr1[pp][s][:], lnv1[s][:]),
                       sem=("tmv" if s == 1 else None), drain=(s == 1))
                wait("scalar", "tmv")
                for s in range(2):
                    op("scalar", lambda E, s=s, pp=pp: E.activation(
                        lnrs[pp][s][:], lnr1[pp][s][:], AF.Sqrt),
                       sem=("tms2" if s == 1 else None), drain=(s == 1))
            if True:
                pr = it
                pp = pr % 2
                wait("vector", "tms2", pr + 1)

                for s in range(2):
                    tc = 2 * pr + s
                    if tc >= 1:
                        wait("vector", "ttr", TTRXN[tc - 1])
                    for h in range(2):
                        op("vector", lambda E, s=s, pp=pp, h=h: E.tensor_scalar(
                            xnb[:, h * 512:(h + 1) * 512], XB[s][:, h * 512:(h + 1) * 512],
                            lnm[pp][s][:], lnrs[pp][s][:], OP.subtract, OP.mult),
                           sem=("xnv" if h == 1 else None), k=1, drain=(h == 1))
                    vqs = qp_head(TL_TM[s], P, D, lambda: xnb[:, 0:D])
                    if tc == 0:
                        wait("tensor", "mld", 48)
                    wait("tensor", "xnv", tc + 1)
                    if tc >= 1:
                        wait("tensor", "tev", TEVXN[tc - 1])
                    for b in range(8):
                        op("tensor", lambda E, b=b: E.transpose(
                            PS[2][:, b * P:(b + 1) * P], xnb[:, b * P:(b + 1) * P],
                            idf[:]), sem=("ttr" if b == 7 else None))
                    TTRXN[tc] = cur("ttr")
                    wait("scalar", "ttr", TTRXN[tc])
                    wait("scalar", f"xnst{s}", 32 * pr)
                    op("scalar", lambda E, s=s: E.activation(
                        xn16t[s][:, 0:D], PS[2][:, 0:D], AF.Copy), sem="tev", drain=True)
                    TEVXN[tc] = cur("tev")
                    wait("sync", "tev", TEVXN[tc])
                    op("sync", lambda E, tc=tc, s=s: E.dma_start(
                        out=xn16_d[:, tc * P:(tc + 1) * P].rearrange("(b d) t -> d b t", b=8),
                        in_=xn16t[s][:].rearrange("p (b t) -> p b t", b=8)), sem=f"xnst{s}", k=16)
                    if s == 1 and pr + 1 < NPR:
                        wait("sync", "xnv", cur("xnv"))
                        emit_xloads(pr + 1)
                    if QPEND:
                        pend_pop()
                    QPEND.append(("tm", tc, s, vqs))

        wsubs = []
        for j in range(3):
            rr = min(P, WGU - j * P)
            wsubs.append((wg_sl[j * P:j * P + rr, :], rr, 1024,
                          wgq_sl_d.ap()[j * P:j * P + rr, :]))
        for j in range(3):
            rr = min(P, WGU - j * P)
            wsubs.append((wu_sl[j * P:j * P + rr, :], rr, 1024,
                          wuq_sl_d.ap()[j * P:j * P + rr, :]))
        for j in range(NQ):
            wsubs.append((wd_sl[:, j * IQ:(j + 1) * IQ], P, IQ,
                          wdq_sl_d.ap()[:, j * IQ:(j + 1) * IQ]))
        wvqs_prev = None
        for k, (wsrc, rr, ww, dst) in enumerate(wsubs):
            if wvqs_prev is not None:
                wait("sync", "qvs", wvqs_prev)
            op("sync", lambda E, wsrc=wsrc, rr=rr, ww=ww: E.dma_start(
                out=wbuf[0:rr, 0:ww], in_=wsrc), sem="wld", k=16)
            wait("vector", "wld", 16 * (k + 1))
            wvqs_prev = qp_head(TL_W[k % 2], rr, ww, lambda rr=rr, ww=ww: wbuf[0:rr, 0:ww])
            if QPEND:
                pend_pop()
            QPEND.append(("w", k, rr, ww, dst, wvqs_prev))
        while QPEND:
            pend_pop()
        emit_mld2()
        wait("gpsimd", "wqst")
        op("gpsimd", lambda E: E.collective_compute(
            "AllGather", OP.bypass, replica_groups=[list(range(NCORE))],
            ins=[wgq_sl_d.ap().opt()], outs=[wgq_d.ap().opt()]), sem="ccw")
        wait("gpsimd", "ccw", 1)
        op("gpsimd", lambda E: E.collective_compute(
            "AllGather", OP.bypass, replica_groups=[list(range(NCORE))],
            ins=[wuq_sl_d.ap().opt()], outs=[wuq_d.ap().opt()]), sem="ccw")
        wait("gpsimd", "ccw", 2)
        op("gpsimd", lambda E: E.collective_compute(
            "AllGather", OP.bypass, replica_groups=[list(range(NCORE))],
            ins=[wdq_sl_d.ap().opt()], outs=[wdq_d.ap().opt()]), sem="ccw")

        if dbg:
            wait("gpsimd", "ccw", 2)
            op("gpsimd", lambda E: E.dma_start(out=dbgwg_d, in_=wgq_d.ap()), sem="ccv", k=16)
            op("gpsimd", lambda E: E.dma_start(out=dbgwu_d, in_=wuq_d.ap()), sem="ccv", k=16)

        ybcnt = [0, 0, 0, 0]

        def xnsl(sl, g):
            if sl % 2 == 0:
                return xnslM[:, g * ASL:(g + 1) * ASL]
            if g < 4:
                return xn16t[0][:, g * ASL:(g + 1) * ASL]
            return xn16t[1][:, (g - 4) * ASL:(g - 3) * ASL]

        def emit_xcl(sl):
            if sl >= NASL:
                return
            wait("sync", "xnst0")
            wait("sync", "xnst1")
            if sl > 1:
                wait("sync", "atv", 8 * (sl - 1))
            if sl % 2 == 0:
                op("sync", lambda E, sl=sl: E.dma_start(
                    out=xnslM[:].rearrange("p (g t) -> p g t", g=NG),
                    in_=xn16_d[:, sl * ASL:(sl + 1) * ASL].rearrange("(g p) t -> p g t", g=NG)),
                   sem="xcl0", k=16)
            else:
                for hh in range(2):
                    op("sync", lambda E, sl=sl, hh=hh: E.dma_start(
                        out=xn16t[hh][:].rearrange("p (g t) -> p g t", g=4),
                        in_=xn16_d[hh * 512:(hh + 1) * 512,
                                   sl * ASL:(sl + 1) * ASL].rearrange("(g p) t -> p g t", g=4)),
                       sem="xcl1", k=16)
        emit_xcl(0)
        emit_xcl(1)

        def emit_wqtl_g():
            wait("sync", "ccw", 1)
            for d in range(NG):
                op("sync", lambda E, d=d: E.dma_start_transpose(
                    AR_W[:, d * DI:(d + 1) * DI], wgq_d.ap()[:, d * P:(d + 1) * P]), sem="wqtl", k=16)

        def emit_wqtl_u():
            wait("sync", "ccw", 2)
            for d in range(NG):
                op("sync", lambda E, d=d: E.dma_start_transpose(
                    AR_W[:, (NG + d) * DI:(NG + d + 1) * DI], wuq_d.ap()[:, d * P:(d + 1) * P]), sem="wqtl", k=16)

        for sl in range(NASL):
            if sl == 3:
                emit_wqtl_g()
            if sl == 6:
                emit_wqtl_u()
            if sl == 0:
                wait("tensor", "tev", 32)
            wait("tensor", "mld", 48)
            wait("tensor", "mld2", 32)
            wait("tensor", f"xcl{sl % 2}", (16 if sl % 2 == 0 else 32) * (sl // 2 + 1))
            if sl > 0:
                wait("tensor", "ats", 8 * sl)
            aps = lambda g: PS[g // 2][:, (g % 2) * 512:(g % 2) * 512 + ASL]
            xps = lambda g: PS[g // 2][:, (g % 2) * 512 + ASL:(g % 2) * 512 + 2 * ASL]
            for g in range(NG):
                op("tensor", lambda E, sl=sl, g=g: E.matmul(
                    aps(g), wab[:, g * P:(g + 1) * P],
                    xnsl(sl, g), start=True, stop=True))
                op("tensor", lambda E, sl=sl, g=g: E.matmul(
                    xps(g), winb[:, g * P:(g + 1) * P],
                    xnsl(sl, g), start=True, stop=True),
                   sem=("att" if g == NG - 1 else None))
            wait("scalar", "att", sl + 1)
            if sl > 0:
                wait("scalar", "atf", sl)
            for g in range(NG):
                op("scalar", lambda E, g=g: E.activation(bb1[g % 2][:], aps(g), AF.Sigmoid))
                op("scalar", lambda E, g=g: E.activation(
                    gxM[:, g * ASL:(g + 1) * ASL], xps(g), AF.Sigmoid), drain=True)
                op("scalar", lambda E, g=g: E.activation(
                    logaM[:, g * ASL:(g + 1) * ASL], bb1[g % 2][:], AF.Copy,
                    scale=spneg[:, g:g + 1]), sem="ats", drain=(g == NG - 1))
            for g in range(NG):
                op("scalar", lambda E, g=g: E.activation(
                    multM[:, g * ASL:(g + 1) * ASL], logaM[:, g * ASL:(g + 1) * ASL], AF.Exp),
                   sem=("atsB" if g == NG - 1 else None), drain=(g == NG - 1))
            wait("vector", "atsB")
            op("vector", lambda E: E.tensor_scalar(aM[:], multM[:], 1.0, None, OP.min),
               sem="atvB", drain=True)
            wait("scalar", "atvB")
            for g in range(NG):
                op("scalar", lambda E, g=g: E.activation(
                    logaM[:, g * ASL:(g + 1) * ASL], aM[:, g * ASL:(g + 1) * ASL], AF.Square),
                   drain=(g == NG - 1))
            for g in range(NG):
                op("scalar", lambda E, g=g: E.activation(
                    multM[:, g * ASL:(g + 1) * ASL], logaM[:, g * ASL:(g + 1) * ASL],
                    AF.Sqrt, bias=af32(1.0), scale=af32(-1.0)),
                   sem=("atsC" if g == NG - 1 else None), drain=(g == NG - 1))
            wait("vector", "atsC")
            if sl == 0:
                wait("vector", "ttr", TTRXN[NTC - 1])
            for hh in range(2):
                op("vector", lambda E, hh=hh: E.tensor_mul(
                    [lnscr, xnb][hh][:], multM[:, hh * 1024:(hh + 1) * 1024],
                    gxM[:, hh * 1024:(hh + 1) * 1024], ), sem=("atf" if hh == 1 else None),
                   drain=(hh == 1))
            for gp in range(NG // 2):
                gg = [2 * gp, 2 * gp + 1]
                bidx = [2 * (gp % 2), 2 * (gp % 2) + 1]
                for s in range(2):
                    b = bidx[s]
                    if ybcnt[b] > 0:
                        wait("vector", f"yb{b}", 32 * ybcnt[b])
                for s in range(2):
                    op("vector", lambda E, sl=sl, g=gg[s], s=s: E.tensor_tensor(
                        bb2[s][:], [lnscr, xnb][g // 4][:, (g % 4) * ASL:(g % 4 + 1) * ASL],
                        xnsl(sl, g), OP.mult), drain=(s == 1))
                for s in range(2):
                    if sl == 0:
                        op("vector", lambda E, g=gg[s], s=s, b=bidx[s]: E.tensor_tensor_scan(
                            yslb[b][:], aM[:, g * ASL:(g + 1) * ASL], bb2[s][:], 0.0, OP.mult, OP.add))
                    else:
                        op("vector", lambda E, g=gg[s], s=s, b=bidx[s]: E.tensor_tensor_scan(
                            yslb[b][:], aM[:, g * ASL:(g + 1) * ASL], bb2[s][:],
                            ycol[:, g:g + 1], OP.mult, OP.add))
                for s in range(2):
                    if sl == 0:
                        op("vector", lambda E, g=gg[s], b=bidx[s]: E.tensor_tensor_scan(
                            pslb[b][:], aM[:, g * ASL:(g + 1) * ASL], zasl[:], 1.0, OP.mult, OP.add),
                           drain=(s == 1))
                    else:
                        op("vector", lambda E, g=gg[s], b=bidx[s]: E.tensor_tensor_scan(
                            pslb[b][:], aM[:, g * ASL:(g + 1) * ASL], zasl[:],
                            pcol[:, g:g + 1], OP.mult, OP.add), drain=(s == 1))
                for s in range(2):
                    op("vector", lambda E, g=gg[s], b=bidx[s]: E.tensor_copy(
                        ycol[:, g:g + 1], yslb[b][:, ASL - 1:ASL]))
                for s in range(2):
                    op("vector", lambda E, g=gg[s], b=bidx[s]: E.tensor_copy(
                        pcol[:, g:g + 1], pslb[b][:, ASL - 1:ASL]),
                       sem=("atv" if s == 1 else None), k=2, drain=(s == 1))
                wait("sync", "atv", cur("atv"))
                for s in range(2):
                    b = bidx[s]
                    op("sync", lambda E, sl=sl, g=gg[s], b=b: E.dma_start(
                        out=y_d[g, :, sl * ASL:(sl + 1) * ASL], in_=yslb[b][:]), sem=f"yb{b}", k=16)
                    op("sync", lambda E, sl=sl, g=gg[s], b=b: E.dma_start(
                        out=p_d[g, :, sl * ASL:(sl + 1) * ASL], in_=pslb[b][:]), sem=f"yb{b}", k=16)
                    ybcnt[b] += 1
            if gp == NG // 2 - 1:
                emit_xcl(sl + 2)

        op("vector", lambda E: E.tensor_copy(ccs[:, 0:NG], pcol[:]), fence=True)
        op("vector", lambda E: E.tensor_copy(ccs[:, NG:16], ycol[:]), sem="ccv", drain=True)
        wait("gpsimd", "ccv")
        op("gpsimd", lambda E: E.dma_start(out=cc_in[:, :], in_=ccs[:]), sem="ccc", k=16)
        wait("gpsimd", "ccc", 16)
        op("gpsimd", lambda E: E.collective_compute(
            "AllGather", OP.bypass, replica_groups=[list(range(NCORE))],
            ins=[cc_in.ap().opt()], outs=[cc_out.ap().opt()]), sem="ccl")
        wait("gpsimd", "ccl", 1)
        op("gpsimd", lambda E: E.dma_start(
            out=carr[:].rearrange("p (r c) -> p r c", r=NCORE),
            in_=cc_out.ap().rearrange("(r p) c -> p r c", p=P)), sem="ccc2", k=16)
        wait("vector", "ccc2", 16)
        op("vector", lambda E: E.memset(cc0[:], 0.0), fence=True)
        op("vector", lambda E: E.memset(cs0[:], 0.0), fence=True)
        ccur, cnxt = cc0, cc1
        scur, snxt = cs0, cs1
        for r in range(NCORE):
            op("vector", lambda E, r=r: E.tensor_copy(selq[:], selc[:, r:r + 1]), fence=True)
            op("vector", lambda E, r=r, ccur=ccur, scur=scur, snxt=snxt: E.scalar_tensor_tensor(
                snxt[:], ccur[:], selq[:], scur[:], OP.mult, OP.add), fence=True)
            scur, snxt = snxt, scur
            if r < NCORE - 1:
                if r + 1 == 4:
                    op("vector", lambda E, cnxt=cnxt: E.memset(cnxt[:], 0.0), fence=True)
                else:
                    op("vector", lambda E, r=r, ccur=ccur: E.tensor_tensor(
                        cct[:], carr[:, r * 16:r * 16 + NG], ccur[:], OP.mult), fence=True)
                    op("vector", lambda E, r=r, cnxt=cnxt: E.tensor_tensor(
                        cnxt[:], cct[:], carr[:, r * 16 + NG:r * 16 + 16], OP.add), fence=True)
                ccur, cnxt = cnxt, ccur
        cmine = scur
        if dbg:
            op("vector", lambda E, cmine=cmine: E.tensor_copy(spt2[:], cmine[:]), fence=True)
            wait("gpsimd", "vfn")
            op("gpsimd", lambda E: E.dma_start(out=dbgcc_d[:, 0:128], in_=carr[:]), sem="ccv", k=16)
            op("gpsimd", lambda E: E.dma_start(out=dbgcc_d[:, 128:136], in_=spt2[:]), sem="ccv", k=16)
            op("gpsimd", lambda E: E.dma_start(out=dbgcc_d[:, 152:160], in_=selc[:]), sem="ccv", k=16)
            op("gpsimd", lambda E: E.dma_start(out=dbgcc_d[:, 136:144], in_=pcol[:]), sem="ccv", k=16)
            op("gpsimd", lambda E: E.dma_start(out=dbgcc_d[:, 144:152], in_=ycol[:]), sem="ccv", k=16)
        op("vector", lambda E, cmine=cmine: E.tensor_scalar(
            cct[:], cmine[:], f32(1.0), None, OP.mult), sem="crv", drain=True)

        IH = IQ // 2
        assert not QPEND

        def gu_tail(ent):
            tc2, q2, it2, vqs2 = ent
            par2 = it2 % 2
            sqs2 = qp_schain_gu(TL_GU[par2], P, IQ, vqs2)
            if it2 >= 2:
                wait("vector", f"hst{par2}", 16 * (it2 // 2))
            qp_tail(TL_GU[par2], P, IQ, lambda par2=par2: hqo[par2][0:P, 0:IQ],
                    "hqv", 1, sqs2)
            wait("gpsimd", "hqv")
            op("gpsimd", lambda E, tc2=tc2, q2=q2, par2=par2: E.dma_start(
                out=hq_d[tc2 * P:(tc2 + 1) * P, q2 * IQ:(q2 + 1) * IQ],
                in_=hqo[par2][0:P, 0:IQ]), sem=f"hst{par2}", k=16)
        if dbg:
            wait("gpsimd", "wqtl", 256)
            op("gpsimd", lambda E: E.dma_start(out=dbgar1_d, in_=AR_W[:, :]), sem="ccv", k=16)
        wait("tensor", "wqtl")
        wait("tensor", "ats")
        for tc in range(NTC):
            wait("sync", "xnst0")
            wait("sync", "xnst1")
            if tc == 0:
                wait("sync", "wqst", cur("wqst"))
            if tc >= 2:
                wait("sync", "gut", 4 * (tc - 1))
            op("sync", lambda E, tc=tc: E.dma_start(
                out=xqblk[tc % 2][:].rearrange("p (b t) -> p b t", b=NG),
                in_=xq16_d[:, tc * P:(tc + 1) * P].rearrange("(b d) t -> d b t", b=NG)),
               sem=f"xqcl{tc % 2}", k=16)
            for q in range(NQ):
                it = tc * NQ + q
                psG = PS[0 if it % 2 == 0 else 2]
                psU = PS[1 if it % 2 == 0 else 3]
                if it >= 2:
                    wait("tensor", "gus", it - 1)
                    wait("tensor", "guh", it - 1)
                if q == 0:
                    wait("tensor", f"xqcl{tc % 2}", 16 * (tc // 2 + 1))
                for d in range(NG):
                    lhs = lambda d=d, tc=tc: xqblk[tc % 2][:, d * P:(d + 1) * P]
                    rg = lambda d=d, q=q: AR_W[:, d * DI + q * IQ: d * DI + (q + 1) * IQ]
                    ru = lambda d=d, q=q: AR_W[:, (NG + d) * DI + q * IQ: (NG + d) * DI + (q + 1) * IQ]
                    st0, st1 = (d == 0), (d == NG - 1)
                    op("tensor", lambda E, lhs=lhs, rg=rg, psG=psG, st0=st0, st1=st1: E.matmul(
                        psG[:, 0:512], lhs(), rg()[:, 0:512], start=st0, stop=st1))
                    op("tensor", lambda E, lhs=lhs, rg=rg, psG=psG, st0=st0, st1=st1: E.matmul(
                        psG[:, 512:IQ], lhs(), rg()[:, 512:IQ], start=st0, stop=st1))
                    op("tensor", lambda E, lhs=lhs, ru=ru, psU=psU, st0=st0, st1=st1: E.matmul(
                        psU[:, 0:512], lhs(), ru()[:, 0:512], start=st0, stop=st1))
                    op("tensor", lambda E, lhs=lhs, ru=ru, psU=psU, st0=st0, st1=st1: E.matmul(
                        psU[:, 512:IQ], lhs(), ru()[:, 512:IQ], start=st0, stop=st1),
                       sem=("gut" if st1 else None))
                wait("scalar", "gut")
                if it >= 1:
                    wait("scalar", "guh", it)
                for s in range(2):
                    op("scalar", lambda E, psG=psG, s=s: E.activation(
                        hsil[:, s * IH:(s + 1) * IH], psG[:, s * IH:(s + 1) * IH], AF.Silu),
                       sem=("gus" if s == 1 else None), drain=(s == 1))
                wait("vector", "gus")
                for s in range(2):
                    op("vector", lambda E, psU=psU, s=s: E.tensor_tensor(
                        hful[:, s * IH:(s + 1) * IH], hsil[:, s * IH:(s + 1) * IH],
                        psU[:, s * IH:(s + 1) * IH], OP.mult),
                       sem=("guh" if s == 1 else None), drain=(s == 1))
                if dbg:
                    wait("gpsimd", "guh", it + 1)
                    op("gpsimd", lambda E, it=it: E.dma_start(
                        out=dbgh_d[it, :, :], in_=hful[:, 0:IQ]), sem="ccv", k=16)
                par = it % 2
                vqs = qp_head(TL_GU[par], P, IQ, lambda: hful[:, 0:IQ])
                if QPEND:
                    gu_tail(QPEND.pop(0))
                QPEND.append((tc, q, it, vqs))

        if dbg:
            wait("gpsimd", "gut", 64)
            op("gpsimd", lambda E: E.dma_start(out=dbgar2_d, in_=AR_W[:, :]), sem="ccv", k=16)

        while QPEND:
            gu_tail(QPEND.pop(0))

        wait("sync", "gut")
        wait("sync", "ccw", 3)
        for i in range(NID):
            op("sync", lambda E, i=i: E.dma_start_transpose(
                AR_W[:, i * 1024:(i + 1) * 1024], wdq_d.ap()[:, i * P:(i + 1) * P]), sem="wdl", k=16)
        HQOFF = NID * 1024
        wait("sync", "hst0")
        wait("sync", "hst1")
        for b in range(4):
            wait("sync", f"yb{b}")

        def hqsl(i, par):
            off = HQOFF + par * NID * 512 + i * 512
            return AR_W[:, off:off + 512]

        wait("vector", "crv")
        MT = [logaM, aM, gxM, multM]

        def emit_loads(j):
            tt, k = divmod(j, 4)
            tpar = tt % 2
            hc = [2 * k, 2 * k + 1]
            if k == 0:
                if tt >= 2:
                    wait("sync", "dnt", 8 * tt - 8)
                for i in range(NID):
                    op("sync", lambda E, tt=tt, i=i, tpar=tpar: E.dma_start_transpose(
                        hqsl(i, tpar), hq_d[tt * 512:(tt + 1) * 512, i * P:(i + 1) * P]), sem=f"dcl{tpar}", k=16)
            wait("sync", "dnv", 2 * j)
            for s in range(2):
                op("sync", lambda E, tt=tt, hc=hc[s], s=s: E.dma_start(
                    out=xb0[:, s * 512:(s + 1) * 512], in_=y_d[hc, :, tt * 512:(tt + 1) * 512]), sem="rld", k=16)
                op("sync", lambda E, tt=tt, hc=hc[s], s=s: E.dma_start(
                    out=xb1[:, s * 512:(s + 1) * 512], in_=p_d[hc, :, tt * 512:(tt + 1) * 512]), sem="rld", k=16)
                op("sync", lambda E, tt=tt, hc=hc[s], s=s: E.dma_start(
                    out=xnb[:, s * 512:(s + 1) * 512], in_=xT[hc * P:(hc + 1) * P, tt * 512:(tt + 1) * 512]), sem="rld", k=16)

        def emit_mm(j):
            tt = j // 4
            tpar = tt % 2
            PD = PS[j % 2]
            if j == 0:
                wait("tensor", "wdl", 16 * NID)
            if j % 4 == 0:
                wait("tensor", f"dcl{tpar}", 16 * NID * (tt // 2 + 1))
            if j >= 2:
                wait("tensor", "dnv", 2 * (j - 1))
            for s in range(2):
                for i in range(NID):
                    st0, st1 = (i == 0), (i == NID - 1)
                    op("tensor", lambda E, i=i, j=j, s=s, tpar=tpar, PD=PD, st0=st0, st1=st1: E.matmul(
                        PD[:, s * 512:(s + 1) * 512],
                        AR_W[:, i * 1024 + (2 * (j % 4) + s) * P: i * 1024 + (2 * (j % 4) + s + 1) * P],
                        hqsl(i, tpar), start=st0, stop=st1), sem=("dnt" if st1 else None))

        def emit_res(j):
            PD = PS[j % 2]
            hc = [2 * (j % 4), 2 * (j % 4) + 1]
            wait("vector", "dnt", 2 * (j + 1))
            wait("vector", "rld", 16 * 6 * (j + 1))
            if j >= 1:
                wait("vector", "trt", 8 * j)
            for s in range(2):
                op("vector", lambda E, s=s, hc=hc[s], cmine=cmine: E.tensor_copy(
                    cmq[s][:], cmine[:, hc:hc + 1]), drain=(s == 1))
            for s in range(2):
                op("vector", lambda E, s=s: E.scalar_tensor_tensor(
                    res1[s](), xb1[:, s * 512:(s + 1) * 512], cmq[s][:],
                    xb0[:, s * 512:(s + 1) * 512], OP.mult, OP.add), drain=(s == 1))
            for s in range(2):
                op("vector", lambda E, s=s, PD=PD: E.scalar_tensor_tensor(
                    res2[s](), PD[:, s * 512:(s + 1) * 512], f32(1.0), res1[s](), OP.mult, OP.add), drain=(s == 1))
            for s in range(2):
                op("vector", lambda E, s=s: E.tensor_tensor(
                    otc[s](), res2[s](), xnb[:, s * 512:(s + 1) * 512], OP.add),
                   sem=("dnv" if s == 1 else None), k=2, drain=(s == 1))

        def emit_tr(j):
            wait("tensor", "dnv", 2 * (j + 1))
            if j >= 1:
                wait("tensor", "otv", 8 * j)
            for s in range(2):
                for blk in range(4):
                    op("tensor", lambda E, s=s, blk=blk: E.transpose(
                        PS[2][:, s * 512 + blk * P:s * 512 + (blk + 1) * P],
                        lnscr[:, s * 512 + blk * P:s * 512 + (blk + 1) * P], idf[:]),
                       sem=("trt" if (s == 1 and blk == 3) else None), k=8)

        def emit_otv(j):
            tt = j // 4
            hc = [2 * (j % 4), 2 * (j % 4) + 1]
            wait("vector", "trt", 8 * (j + 1))
            if j % 4 == 0 and tt >= 1:
                wait("vector", "ost", 64 * tt)
            for blk in range(4):
                for s in range(2):
                    op("vector", lambda E, blk=blk, s=s, hc=hc[s]: E.tensor_copy(
                        MT[blk][:, hc * P:(hc + 1) * P], PS[2][:, s * 512 + blk * P:s * 512 + (blk + 1) * P]),
                       sem=("otv" if (blk == 3 and s == 1) else None), k=8,
                       drain=(blk == 3 and s == 1))
            if j % 4 == 3:
                wait("gpsimd", "otv", cur("otv"))
                for blk in range(4):
                    op("gpsimd", lambda E, tt=tt, blk=blk: E.dma_start(
                        out=out[tt * 512 + blk * P: tt * 512 + (blk + 1) * P, :],
                        in_=MT[blk][:, 0:1024]), sem="ost", k=16)

        NJP = 16
        for j in range(NJP):
            emit_loads(j)
        for j in range(NJP):
            emit_mm(j)
            if j >= 1:
                emit_tr(j - 1)
            emit_res(j)
            if j >= 1:
                emit_otv(j - 1)
        emit_tr(NJP - 1)
        emit_otv(NJP - 1)

        wait("sync", "ost")
        wait("gpsimd", "ost")

        for (weng, wsem, wv, pos) in WAITLOG:
            assert wv <= cnt[wsem], (
                f"wait({weng}, {wsem}, {wv}) exceeds final total {cnt[wsem]} (hang)")

        block = ctx.enter_context(nc.Block())

        @block.sync
        def _(E):
            for fn in sched["sync"]:
                fn(E)

        @block.vector
        def _(E):
            for fn in sched["vector"]:
                fn(E)

        @block.scalar
        def _(E):
            for fn in sched["scalar"]:
                fn(E)

        @block.tensor
        def _(E):
            for fn in sched["tensor"]:
                fn(E)

        @block.gpsimd
        def _(E):
            for fn in sched["gpsimd"]:
                fn(E)

    return nc


_NC_CACHE = [None]


def kernel(L0_attn_x, L0_attn_a_param, L0_attn_w_in, L0_attn_w_a,
           L0_ffn_w_gate, L0_ffn_w_up, L0_ffn_w_down):
    x = np.asarray(L0_attn_x, dtype=np.float32)
    a_param = np.asarray(L0_attn_a_param, dtype=np.float32)
    w_in = np.ascontiguousarray(np.asarray(L0_attn_w_in, dtype=np.float32))
    w_a = np.ascontiguousarray(np.asarray(L0_attn_w_a, dtype=np.float32))
    wg = np.asarray(L0_ffn_w_gate, dtype=np.float32)
    wu = np.asarray(L0_ffn_w_up, dtype=np.float32)
    wd = np.asarray(L0_ffn_w_down, dtype=np.float32)

    if _NC_CACHE[0] is None:
        _NC_CACHE[0] = build_nc()
    nc = _NC_CACHE[0]

    apg = np.ascontiguousarray(a_param.reshape(NG, P).T)
    identf = np.eye(P, dtype=np.float32)
    in_maps = []
    for c in range(NCORE):
        s, k = divmod(c, 4)
        xs = np.ascontiguousarray(x[s, k * TOK:(k + 1) * TOK, :])
        sel = np.zeros((P, NG), np.float32)
        sel[:, c] = 1.0
        in_maps.append({
            "x_tm": xs,
            "xT": np.ascontiguousarray(xs.T),
            "wg_sl": np.ascontiguousarray(wg[c * WGU:(c + 1) * WGU, :]),
            "wu_sl": np.ascontiguousarray(wu[c * WGU:(c + 1) * WGU, :]),
            "wd_sl": np.ascontiguousarray(wd[c * P:(c + 1) * P, :]),
            "w_in": w_in,
            "w_a": w_a,
            "apg": apg,
            "selcol": sel,
            "identf": identf,
        })
    res = run_bass_kernel_spmd(nc, in_maps, core_ids=list(range(NCORE)))
    outp = np.empty((2, 8192, D), np.float32)
    for c in range(NCORE):
        s, k = divmod(c, 4)
        outp[s, k * TOK:(k + 1) * TOK, :] = res.results[c]["out"]
    return outp


# version 30
# speedup vs baseline: 1.0028x; 1.0028x over previous
import numpy as np
from contextlib import ExitStack
import concourse.bass as bass
import concourse.mybir as mybir
from concourse.bass_utils import run_bass_kernel_spmd

F32 = mybir.dt.float32
BF16 = mybir.dt.bfloat16
I32 = mybir.dt.int32
AF = mybir.ActivationFunctionType
OP = mybir.AluOpType
AX = mybir.AxisListType

NCORE = 8
P = 128
TOK = 2048
D = 1024
NG = 8
DI = 2816
IQ = 704
NQ = 4
ASL = 256
NASL = TOK // ASL
NTC = TOK // P
NPR = NTC // 2
NID = DI // P
WGU = 352
C6TH = float(np.float32(1.0 / 6.0))
f32 = lambda v: float(np.float32(v))
af32 = np.float32


def build_nc(dbg=False):
    dk = (lambda: {"kind": "ExternalOutput"}) if dbg else (lambda: {})
    nc = bass.Bass("TRN2", target_bir_lowering=False, debug=False)

    x_tm = nc.dram_tensor("x_tm", [TOK, D], F32, kind="ExternalInput").ap()
    xT = nc.dram_tensor("xT", [D, TOK], F32, kind="ExternalInput").ap()
    wg_sl = nc.dram_tensor("wg_sl", [WGU, D], F32, kind="ExternalInput").ap()
    wu_sl = nc.dram_tensor("wu_sl", [WGU, D], F32, kind="ExternalInput").ap()
    wd_sl = nc.dram_tensor("wd_sl", [P, DI], F32, kind="ExternalInput").ap()
    w_in = nc.dram_tensor("w_in", [NG, P, P], F32, kind="ExternalInput").ap()
    w_a = nc.dram_tensor("w_a", [NG, P, P], F32, kind="ExternalInput").ap()
    apg = nc.dram_tensor("apg", [P, NG], F32, kind="ExternalInput").ap()
    selcol = nc.dram_tensor("selcol", [P, NG], F32, kind="ExternalInput").ap()
    identf = nc.dram_tensor("identf", [P, P], F32, kind="ExternalInput").ap()
    out = nc.dram_tensor("out", [TOK, D], F32, kind="ExternalOutput").ap()

    xn16_d = nc.dram_tensor("xn16_d", [D, TOK], BF16, **dk()).ap()
    xq16_d = nc.dram_tensor("xq16_d", [D, TOK], BF16, **dk()).ap()
    wgq_sl_d = nc.dram_tensor("wgq_sl_d", [WGU, D], BF16)
    wuq_sl_d = nc.dram_tensor("wuq_sl_d", [WGU, D], BF16)
    wdq_sl_d = nc.dram_tensor("wdq_sl_d", [P, DI], BF16)
    wgq_d = nc.dram_tensor("wgq_d", [DI, D], BF16)
    wuq_d = nc.dram_tensor("wuq_d", [DI, D], BF16)
    wdq_d = nc.dram_tensor("wdq_d", [D, DI], BF16)
    y_d = nc.dram_tensor("y_d", [NG, P, TOK], F32, **dk()).ap()
    p_d = nc.dram_tensor("p_d", [NG, P, TOK], F32, **dk()).ap()
    hq_d = nc.dram_tensor("hq_d", [TOK, DI], BF16, **dk()).ap()
    dbgcc_d = nc.dram_tensor("dbgcc_d", [P, 160], F32, **dk()).ap() if True else None
    dbgwg_d = nc.dram_tensor("dbgwg_d", [DI, D], BF16, **dk()).ap() if dbg else None
    dbgwu_d = nc.dram_tensor("dbgwu_d", [DI, D], BF16, **dk()).ap() if dbg else None
    dbgh_d = nc.dram_tensor("dbgh_d", [NTC * NQ, P, IQ], F32, **dk()).ap() if dbg else None
    dbgar1_d = nc.dram_tensor("dbgar1_d", [P, 16 * DI], BF16, **dk()).ap() if dbg else None
    dbgar2_d = nc.dram_tensor("dbgar2_d", [P, 16 * DI], BF16, **dk()).ap() if dbg else None
    cc_in = nc.dram_tensor("cc_in", [P, 16], F32)
    cc_out = nc.dram_tensor("cc_out", [NCORE * P, 16], F32)

    ctx = ExitStack()
    with ctx:
        def sb(name, shape, dt):
            return ctx.enter_context(nc.sbuf_tensor(name, shape, dt))

        AR_W = sb("AR_W", [P, 16 * DI], BF16)
        xqblk = [sb("xqblk0", [P, NG * P], BF16), sb("xqblk1", [P, NG * P], BF16)]

        qv = sb("qv", [P, 1024], F32)
        qi = sb("qi", [P, 1024], I32)
        qi2 = sb("qi2", [P, IQ], I32)
        qc1 = sb("qc1", [P, 1024], BF16)
        qc2 = sb("qc2", [P, 1024], BF16)
        qc3 = sb("qc3", [P, 1024], BF16)
        qa1 = sb("qa1", [P, 1024], BF16)
        qsg = sb("qsg", [P, 1024], BF16)
        qrse = sb("qrse", [P, 1024], F32)
        qav = qrse
        qsse = sb("qsse", [P, 1024], BF16)
        qbm = sb("qbm", [P, 192], F32)
        qsc = sb("qsc", [P, 192], F32)
        qrs = sb("qrs", [P, 192], F32)
        qsb = sb("qsb", [P, 192], BF16)
        qsb2 = sb("qsb2", [P, 192], BF16)

        xb0 = sb("xb0", [P, 1024], F32)
        xb1 = sb("xb1", [P, 1024], F32)
        xnb = sb("xnb", [P, 1024], F32)
        xn16t = [sb("xn16t0", [P, 1024], BF16), sb("xn16t1", [P, 1024], BF16)]
        xqo = [sb("xqo0", [P, 1024], BF16), sb("xqo1", [P, 1024], BF16)]
        lnscr = sb("lnscr", [P, 1024], F32)
        lnsx = [[sb(f"lnsx{pp}{s}", [P, 1], F32) for s in range(2)] for pp in range(2)]
        lnsq = [[sb(f"lnsq{pp}{s}", [P, 1], F32) for s in range(2)] for pp in range(2)]
        lnm = [[sb(f"lnm{pp}{s}", [P, 1], F32) for s in range(2)] for pp in range(2)]
        lnr1 = [[sb(f"lnr1{pp}{s}", [P, 1], F32) for s in range(2)] for pp in range(2)]
        lnrs = [[sb(f"lnrs{pp}{s}", [P, 1], F32) for s in range(2)] for pp in range(2)]
        lnm2 = [sb(f"lnm2{s}", [P, 1], F32) for s in range(2)]
        lnv0 = [sb(f"lnv0{s}", [P, 1], F32) for s in range(2)]
        lnv1 = [sb(f"lnv1{s}", [P, 1], F32) for s in range(2)]

        wbuf = sb("wbuf", [P, 1024], F32)
        wqo = xqblk[1]

        logaM = sb("logaM", [P, NG * ASL], F32)
        aM = sb("aM", [P, NG * ASL], F32)
        gxM = sb("gxM", [P, NG * ASL], F32)
        multM = sb("multM", [P, NG * ASL], F32)
        xnslM = sb("xnslM", [P, NG * ASL], BF16)
        bb1 = [sb("bb1a", [P, ASL], F32), sb("bb1b", [P, ASL], F32)]
        bb2 = [sb("bb2a", [P, ASL], F32), sb("bb2b", [P, ASL], F32)]
        yslb = [sb(f"ysl{i}", [P, ASL], F32) for i in range(4)]
        pslb = [sb(f"psl{i}", [P, ASL], F32) for i in range(4)]
        zasl = sb("zasl", [P, ASL], F32)
        ycol = sb("ycol", [P, NG], F32)
        pcol = sb("pcol", [P, NG], F32)
        winb = sb("winb", [P, NG * P], BF16)
        wab = sb("wab", [P, NG * P], BF16)
        spu = sb("spu", [P, NG], F32)
        spt1 = sb("spt1", [P, NG], F32)
        spt2 = sb("spt2", [P, NG], F32)
        spneg = sb("spneg", [P, NG], F32)
        selc = sb("selc", [P, NG], F32)
        idf = sb("idf", [P, P], F32)
        ccs = sb("ccs", [P, 16], F32)
        carr = sb("carr", [P, NCORE * 16], F32)
        cc0 = sb("cc0", [P, NG], F32)
        cc1 = sb("cc1", [P, NG], F32)
        cct = sb("cct", [P, NG], F32)
        cs0 = sb("cs0", [P, NG], F32)
        cs1 = sb("cs1", [P, NG], F32)
        selq = sb("selq", [P, 1], F32)
        cmq = [sb("cmqa", [P, 1], F32), sb("cmqb", [P, 1], F32)]

        hsil = xnb
        hful = lnscr
        hqo = xqo

        otc = [lambda: lnscr[:, 0:512], lambda: lnscr[:, 512:1024]]
        res1 = [lambda: qrse[:, 0:512], lambda: qrse[:, 512:1024]]
        res2 = [lambda: qv[:, 0:512], lambda: qv[:, 512:1024]]

        PS = [ctx.enter_context(nc.psum_tensor(f"PS{i}", [P, 1024], F32)) for i in range(4)]

        sem_names = [
            "xld0", "xld1", "mld", "mld2", "wld", "spv", "tms", "tmv", "tms2", "tmv2",
            "xnst0", "xnst1",
            "wqv", "wqst", "ccw", "xcl0", "xcl1", "att", "ats", "atsB", "atsC", "atf",
            "atv", "atvB", "yb0", "yb1", "yb2", "yb3",
            "ccv", "ccc", "ccl", "ccc2", "crv", "xqcl0", "xqcl1", "wqtl", "wdl", "gut", "gus",
            "guh", "hqv", "hst0", "hst1", "dcl0", "dcl1", "rld", "dnt", "dnv", "trt", "otv",
            "ost", "qvs", "qss", "xnv", "xqv", "ttr", "tev", "vfn",
        ]
        sems = {n: ctx.enter_context(nc.semaphore(n)) for n in sem_names}

        ENG = ("sync", "vector", "scalar", "tensor", "gpsimd")
        sched = {e: [] for e in ENG}
        cnt = {n: 0 for n in sem_names}

        def op(eng, build, sem=None, k=1, drain=False, fence=False):
            if fence and sem is None:
                sem, k, drain = "vfn", 1, True
            if sem is not None:
                cnt[sem] += k

            def fn(E, build=build, sem=sem, k=k, drain=drain):
                ins = build(E)
                if drain:
                    d = E.drain()
                    if sem is not None:
                        d.then_inc(sems[sem], k)
                elif sem is not None:
                    ins.then_inc(sems[sem], k)
            sched[eng].append(fn)
            return None

        WAITLOG = []

        def wait(eng, sem, val=None):
            v = cnt[sem] if val is None else val
            if v <= 0:
                return
            WAITLOG.append((eng, sem, v, len(sched[eng])))
            sched[eng].append(lambda E, sem=sem, v=v: E.wait_ge(sems[sem], v))

        def cur(sem):
            return cnt[sem]

        def emit_quant2(rows, width, src, dst, dst_sem=None, dst_k=1, soff=True):
            nb = width // 16
            R = rows
            V = lambda fn, dr=False, sem=None, k=1: op("vector", fn, drain=dr, sem=sem, k=k)
            S = lambda fn, dr=False, sem=None: op("scalar", fn, drain=dr, sem=sem)
            V(lambda E: E.reduce_max(
                qbm[0:R, 0:nb].unsqueeze(2).unsqueeze(3),
                src(0, width).rearrange("p (a b c) -> p a b c", b=4, c=4),
                axis=AX.XY, apply_absolute_value=True), dr=True)
            V(lambda E: E.tensor_scalar(qsc[0:R, 0:nb], qbm[0:R, 0:nb], C6TH, 1e-12, OP.mult, OP.max), dr=True)
            V(lambda E: E.reciprocal(qrs[0:R, 0:nb], qsc[0:R, 0:nb]))
            V(lambda E: E.tensor_copy(qsb[0:R, 0:nb], qsc[0:R, 0:nb]), dr=True)
            V(lambda E: E.tensor_copy(
                qrse[0:R, 0:width].rearrange("p (a b) -> p a b", a=nb),
                qrs[0:R, 0:nb].unsqueeze(2).broadcast_to([R, nb, 16])))
            V(lambda E: E.tensor_copy(
                qsse[0:R, 0:width].rearrange("p (a b) -> p a b", a=nb),
                qsb[0:R, 0:nb].unsqueeze(2).broadcast_to([R, nb, 16])), dr=True)
            V(lambda E: E.tensor_tensor(qv[0:R, 0:width], src(0, width), qrse[0:R, 0:width], OP.mult),
              dr=True, sem="qvs")
            wait("scalar", "qvs")
            if soff:
                S(lambda E: E.activation(qav[0:R, 0:width], qv[0:R, 0:width], AF.Abs), dr=True)
                S(lambda E: E.activation(qsg[0:R, 0:width], qv[0:R, 0:width], AF.Sign))
                S(lambda E: E.activation(qi[0:R, 0:width], qav[0:R, 0:width], AF.Copy,
                                         scale=af32(2.0)), dr=True)
                S(lambda E: E.activation(qa1[0:R, 0:width], qi[0:R, 0:width], AF.Copy,
                                         scale=af32(0.5)), dr=True)
                S(lambda E: E.activation(qi[0:R, 0:width], qav[0:R, 0:width], AF.Copy,
                                         scale=af32(1.0)), dr=True)
                S(lambda E: E.activation(qc2[0:R, 0:width], qi[0:R, 0:width], AF.Copy,
                                         scale=af32(1.0)), dr=True)
                S(lambda E: E.activation(qi[0:R, 0:width], qav[0:R, 0:width], AF.Copy,
                                         scale=af32(0.125), bias=float(-0.125)), dr=True)
                S(lambda E: E.activation(qc3[0:R, 0:width], qi[0:R, 0:width], AF.Copy,
                                         scale=af32(2.0)), dr=True, sem="qss")
                wait("vector", "qss")
                V(lambda E: E.tensor_scalar(qc1[0:R, 0:width], qa1[0:R, 0:width], f32(2.0), None, OP.min), dr=True)
                V(lambda E: E.tensor_scalar(qa1[0:R, 0:width], qc2[0:R, 0:width], f32(2.0), f32(4.0), OP.max, OP.min), dr=True)
                V(lambda E: E.tensor_scalar(qc2[0:R, 0:width], qc3[0:R, 0:width], f32(-2.0), None, OP.add), dr=True)
                V(lambda E: E.tensor_tensor(qc3[0:R, 0:width], qc1[0:R, 0:width], qa1[0:R, 0:width], OP.add), dr=True)
                V(lambda E: E.tensor_tensor(qc1[0:R, 0:width], qc3[0:R, 0:width], qc2[0:R, 0:width], OP.add), dr=True)
                V(lambda E: E.tensor_tensor(qa1[0:R, 0:width], qc1[0:R, 0:width], qsg[0:R, 0:width], OP.mult), dr=True)
                V(lambda E: E.tensor_tensor(dst(0, width), qa1[0:R, 0:width], qsse[0:R, 0:width], OP.mult),
                  dr=True, sem=dst_sem, k=dst_k)
            else:
                S(lambda E: E.activation(qav[0:R, 0:width], qv[0:R, 0:width], AF.Abs))
                S(lambda E: E.activation(qsg[0:R, 0:width], qv[0:R, 0:width], AF.Sign),
                  dr=True, sem="qss")
                wait("vector", "qss")
                V(lambda E: E.tensor_scalar(qi[0:R, 0:width], qav[0:R, 0:width], f32(2.0), None, OP.mult), dr=True)
                V(lambda E: E.tensor_copy(qa1[0:R, 0:width], qi[0:R, 0:width]), dr=True)
                V(lambda E: E.tensor_scalar(qi[0:R, 0:width], qav[0:R, 0:width], f32(1.0), None, OP.mult))
                V(lambda E: E.tensor_scalar(qc1[0:R, 0:width], qa1[0:R, 0:width], f32(4.0), f32(0.5), OP.min, OP.mult), dr=True)
                V(lambda E: E.tensor_copy(qa1[0:R, 0:width], qi[0:R, 0:width]), dr=True)
                V(lambda E: E.tensor_scalar(qi[0:R, 0:width], qav[0:R, 0:width], f32(0.125), f32(-0.125), OP.mult, OP.add))
                V(lambda E: E.tensor_scalar(qc2[0:R, 0:width], qa1[0:R, 0:width], f32(2.0), f32(4.0), OP.max, OP.min), dr=True)
                V(lambda E: E.tensor_copy(qa1[0:R, 0:width], qi[0:R, 0:width]), dr=True)
                V(lambda E: E.tensor_scalar(qc3[0:R, 0:width], qa1[0:R, 0:width], f32(2.0), f32(-2.0), OP.mult, OP.add), dr=True)
                V(lambda E: E.tensor_tensor(qa1[0:R, 0:width], qc1[0:R, 0:width], qc2[0:R, 0:width], OP.add), dr=True)
                V(lambda E: E.tensor_tensor(qc1[0:R, 0:width], qa1[0:R, 0:width], qc3[0:R, 0:width], OP.add), dr=True)
                V(lambda E: E.tensor_tensor(qc2[0:R, 0:width], qc1[0:R, 0:width], qsg[0:R, 0:width], OP.mult), dr=True)
                V(lambda E: E.tensor_tensor(dst(0, width), qc2[0:R, 0:width], qsse[0:R, 0:width], OP.mult),
                  dr=True, sem=dst_sem, k=dst_k)

        def qp_head(T, R, W, srcf):
            V = lambda fn, dr=False, sem=None: op("vector", fn, drain=dr, sem=sem)
            nb = W // 16
            V(lambda E: E.reduce_max(
                qbm[0:R, 0:nb].unsqueeze(2).unsqueeze(3),
                srcf().rearrange("p (a b c) -> p a b c", b=4, c=4),
                axis=AX.XY, apply_absolute_value=True), dr=True)
            V(lambda E: E.tensor_scalar(qsc[0:R, 0:nb], qbm[0:R, 0:nb], C6TH, 1e-12, OP.mult, OP.max), dr=True)
            V(lambda E: E.reciprocal(qrs[0:R, 0:nb], qsc[0:R, 0:nb]))
            V(lambda E: E.tensor_copy(T["qsb"](R, nb), qsc[0:R, 0:nb]), dr=True)
            V(lambda E: E.tensor_copy(
                qrse[0:R, 0:W].rearrange("p (a b) -> p a b", a=nb),
                qrs[0:R, 0:nb].unsqueeze(2).broadcast_to([R, nb, 16])), dr=True)
            V(lambda E: E.tensor_tensor(T["qv"](R, W), srcf(), qrse[0:R, 0:W], OP.mult),
              dr=True, sem="qvs")
            return cur("qvs")

        def qp_schain_gu(T, R, W, vqs):
            S = lambda fn, dr=False, sem=None: op("scalar", fn, drain=dr, sem=sem)
            wait("scalar", "qvs", vqs)
            S(lambda E: E.activation(T["qav"](R, W), T["qv"](R, W), AF.Abs))
            S(lambda E: E.activation(T["qsg"](R, W), T["qv"](R, W), AF.Sign))
            S(lambda E: E.activation(qi[0:R, 0:W], T["qv"](R, W), AF.Abs, scale=af32(2.0)))
            S(lambda E: E.activation(qi2[0:R, 0:W], T["qv"](R, W), AF.Abs, scale=af32(1.0)), dr=True)
            S(lambda E: E.activation(T["qa1"](R, W), qi[0:R, 0:W], AF.Copy, scale=af32(0.5)))
            S(lambda E: E.activation(T["qc2"](R, W), qi2[0:R, 0:W], AF.Copy, scale=af32(1.0)), dr=True)
            S(lambda E: E.activation(qi[0:R, 0:W], T["qav"](R, W), AF.Copy,
                                     scale=af32(0.125), bias=float(-0.125)), dr=True)
            S(lambda E: E.activation(T["qc3"](R, W), qi[0:R, 0:W], AF.Copy, scale=af32(2.0)),
              dr=True, sem="qss")
            return cur("qss")

        def qp_schain(T, R, W, vqs):
            S = lambda fn, dr=False, sem=None: op("scalar", fn, drain=dr, sem=sem)
            wait("scalar", "qvs", vqs)
            S(lambda E: E.activation(T["qav"](R, W), T["qv"](R, W), AF.Abs), dr=True)
            S(lambda E: E.activation(T["qsg"](R, W), T["qv"](R, W), AF.Sign))
            S(lambda E: E.activation(qi[0:R, 0:W], T["qav"](R, W), AF.Copy, scale=af32(2.0)), dr=True)
            S(lambda E: E.activation(T["qa1"](R, W), qi[0:R, 0:W], AF.Copy, scale=af32(0.5)), dr=True)
            S(lambda E: E.activation(qi[0:R, 0:W], T["qav"](R, W), AF.Copy, scale=af32(1.0)), dr=True)
            S(lambda E: E.activation(T["qc2"](R, W), qi[0:R, 0:W], AF.Copy, scale=af32(1.0)), dr=True)
            S(lambda E: E.activation(qi[0:R, 0:W], T["qav"](R, W), AF.Copy,
                                     scale=af32(0.125), bias=float(-0.125)), dr=True)
            S(lambda E: E.activation(T["qc3"](R, W), qi[0:R, 0:W], AF.Copy, scale=af32(2.0)),
              dr=True, sem="qss")
            return cur("qss")

        def qp_tail(T, R, W, dstf, dst_sem, dst_k, sqs):
            V = lambda fn, dr=False, sem=None, k=1: op("vector", fn, drain=dr, sem=sem, k=k)
            nb = W // 16
            wait("vector", "qss", sqs)
            V(lambda E: E.tensor_scalar(qc1[0:R, 0:W], T["qa1"](R, W), f32(2.0), None, OP.min), dr=True)
            V(lambda E: E.tensor_scalar(T["qa1"](R, W), T["qc2"](R, W), f32(2.0), f32(4.0), OP.max, OP.min), dr=True)
            V(lambda E: E.tensor_scalar(T["qc2"](R, W), T["qc3"](R, W), f32(-2.0), None, OP.add), dr=True)
            V(lambda E: E.tensor_tensor(T["qc3"](R, W), qc1[0:R, 0:W], T["qa1"](R, W), OP.add), dr=True)
            V(lambda E: E.tensor_copy(
                qsse[0:R, 0:W].rearrange("p (a b) -> p a b", a=nb),
                T["qsb"](R, nb).unsqueeze(2).broadcast_to([R, nb, 16])))
            V(lambda E: E.tensor_tensor(qc1[0:R, 0:W], T["qc3"](R, W), T["qc2"](R, W), OP.add), dr=True)
            V(lambda E: E.tensor_tensor(T["qa1"](R, W), qc1[0:R, 0:W], T["qsg"](R, W), OP.mult), dr=True)
            V(lambda E: E.tensor_tensor(dstf(), T["qa1"](R, W), qsse[0:R, 0:W], OP.mult),
              dr=True, sem=dst_sem, k=dst_k)

        def qp_schain_lite(T, R, W, vqs):
            S = lambda fn, dr=False, sem=None: op("scalar", fn, drain=dr, sem=sem)
            wait("scalar", "qvs", vqs)
            S(lambda E: E.activation(T["qav"](R, W), T["qv"](R, W), AF.Abs))
            S(lambda E: E.activation(T["qsg"](R, W), T["qv"](R, W), AF.Sign),
              dr=True, sem="qss")
            return cur("qss")

        def qp_tail_v(T, R, W, dstf, dst_sem, dst_k, sqs):
            V = lambda fn, dr=False, sem=None, k=1: op("vector", fn, drain=dr, sem=sem, k=k)
            nb = W // 16
            wait("vector", "qss", sqs)
            V(lambda E: E.tensor_scalar(qi[0:R, 0:W], T["qav"](R, W), f32(2.0), None, OP.mult), dr=True)
            V(lambda E: E.tensor_copy(qa1[0:R, 0:W], qi[0:R, 0:W]), dr=True)
            V(lambda E: E.tensor_scalar(qi[0:R, 0:W], T["qav"](R, W), f32(1.0), None, OP.mult))
            V(lambda E: E.tensor_scalar(qc1[0:R, 0:W], qa1[0:R, 0:W], f32(4.0), f32(0.5), OP.min, OP.mult), dr=True)
            V(lambda E: E.tensor_copy(qa1[0:R, 0:W], qi[0:R, 0:W]), dr=True)
            V(lambda E: E.tensor_scalar(qi[0:R, 0:W], T["qav"](R, W), f32(0.125), f32(-0.125), OP.mult, OP.add))
            V(lambda E: E.tensor_scalar(qc2[0:R, 0:W], qa1[0:R, 0:W], f32(2.0), f32(4.0), OP.max, OP.min), dr=True)
            V(lambda E: E.tensor_copy(qa1[0:R, 0:W], qi[0:R, 0:W]), dr=True)
            V(lambda E: E.tensor_scalar(qc3[0:R, 0:W], qa1[0:R, 0:W], f32(2.0), f32(-2.0), OP.mult, OP.add), dr=True)
            V(lambda E: E.tensor_tensor(qa1[0:R, 0:W], qc1[0:R, 0:W], qc2[0:R, 0:W], OP.add), dr=True)
            V(lambda E: E.tensor_copy(
                qsse[0:R, 0:W].rearrange("p (a b) -> p a b", a=nb),
                T["qsb"](R, nb).unsqueeze(2).broadcast_to([R, nb, 16])))
            V(lambda E: E.tensor_tensor(qc1[0:R, 0:W], qa1[0:R, 0:W], qc3[0:R, 0:W], OP.add), dr=True)
            V(lambda E: E.tensor_tensor(qc2[0:R, 0:W], qc1[0:R, 0:W], T["qsg"](R, W), OP.mult), dr=True)
            V(lambda E: E.tensor_tensor(dstf(), qc2[0:R, 0:W], qsse[0:R, 0:W], OP.mult),
              dr=True, sem=dst_sem, k=dst_k)

        TL_TM = [
            {"qv": lambda R, W: qv[0:R, 0:W], "qav": lambda R, W: aM[0:R, 0:W],
             "qa1": lambda R, W: qa1[0:R, 0:W], "qc2": lambda R, W: qc2[0:R, 0:W],
             "qc3": lambda R, W: qc3[0:R, 0:W], "qsg": lambda R, W: qsg[0:R, 0:W],
             "qsb": lambda R, nb: qsb[0:R, 0:nb]},
            {"qv": lambda R, W: logaM[0:R, 0:W], "qav": lambda R, W: aM[0:R, 1024:1024 + W],
             "qa1": lambda R, W: xqblk[0][0:R, 0:W], "qc2": lambda R, W: xqblk[1][0:R, 0:W],
             "qc3": lambda R, W: xnslM[0:R, 1024:1024 + W], "qsg": lambda R, W: winb[0:R, 0:W],
             "qsb": lambda R, nb: qsb2[0:R, 0:nb]},
        ]
        TL_W = [
            {"qv": lambda R, W: qv[0:R, 0:W], "qav": lambda R, W: aM[0:R, 0:W],
             "qsg": lambda R, W: qsg[0:R, 0:W], "qsb": lambda R, nb: qsb[0:R, 0:nb]},
            {"qv": lambda R, W: xb0[0:R, 0:W], "qav": lambda R, W: xb1[0:R, 0:W],
             "qsg": lambda R, W: xqblk[0][0:R, 0:W], "qsb": lambda R, nb: qsb2[0:R, 0:nb]},
        ]
        TL_GU = [
            TL_TM[0],
            {"qv": lambda R, W: logaM[0:R, 0:W], "qav": lambda R, W: aM[0:R, 1024:1024 + W],
             "qa1": lambda R, W: winb[0:R, 0:W], "qc2": lambda R, W: wab[0:R, 0:W],
             "qc3": lambda R, W: xn16t[0][0:R, 0:W], "qsg": lambda R, W: xn16t[1][0:R, 0:W],
             "qsb": lambda R, nb: qsb2[0:R, 0:nb]},
        ]

        op("sync", lambda E: E.dma_start(out=idf[:], in_=identf), sem="mld", k=16)
        op("sync", lambda E: E.dma_start(out=selc[:], in_=selcol), sem="mld", k=16)
        op("sync", lambda E: E.dma_start(out=spu[:], in_=apg), sem="mld", k=16)
        def emit_mld2():
            wait("gpsimd", "xnst0", 256)
            wait("gpsimd", "xnst1", 256)
            op("gpsimd", lambda E: E.dma_start(
                out=winb[:].rearrange("d (h e) -> d h e", h=NG),
                in_=w_in.rearrange("h d e -> d h e")), sem="mld2", k=16)
            op("gpsimd", lambda E: E.dma_start(
                out=wab[:].rearrange("d (h e) -> d h e", h=NG),
                in_=w_a.rearrange("h d e -> d h e")), sem="mld2", k=16)
        def emit_xloads(pr):
            op("sync", lambda E, pr=pr: E.dma_start(
                out=xb0[:], in_=x_tm[2 * pr * P:(2 * pr + 1) * P, :]), sem="xld0", k=16)
            op("sync", lambda E, pr=pr: E.dma_start(
                out=xb1[:], in_=x_tm[(2 * pr + 1) * P:(2 * pr + 2) * P, :]), sem="xld1", k=16)
        emit_xloads(0)

        wait("scalar", "mld", 48)
        op("scalar", lambda E: E.activation(spt1[:], spu[:], AF.Exp), sem="spv", drain=True)
        wait("vector", "spv")
        op("vector", lambda E: E.tensor_scalar(spt2[:], spt1[:], f32(-0.25), f32(1.0 / 3.0), OP.mult, OP.add), fence=True)
        op("vector", lambda E: E.scalar_tensor_tensor(spneg[:], spt2[:], f32(-1.0), spt1[:], OP.mult, OP.mult), fence=True)
        op("vector", lambda E: E.tensor_scalar(spt2[:], spneg[:], f32(0.5), None, OP.add), fence=True)
        op("vector", lambda E: E.scalar_tensor_tensor(spneg[:], spt2[:], f32(-1.0), spt1[:], OP.mult, OP.mult), fence=True)
        op("vector", lambda E: E.tensor_scalar(spt2[:], spneg[:], f32(1.0), None, OP.add), fence=True)
        op("vector", lambda E: E.tensor_mul(spu[:], spt2[:], spt1[:]), fence=True)
        op("vector", lambda E: E.tensor_scalar(spneg[:], spu[:], f32(-8.0), None, OP.mult), fence=True)
        op("vector", lambda E: E.memset(zasl[:], 0.0), fence=True)
        op("vector", lambda E: E.memset(xnslM[:, 0:1024], 1.0), sem="spv", drain=True)

        XB = [xb0, xb1]
        SCR2 = [lnscr, xnb]
        SSC = [qv, qrse]
        QPEND = []
        TTRXN, TEVXN, TTRXQ, TEVXQ = {}, {}, {}, {}

        def pend_pop():
            ent = QPEND.pop(0)
            if ent[0] == "tm":
                _, tc2, s2, vqs2 = ent
                sqs2 = qp_schain(TL_TM[s2], P, D, vqs2)
                qp_tail(TL_TM[s2], P, D, lambda s2=s2: gxM[:, s2 * 1024:(s2 + 1) * 1024],
                        "xqv", 1, sqs2)
                wait("tensor", "xqv", tc2 + 1)
                if tc2 >= 1:
                    wait("tensor", "tev", TEVXQ[tc2 - 1])
                for b in range(8):
                    op("tensor", lambda E, s2=s2, b=b: E.transpose(
                        PS[3][:, b * P:(b + 1) * P], gxM[:, s2 * 1024 + b * P:s2 * 1024 + (b + 1) * P],
                        idf[:]), sem=("ttr" if b == 7 else None))
                TTRXQ[tc2] = cur("ttr")
                wait("scalar", "ttr", TTRXQ[tc2])
                wait("scalar", f"xnst{s2}", 32 * (tc2 // 2))
                op("scalar", lambda E, s2=s2: E.activation(
                    xqo[s2][:, 0:D], PS[3][:, 0:D], AF.Copy), sem="tev", drain=True)
                TEVXQ[tc2] = cur("tev")
                wait("sync", "tev", TEVXQ[tc2])
                op("sync", lambda E, tc2=tc2, s2=s2: E.dma_start(
                    out=xq16_d[:, tc2 * P:(tc2 + 1) * P].rearrange("(b d) t -> d b t", b=8),
                    in_=xqo[s2][:].rearrange("p (b t) -> p b t", b=8)), sem=f"xnst{s2}", k=16)
            else:
                _, k2, rr2, ww2, dst2, vqs2 = ent
                p2 = k2 % 2
                sqs2 = qp_schain_lite(TL_W[p2], rr2, ww2, vqs2)
                wait("vector", "wqst", 16 * k2)
                qp_tail_v(TL_W[p2], rr2, ww2,
                          lambda rr2=rr2, ww2=ww2: wqo[0:rr2, 0:ww2], "wqv", 1, sqs2)
                wait("gpsimd", "wqv")
                op("gpsimd", lambda E, dst2=dst2, rr2=rr2, ww2=ww2: E.dma_start(
                    out=dst2, in_=wqo[0:rr2, 0:ww2]), sem="wqst", k=16)

        for it in range(NPR):
            if True:
                pr = it
                pp = pr % 2
                wait("vector", "xld0", 16 * (pr + 1))
                wait("vector", "xld1", 16 * (pr + 1))
                if pr >= 2:
                    wait("vector", "tms2", pr - 1)
                for s in range(2):
                    op("vector", lambda E, s=s: E.tensor_tensor(
                        SCR2[s][:], XB[s][:], XB[s][:], OP.mult))
                for s in range(2):
                    op("vector", lambda E, s=s: E.tensor_tensor_scan(
                        SSC[s][:], xnslM[:, 0:1024], XB[s][:], 0.0, OP.mult, OP.add), drain=(s == 1))
                for s in range(2):
                    op("vector", lambda E, s=s, pp=pp: E.tensor_copy(
                        lnsx[pp][s][:], SSC[s][:, 1023:1024]), drain=(s == 1))
                for s in range(2):
                    op("vector", lambda E, s=s: E.tensor_tensor_scan(
                        SSC[s][:], xnslM[:, 0:1024], SCR2[s][:], 0.0, OP.mult, OP.add), drain=(s == 1))
                for s in range(2):
                    op("vector", lambda E, s=s, pp=pp: E.tensor_copy(
                        lnsq[pp][s][:], SSC[s][:, 1023:1024]))
                for s in range(2):
                    op("vector", lambda E, s=s, pp=pp: E.tensor_scalar(
                        lnm[pp][s][:], lnsx[pp][s][:], f32(1.0 / D), None, OP.mult), drain=(s == 1))
                for s in range(2):
                    op("vector", lambda E, s=s, pp=pp: E.tensor_mul(
                        lnm2[s][:], lnm[pp][s][:], lnm[pp][s][:]), drain=(s == 1))
                for s in range(2):
                    op("vector", lambda E, s=s, pp=pp: E.scalar_tensor_tensor(
                        lnv0[s][:], lnsq[pp][s][:], f32(1.0 / D), lnm2[s][:], OP.mult, OP.subtract), drain=(s == 1))
                for s in range(2):
                    op("vector", lambda E, s=s: E.tensor_scalar(
                        lnv1[s][:], lnv0[s][:], f32(1e-5), None, OP.add), drain=(s == 1))
                for s in range(2):
                    op("vector", lambda E, s=s, pp=pp: E.reciprocal(
                        lnr1[pp][s][:], lnv1[s][:]),
                       sem=("tmv" if s == 1 else None), drain=(s == 1))
                wait("scalar", "tmv")
                for s in range(2):
                    op("scalar", lambda E, s=s, pp=pp: E.activation(
                        lnrs[pp][s][:], lnr1[pp][s][:], AF.Sqrt),
                       sem=("tms2" if s == 1 else None), drain=(s == 1))
            if True:
                pr = it
                pp = pr % 2
                wait("vector", "tms2", pr + 1)

                for s in range(2):
                    tc = 2 * pr + s
                    if tc >= 1:
                        wait("vector", "ttr", TTRXN[tc - 1])
                    for h in range(2):
                        op("vector", lambda E, s=s, pp=pp, h=h: E.tensor_scalar(
                            xnb[:, h * 512:(h + 1) * 512], XB[s][:, h * 512:(h + 1) * 512],
                            lnm[pp][s][:], lnrs[pp][s][:], OP.subtract, OP.mult),
                           sem=("xnv" if h == 1 else None), k=1, drain=(h == 1))
                    vqs = qp_head(TL_TM[s], P, D, lambda: xnb[:, 0:D])
                    if tc == 0:
                        wait("tensor", "mld", 48)
                    wait("tensor", "xnv", tc + 1)
                    if tc >= 1:
                        wait("tensor", "tev", TEVXN[tc - 1])
                    for b in range(8):
                        op("tensor", lambda E, b=b: E.transpose(
                            PS[2][:, b * P:(b + 1) * P], xnb[:, b * P:(b + 1) * P],
                            idf[:]), sem=("ttr" if b == 7 else None))
                    TTRXN[tc] = cur("ttr")
                    wait("scalar", "ttr", TTRXN[tc])
                    wait("scalar", f"xnst{s}", 32 * pr)
                    op("scalar", lambda E, s=s: E.activation(
                        xn16t[s][:, 0:D], PS[2][:, 0:D], AF.Copy), sem="tev", drain=True)
                    TEVXN[tc] = cur("tev")
                    wait("sync", "tev", TEVXN[tc])
                    op("sync", lambda E, tc=tc, s=s: E.dma_start(
                        out=xn16_d[:, tc * P:(tc + 1) * P].rearrange("(b d) t -> d b t", b=8),
                        in_=xn16t[s][:].rearrange("p (b t) -> p b t", b=8)), sem=f"xnst{s}", k=16)
                    if s == 1 and pr + 1 < NPR:
                        wait("sync", "xnv", cur("xnv"))
                        emit_xloads(pr + 1)
                    if QPEND:
                        pend_pop()
                    QPEND.append(("tm", tc, s, vqs))

        wsubs = []
        for j in range(3):
            rr = min(P, WGU - j * P)
            wsubs.append((wg_sl[j * P:j * P + rr, :], rr, 1024,
                          wgq_sl_d.ap()[j * P:j * P + rr, :]))
        for j in range(3):
            rr = min(P, WGU - j * P)
            wsubs.append((wu_sl[j * P:j * P + rr, :], rr, 1024,
                          wuq_sl_d.ap()[j * P:j * P + rr, :]))
        for j in range(NQ):
            wsubs.append((wd_sl[:, j * IQ:(j + 1) * IQ], P, IQ,
                          wdq_sl_d.ap()[:, j * IQ:(j + 1) * IQ]))
        wvqs_prev = None
        for k, (wsrc, rr, ww, dst) in enumerate(wsubs):
            if wvqs_prev is not None:
                wait("sync", "qvs", wvqs_prev)
            op("sync", lambda E, wsrc=wsrc, rr=rr, ww=ww: E.dma_start(
                out=wbuf[0:rr, 0:ww], in_=wsrc), sem="wld", k=16)
            wait("vector", "wld", 16 * (k + 1))
            wvqs_prev = qp_head(TL_W[k % 2], rr, ww, lambda rr=rr, ww=ww: wbuf[0:rr, 0:ww])
            if QPEND:
                pend_pop()
            QPEND.append(("w", k, rr, ww, dst, wvqs_prev))
        while QPEND:
            pend_pop()
        emit_mld2()
        wait("gpsimd", "wqst")
        op("gpsimd", lambda E: E.collective_compute(
            "AllGather", OP.bypass, replica_groups=[list(range(NCORE))],
            ins=[wgq_sl_d.ap().opt()], outs=[wgq_d.ap().opt()]), sem="ccw")
        wait("gpsimd", "ccw", 1)
        op("gpsimd", lambda E: E.collective_compute(
            "AllGather", OP.bypass, replica_groups=[list(range(NCORE))],
            ins=[wuq_sl_d.ap().opt()], outs=[wuq_d.ap().opt()]), sem="ccw")
        wait("gpsimd", "ccw", 2)
        op("gpsimd", lambda E: E.collective_compute(
            "AllGather", OP.bypass, replica_groups=[list(range(NCORE))],
            ins=[wdq_sl_d.ap().opt()], outs=[wdq_d.ap().opt()]), sem="ccw")

        if dbg:
            wait("gpsimd", "ccw", 2)
            op("gpsimd", lambda E: E.dma_start(out=dbgwg_d, in_=wgq_d.ap()), sem="ccv", k=16)
            op("gpsimd", lambda E: E.dma_start(out=dbgwu_d, in_=wuq_d.ap()), sem="ccv", k=16)

        ybcnt = [0, 0, 0, 0]

        def xnsl(sl, g):
            if sl % 2 == 0:
                return xnslM[:, g * ASL:(g + 1) * ASL]
            if g < 4:
                return xn16t[0][:, g * ASL:(g + 1) * ASL]
            return xn16t[1][:, (g - 4) * ASL:(g - 3) * ASL]

        def emit_xcl(sl):
            if sl >= NASL:
                return
            wait("sync", "xnst0")
            wait("sync", "xnst1")
            if sl > 1:
                wait("sync", "atv", 8 * (sl - 1))
            if sl % 2 == 0:
                op("sync", lambda E, sl=sl: E.dma_start(
                    out=xnslM[:].rearrange("p (g t) -> p g t", g=NG),
                    in_=xn16_d[:, sl * ASL:(sl + 1) * ASL].rearrange("(g p) t -> p g t", g=NG)),
                   sem="xcl0", k=16)
            else:
                for hh in range(2):
                    op("sync", lambda E, sl=sl, hh=hh: E.dma_start(
                        out=xn16t[hh][:].rearrange("p (g t) -> p g t", g=4),
                        in_=xn16_d[hh * 512:(hh + 1) * 512,
                                   sl * ASL:(sl + 1) * ASL].rearrange("(g p) t -> p g t", g=4)),
                       sem="xcl1", k=16)
        emit_xcl(0)
        emit_xcl(1)

        def emit_wqtl():
            wait("sync", "ccw", 1)
            for d in range(NG):
                op("sync", lambda E, d=d: E.dma_start_transpose(
                    AR_W[:, d * DI:(d + 1) * DI], wgq_d.ap()[:, d * P:(d + 1) * P]), sem="wqtl", k=16)
            wait("sync", "ccw", 2)
            for d in range(NG):
                op("sync", lambda E, d=d: E.dma_start_transpose(
                    AR_W[:, (NG + d) * DI:(NG + d + 1) * DI], wuq_d.ap()[:, d * P:(d + 1) * P]), sem="wqtl", k=16)

        for sl in range(NASL):
            if sl == 4:
                emit_wqtl()
            if sl == 0:
                wait("tensor", "tev", 32)
            wait("tensor", "mld", 48)
            wait("tensor", "mld2", 32)
            wait("tensor", f"xcl{sl % 2}", (16 if sl % 2 == 0 else 32) * (sl // 2 + 1))
            if sl > 0:
                wait("tensor", "ats", 8 * sl)
            aps = lambda g: PS[g // 2][:, (g % 2) * 512:(g % 2) * 512 + ASL]
            xps = lambda g: PS[g // 2][:, (g % 2) * 512 + ASL:(g % 2) * 512 + 2 * ASL]
            for g in range(NG):
                op("tensor", lambda E, sl=sl, g=g: E.matmul(
                    aps(g), wab[:, g * P:(g + 1) * P],
                    xnsl(sl, g), start=True, stop=True))
                op("tensor", lambda E, sl=sl, g=g: E.matmul(
                    xps(g), winb[:, g * P:(g + 1) * P],
                    xnsl(sl, g), start=True, stop=True),
                   sem=("att" if g == NG - 1 else None))
            wait("scalar", "att", sl + 1)
            if sl > 0:
                wait("scalar", "atf", sl)
            for g in range(NG):
                op("scalar", lambda E, g=g: E.activation(bb1[g % 2][:], aps(g), AF.Sigmoid))
                op("scalar", lambda E, g=g: E.activation(
                    gxM[:, g * ASL:(g + 1) * ASL], xps(g), AF.Sigmoid), drain=True)
                op("scalar", lambda E, g=g: E.activation(
                    logaM[:, g * ASL:(g + 1) * ASL], bb1[g % 2][:], AF.Copy,
                    scale=spneg[:, g:g + 1]), sem="ats", drain=(g == NG - 1))
            for g in range(NG):
                op("scalar", lambda E, g=g: E.activation(
                    multM[:, g * ASL:(g + 1) * ASL], logaM[:, g * ASL:(g + 1) * ASL], AF.Exp),
                   sem=("atsB" if g == NG - 1 else None), drain=(g == NG - 1))
            wait("vector", "atsB")
            op("vector", lambda E: E.tensor_scalar(aM[:], multM[:], 1.0, None, OP.min),
               drain=True)
            op("vector", lambda E: E.tensor_mul(logaM[:], aM[:], aM[:]),
               sem="atvB", drain=True)
            wait("scalar", "atvB")
            for g in range(NG):
                op("scalar", lambda E, g=g: E.activation(
                    multM[:, g * ASL:(g + 1) * ASL], logaM[:, g * ASL:(g + 1) * ASL],
                    AF.Sqrt, bias=af32(1.0), scale=af32(-1.0)),
                   sem=("atsC" if g == NG - 1 else None), drain=(g == NG - 1))
            wait("vector", "atsC")
            if sl == 0:
                wait("vector", "ttr", TTRXN[NTC - 1])
            for hh in range(2):
                op("vector", lambda E, hh=hh: E.tensor_mul(
                    [lnscr, xnb][hh][:], multM[:, hh * 1024:(hh + 1) * 1024],
                    gxM[:, hh * 1024:(hh + 1) * 1024], ), sem=("atf" if hh == 1 else None),
                   drain=(hh == 1))
            for gp in range(NG // 2):
                gg = [2 * gp, 2 * gp + 1]
                bidx = [2 * (gp % 2), 2 * (gp % 2) + 1]
                for s in range(2):
                    b = bidx[s]
                    if ybcnt[b] > 0:
                        wait("vector", f"yb{b}", 32 * ybcnt[b])
                for s in range(2):
                    op("vector", lambda E, sl=sl, g=gg[s], s=s: E.tensor_tensor(
                        bb2[s][:], [lnscr, xnb][g // 4][:, (g % 4) * ASL:(g % 4 + 1) * ASL],
                        xnsl(sl, g), OP.mult), drain=(s == 1))
                for s in range(2):
                    if sl == 0:
                        op("vector", lambda E, g=gg[s], s=s, b=bidx[s]: E.tensor_tensor_scan(
                            yslb[b][:], aM[:, g * ASL:(g + 1) * ASL], bb2[s][:], 0.0, OP.mult, OP.add))
                    else:
                        op("vector", lambda E, g=gg[s], s=s, b=bidx[s]: E.tensor_tensor_scan(
                            yslb[b][:], aM[:, g * ASL:(g + 1) * ASL], bb2[s][:],
                            ycol[:, g:g + 1], OP.mult, OP.add))
                for s in range(2):
                    if sl == 0:
                        op("vector", lambda E, g=gg[s], b=bidx[s]: E.tensor_tensor_scan(
                            pslb[b][:], aM[:, g * ASL:(g + 1) * ASL], zasl[:], 1.0, OP.mult, OP.add),
                           drain=(s == 1))
                    else:
                        op("vector", lambda E, g=gg[s], b=bidx[s]: E.tensor_tensor_scan(
                            pslb[b][:], aM[:, g * ASL:(g + 1) * ASL], zasl[:],
                            pcol[:, g:g + 1], OP.mult, OP.add), drain=(s == 1))
                for s in range(2):
                    op("vector", lambda E, g=gg[s], b=bidx[s]: E.tensor_copy(
                        ycol[:, g:g + 1], yslb[b][:, ASL - 1:ASL]))
                for s in range(2):
                    op("vector", lambda E, g=gg[s], b=bidx[s]: E.tensor_copy(
                        pcol[:, g:g + 1], pslb[b][:, ASL - 1:ASL]),
                       sem=("atv" if s == 1 else None), k=2, drain=(s == 1))
                wait("sync", "atv", cur("atv"))
                for s in range(2):
                    b = bidx[s]
                    op("sync", lambda E, sl=sl, g=gg[s], b=b: E.dma_start(
                        out=y_d[g, :, sl * ASL:(sl + 1) * ASL], in_=yslb[b][:]), sem=f"yb{b}", k=16)
                    op("sync", lambda E, sl=sl, g=gg[s], b=b: E.dma_start(
                        out=p_d[g, :, sl * ASL:(sl + 1) * ASL], in_=pslb[b][:]), sem=f"yb{b}", k=16)
                    ybcnt[b] += 1
            if gp == NG // 2 - 1:
                emit_xcl(sl + 2)

        op("vector", lambda E: E.tensor_copy(ccs[:, 0:NG], pcol[:]), fence=True)
        op("vector", lambda E: E.tensor_copy(ccs[:, NG:16], ycol[:]), sem="ccv", drain=True)
        wait("gpsimd", "ccv")
        op("gpsimd", lambda E: E.dma_start(out=cc_in[:, :], in_=ccs[:]), sem="ccc", k=16)
        wait("gpsimd", "ccc", 16)
        op("gpsimd", lambda E: E.collective_compute(
            "AllGather", OP.bypass, replica_groups=[list(range(NCORE))],
            ins=[cc_in.ap().opt()], outs=[cc_out.ap().opt()]), sem="ccl")
        wait("gpsimd", "ccl", 1)
        op("gpsimd", lambda E: E.dma_start(
            out=carr[:].rearrange("p (r c) -> p r c", r=NCORE),
            in_=cc_out.ap().rearrange("(r p) c -> p r c", p=P)), sem="ccc2", k=16)
        wait("vector", "ccc2", 16)
        op("vector", lambda E: E.memset(cc0[:], 0.0), fence=True)
        op("vector", lambda E: E.memset(cs0[:], 0.0), fence=True)
        ccur, cnxt = cc0, cc1
        scur, snxt = cs0, cs1
        for r in range(NCORE):
            op("vector", lambda E, r=r: E.tensor_copy(selq[:], selc[:, r:r + 1]), fence=True)
            op("vector", lambda E, r=r, ccur=ccur, scur=scur, snxt=snxt: E.scalar_tensor_tensor(
                snxt[:], ccur[:], selq[:], scur[:], OP.mult, OP.add), fence=True)
            scur, snxt = snxt, scur
            if r < NCORE - 1:
                if r + 1 == 4:
                    op("vector", lambda E, cnxt=cnxt: E.memset(cnxt[:], 0.0), fence=True)
                else:
                    op("vector", lambda E, r=r, ccur=ccur: E.tensor_tensor(
                        cct[:], carr[:, r * 16:r * 16 + NG], ccur[:], OP.mult), fence=True)
                    op("vector", lambda E, r=r, cnxt=cnxt: E.tensor_tensor(
                        cnxt[:], cct[:], carr[:, r * 16 + NG:r * 16 + 16], OP.add), fence=True)
                ccur, cnxt = cnxt, ccur
        cmine = scur
        if dbg:
            op("vector", lambda E, cmine=cmine: E.tensor_copy(spt2[:], cmine[:]), fence=True)
            wait("gpsimd", "vfn")
            op("gpsimd", lambda E: E.dma_start(out=dbgcc_d[:, 0:128], in_=carr[:]), sem="ccv", k=16)
            op("gpsimd", lambda E: E.dma_start(out=dbgcc_d[:, 128:136], in_=spt2[:]), sem="ccv", k=16)
            op("gpsimd", lambda E: E.dma_start(out=dbgcc_d[:, 152:160], in_=selc[:]), sem="ccv", k=16)
            op("gpsimd", lambda E: E.dma_start(out=dbgcc_d[:, 136:144], in_=pcol[:]), sem="ccv", k=16)
            op("gpsimd", lambda E: E.dma_start(out=dbgcc_d[:, 144:152], in_=ycol[:]), sem="ccv", k=16)
        op("vector", lambda E, cmine=cmine: E.tensor_scalar(
            cct[:], cmine[:], f32(1.0), None, OP.mult), sem="crv", drain=True)

        IH = IQ // 2
        assert not QPEND

        def gu_tail(ent):
            tc2, q2, it2, vqs2 = ent
            par2 = it2 % 2
            sqs2 = qp_schain_gu(TL_GU[par2], P, IQ, vqs2)
            if it2 >= 2:
                wait("vector", f"hst{par2}", 16 * (it2 // 2))
            qp_tail(TL_GU[par2], P, IQ, lambda par2=par2: hqo[par2][0:P, 0:IQ],
                    "hqv", 1, sqs2)
            wait("gpsimd", "hqv")
            op("gpsimd", lambda E, tc2=tc2, q2=q2, par2=par2: E.dma_start(
                out=hq_d[tc2 * P:(tc2 + 1) * P, q2 * IQ:(q2 + 1) * IQ],
                in_=hqo[par2][0:P, 0:IQ]), sem=f"hst{par2}", k=16)
        if dbg:
            wait("gpsimd", "wqtl", 256)
            op("gpsimd", lambda E: E.dma_start(out=dbgar1_d, in_=AR_W[:, :]), sem="ccv", k=16)
        wait("tensor", "wqtl")
        wait("tensor", "ats")
        for tc in range(NTC):
            wait("sync", "xnst0")
            wait("sync", "xnst1")
            if tc == 0:
                wait("sync", "wqst", cur("wqst"))
            if tc >= 2:
                wait("sync", "gut", 4 * (tc - 1))
            op("sync", lambda E, tc=tc: E.dma_start(
                out=xqblk[tc % 2][:].rearrange("p (b t) -> p b t", b=NG),
                in_=xq16_d[:, tc * P:(tc + 1) * P].rearrange("(b d) t -> d b t", b=NG)),
               sem=f"xqcl{tc % 2}", k=16)
            for q in range(NQ):
                it = tc * NQ + q
                psG = PS[0 if it % 2 == 0 else 2]
                psU = PS[1 if it % 2 == 0 else 3]
                if it >= 2:
                    wait("tensor", "gus", it - 1)
                    wait("tensor", "guh", it - 1)
                if q == 0:
                    wait("tensor", f"xqcl{tc % 2}", 16 * (tc // 2 + 1))
                for d in range(NG):
                    lhs = lambda d=d, tc=tc: xqblk[tc % 2][:, d * P:(d + 1) * P]
                    rg = lambda d=d, q=q: AR_W[:, d * DI + q * IQ: d * DI + (q + 1) * IQ]
                    ru = lambda d=d, q=q: AR_W[:, (NG + d) * DI + q * IQ: (NG + d) * DI + (q + 1) * IQ]
                    st0, st1 = (d == 0), (d == NG - 1)
                    op("tensor", lambda E, lhs=lhs, rg=rg, psG=psG, st0=st0, st1=st1: E.matmul(
                        psG[:, 0:512], lhs(), rg()[:, 0:512], start=st0, stop=st1))
                    op("tensor", lambda E, lhs=lhs, rg=rg, psG=psG, st0=st0, st1=st1: E.matmul(
                        psG[:, 512:IQ], lhs(), rg()[:, 512:IQ], start=st0, stop=st1))
                    op("tensor", lambda E, lhs=lhs, ru=ru, psU=psU, st0=st0, st1=st1: E.matmul(
                        psU[:, 0:512], lhs(), ru()[:, 0:512], start=st0, stop=st1))
                    op("tensor", lambda E, lhs=lhs, ru=ru, psU=psU, st0=st0, st1=st1: E.matmul(
                        psU[:, 512:IQ], lhs(), ru()[:, 512:IQ], start=st0, stop=st1),
                       sem=("gut" if st1 else None))
                wait("scalar", "gut")
                if it >= 1:
                    wait("scalar", "guh", it)
                for s in range(2):
                    op("scalar", lambda E, psG=psG, s=s: E.activation(
                        hsil[:, s * IH:(s + 1) * IH], psG[:, s * IH:(s + 1) * IH], AF.Silu),
                       sem=("gus" if s == 1 else None), drain=(s == 1))
                wait("vector", "gus")
                op("vector", lambda E, psU=psU: E.tensor_tensor(
                    hful[:, 0:IQ], hsil[:, 0:IQ], psU[:, 0:IQ], OP.mult),
                   sem="guh", drain=True)
                if dbg:
                    wait("gpsimd", "guh", it + 1)
                    op("gpsimd", lambda E, it=it: E.dma_start(
                        out=dbgh_d[it, :, :], in_=hful[:, 0:IQ]), sem="ccv", k=16)
                par = it % 2
                vqs = qp_head(TL_GU[par], P, IQ, lambda: hful[:, 0:IQ])
                if QPEND:
                    gu_tail(QPEND.pop(0))
                QPEND.append((tc, q, it, vqs))

        if dbg:
            wait("gpsimd", "gut", 64)
            op("gpsimd", lambda E: E.dma_start(out=dbgar2_d, in_=AR_W[:, :]), sem="ccv", k=16)

        while QPEND:
            gu_tail(QPEND.pop(0))

        wait("sync", "gut")
        wait("sync", "ccw", 3)
        for i in range(NID):
            op("sync", lambda E, i=i: E.dma_start_transpose(
                AR_W[:, i * 1024:(i + 1) * 1024], wdq_d.ap()[:, i * P:(i + 1) * P]), sem="wdl", k=16)
        HQOFF = NID * 1024
        wait("sync", "hst0")
        wait("sync", "hst1")
        for b in range(4):
            wait("sync", f"yb{b}")

        def hqsl(i, par):
            off = HQOFF + par * NID * 512 + i * 512
            return AR_W[:, off:off + 512]

        wait("vector", "crv")
        MT = [logaM, aM, gxM, multM]

        def emit_loads(j):
            tt, k = divmod(j, 4)
            tpar = tt % 2
            hc = [2 * k, 2 * k + 1]
            if k == 0:
                if tt >= 2:
                    wait("sync", "dnt", 8 * tt - 8)
                for i in range(NID):
                    op("sync", lambda E, tt=tt, i=i, tpar=tpar: E.dma_start_transpose(
                        hqsl(i, tpar), hq_d[tt * 512:(tt + 1) * 512, i * P:(i + 1) * P]), sem=f"dcl{tpar}", k=16)
            wait("sync", "dnv", 2 * j)
            for s in range(2):
                op("sync", lambda E, tt=tt, hc=hc[s], s=s: E.dma_start(
                    out=xb0[:, s * 512:(s + 1) * 512], in_=y_d[hc, :, tt * 512:(tt + 1) * 512]), sem="rld", k=16)
                op("sync", lambda E, tt=tt, hc=hc[s], s=s: E.dma_start(
                    out=xb1[:, s * 512:(s + 1) * 512], in_=p_d[hc, :, tt * 512:(tt + 1) * 512]), sem="rld", k=16)
                op("sync", lambda E, tt=tt, hc=hc[s], s=s: E.dma_start(
                    out=xnb[:, s * 512:(s + 1) * 512], in_=xT[hc * P:(hc + 1) * P, tt * 512:(tt + 1) * 512]), sem="rld", k=16)

        def emit_mm(j):
            tt = j // 4
            tpar = tt % 2
            PD = PS[j % 2]
            if j == 0:
                wait("tensor", "wdl", 16 * NID)
            if j % 4 == 0:
                wait("tensor", f"dcl{tpar}", 16 * NID * (tt // 2 + 1))
            if j >= 2:
                wait("tensor", "dnv", 2 * (j - 1))
            for s in range(2):
                for i in range(NID):
                    st0, st1 = (i == 0), (i == NID - 1)
                    op("tensor", lambda E, i=i, j=j, s=s, tpar=tpar, PD=PD, st0=st0, st1=st1: E.matmul(
                        PD[:, s * 512:(s + 1) * 512],
                        AR_W[:, i * 1024 + (2 * (j % 4) + s) * P: i * 1024 + (2 * (j % 4) + s + 1) * P],
                        hqsl(i, tpar), start=st0, stop=st1), sem=("dnt" if st1 else None))

        def emit_res(j):
            PD = PS[j % 2]
            hc = [2 * (j % 4), 2 * (j % 4) + 1]
            wait("vector", "dnt", 2 * (j + 1))
            wait("vector", "rld", 16 * 6 * (j + 1))
            if j >= 1:
                wait("vector", "trt", 8 * j)
            for s in range(2):
                op("vector", lambda E, s=s, hc=hc[s], cmine=cmine: E.tensor_copy(
                    cmq[s][:], cmine[:, hc:hc + 1]), drain=(s == 1))
            for s in range(2):
                op("vector", lambda E, s=s: E.scalar_tensor_tensor(
                    res1[s](), xb1[:, s * 512:(s + 1) * 512], cmq[s][:],
                    xb0[:, s * 512:(s + 1) * 512], OP.mult, OP.add), drain=(s == 1))
            for s in range(2):
                op("vector", lambda E, s=s, PD=PD: E.scalar_tensor_tensor(
                    res2[s](), PD[:, s * 512:(s + 1) * 512], f32(1.0), res1[s](), OP.mult, OP.add), drain=(s == 1))
            for s in range(2):
                op("vector", lambda E, s=s: E.tensor_tensor(
                    otc[s](), res2[s](), xnb[:, s * 512:(s + 1) * 512], OP.add),
                   sem=("dnv" if s == 1 else None), k=2, drain=(s == 1))

        def emit_tr(j):
            wait("tensor", "dnv", 2 * (j + 1))
            if j >= 1:
                wait("tensor", "otv", 8 * j)
            for s in range(2):
                for blk in range(4):
                    op("tensor", lambda E, s=s, blk=blk: E.transpose(
                        PS[2][:, s * 512 + blk * P:s * 512 + (blk + 1) * P],
                        lnscr[:, s * 512 + blk * P:s * 512 + (blk + 1) * P], idf[:]),
                       sem=("trt" if (s == 1 and blk == 3) else None), k=8)

        def emit_otv(j):
            tt = j // 4
            hc = [2 * (j % 4), 2 * (j % 4) + 1]
            wait("vector", "trt", 8 * (j + 1))
            if j % 4 == 0 and tt >= 1:
                wait("vector", "ost", 64 * tt)
            for blk in range(4):
                for s in range(2):
                    op("vector", lambda E, blk=blk, s=s, hc=hc[s]: E.tensor_copy(
                        MT[blk][:, hc * P:(hc + 1) * P], PS[2][:, s * 512 + blk * P:s * 512 + (blk + 1) * P]),
                       sem=("otv" if (blk == 3 and s == 1) else None), k=8,
                       drain=(blk == 3 and s == 1))
            if j % 4 == 3:
                wait("gpsimd", "otv", cur("otv"))
                for blk in range(4):
                    op("gpsimd", lambda E, tt=tt, blk=blk: E.dma_start(
                        out=out[tt * 512 + blk * P: tt * 512 + (blk + 1) * P, :],
                        in_=MT[blk][:, 0:1024]), sem="ost", k=16)

        NJP = 16
        for j in range(NJP):
            emit_loads(j)
        for j in range(NJP):
            emit_mm(j)
            if j >= 1:
                emit_tr(j - 1)
            emit_res(j)
            if j >= 1:
                emit_otv(j - 1)
        emit_tr(NJP - 1)
        emit_otv(NJP - 1)

        wait("sync", "ost")
        wait("gpsimd", "ost")

        for (weng, wsem, wv, pos) in WAITLOG:
            assert wv <= cnt[wsem], (
                f"wait({weng}, {wsem}, {wv}) exceeds final total {cnt[wsem]} (hang)")

        block = ctx.enter_context(nc.Block())

        @block.sync
        def _(E):
            for fn in sched["sync"]:
                fn(E)

        @block.vector
        def _(E):
            for fn in sched["vector"]:
                fn(E)

        @block.scalar
        def _(E):
            for fn in sched["scalar"]:
                fn(E)

        @block.tensor
        def _(E):
            for fn in sched["tensor"]:
                fn(E)

        @block.gpsimd
        def _(E):
            for fn in sched["gpsimd"]:
                fn(E)

    return nc


_NC_CACHE = [None]


def kernel(L0_attn_x, L0_attn_a_param, L0_attn_w_in, L0_attn_w_a,
           L0_ffn_w_gate, L0_ffn_w_up, L0_ffn_w_down):
    x = np.asarray(L0_attn_x, dtype=np.float32)
    a_param = np.asarray(L0_attn_a_param, dtype=np.float32)
    w_in = np.ascontiguousarray(np.asarray(L0_attn_w_in, dtype=np.float32))
    w_a = np.ascontiguousarray(np.asarray(L0_attn_w_a, dtype=np.float32))
    wg = np.asarray(L0_ffn_w_gate, dtype=np.float32)
    wu = np.asarray(L0_ffn_w_up, dtype=np.float32)
    wd = np.asarray(L0_ffn_w_down, dtype=np.float32)

    if _NC_CACHE[0] is None:
        _NC_CACHE[0] = build_nc()
    nc = _NC_CACHE[0]

    apg = np.ascontiguousarray(a_param.reshape(NG, P).T)
    identf = np.eye(P, dtype=np.float32)
    in_maps = []
    for c in range(NCORE):
        s, k = divmod(c, 4)
        xs = np.ascontiguousarray(x[s, k * TOK:(k + 1) * TOK, :])
        sel = np.zeros((P, NG), np.float32)
        sel[:, c] = 1.0
        in_maps.append({
            "x_tm": xs,
            "xT": np.ascontiguousarray(xs.T),
            "wg_sl": np.ascontiguousarray(wg[c * WGU:(c + 1) * WGU, :]),
            "wu_sl": np.ascontiguousarray(wu[c * WGU:(c + 1) * WGU, :]),
            "wd_sl": np.ascontiguousarray(wd[c * P:(c + 1) * P, :]),
            "w_in": w_in,
            "w_a": w_a,
            "apg": apg,
            "selcol": sel,
            "identf": identf,
        })
    res = run_bass_kernel_spmd(nc, in_maps, core_ids=list(range(NCORE)))
    outp = np.empty((2, 8192, D), np.float32)
    for c in range(NCORE):
        s, k = divmod(c, 4)
        outp[s, k * TOK:(k + 1) * TOK, :] = res.results[c]["out"]
    return outp


# version 31
# speedup vs baseline: 1.0070x; 1.0042x over previous
import numpy as np
from contextlib import ExitStack
import concourse.bass as bass
import concourse.mybir as mybir
from concourse.bass_utils import run_bass_kernel_spmd

F32 = mybir.dt.float32
BF16 = mybir.dt.bfloat16
I32 = mybir.dt.int32
AF = mybir.ActivationFunctionType
OP = mybir.AluOpType
AX = mybir.AxisListType

NCORE = 8
P = 128
TOK = 2048
D = 1024
NG = 8
DI = 2816
IQ = 704
NQ = 4
ASL = 256
NASL = TOK // ASL
NTC = TOK // P
NPR = NTC // 2
NID = DI // P
WGU = 352
C6TH = float(np.float32(1.0 / 6.0))
f32 = lambda v: float(np.float32(v))
af32 = np.float32


def build_nc(dbg=False):
    dk = (lambda: {"kind": "ExternalOutput"}) if dbg else (lambda: {})
    nc = bass.Bass("TRN2", target_bir_lowering=False, debug=False)

    x_tm = nc.dram_tensor("x_tm", [TOK, D], F32, kind="ExternalInput").ap()
    xT = nc.dram_tensor("xT", [D, TOK], F32, kind="ExternalInput").ap()
    wg_sl = nc.dram_tensor("wg_sl", [WGU, D], F32, kind="ExternalInput").ap()
    wu_sl = nc.dram_tensor("wu_sl", [WGU, D], F32, kind="ExternalInput").ap()
    wd_sl = nc.dram_tensor("wd_sl", [P, DI], F32, kind="ExternalInput").ap()
    w_in = nc.dram_tensor("w_in", [NG, P, P], F32, kind="ExternalInput").ap()
    w_a = nc.dram_tensor("w_a", [NG, P, P], F32, kind="ExternalInput").ap()
    apg = nc.dram_tensor("apg", [P, NG], F32, kind="ExternalInput").ap()
    selcol = nc.dram_tensor("selcol", [P, NG], F32, kind="ExternalInput").ap()
    identf = nc.dram_tensor("identf", [P, P], F32, kind="ExternalInput").ap()
    out = nc.dram_tensor("out", [TOK, D], F32, kind="ExternalOutput").ap()

    xn16_d = nc.dram_tensor("xn16_d", [D, TOK], BF16, **dk()).ap()
    xq16_d = nc.dram_tensor("xq16_d", [D, TOK], BF16, **dk()).ap()
    wgq_sl_d = nc.dram_tensor("wgq_sl_d", [WGU, D], BF16)
    wuq_sl_d = nc.dram_tensor("wuq_sl_d", [WGU, D], BF16)
    wdq_sl_d = nc.dram_tensor("wdq_sl_d", [P, DI], BF16)
    wgq_d = nc.dram_tensor("wgq_d", [DI, D], BF16)
    wuq_d = nc.dram_tensor("wuq_d", [DI, D], BF16)
    wdq_d = nc.dram_tensor("wdq_d", [D, DI], BF16)
    y_d = nc.dram_tensor("y_d", [NG, P, TOK], F32, **dk()).ap()
    p_d = nc.dram_tensor("p_d", [NG, P, TOK], F32, **dk()).ap()
    hq_d = nc.dram_tensor("hq_d", [TOK, DI], BF16, **dk()).ap()
    dbgcc_d = nc.dram_tensor("dbgcc_d", [P, 160], F32, **dk()).ap() if True else None
    dbgwg_d = nc.dram_tensor("dbgwg_d", [DI, D], BF16, **dk()).ap() if dbg else None
    dbgwu_d = nc.dram_tensor("dbgwu_d", [DI, D], BF16, **dk()).ap() if dbg else None
    dbgh_d = nc.dram_tensor("dbgh_d", [NTC * NQ, P, IQ], F32, **dk()).ap() if dbg else None
    dbgar1_d = nc.dram_tensor("dbgar1_d", [P, 16 * DI], BF16, **dk()).ap() if dbg else None
    dbgar2_d = nc.dram_tensor("dbgar2_d", [P, 16 * DI], BF16, **dk()).ap() if dbg else None
    cc_in = nc.dram_tensor("cc_in", [P, 16], F32)
    cc_out = nc.dram_tensor("cc_out", [NCORE * P, 16], F32)

    ctx = ExitStack()
    with ctx:
        def sb(name, shape, dt):
            return ctx.enter_context(nc.sbuf_tensor(name, shape, dt))

        AR_W = sb("AR_W", [P, 16 * DI], BF16)
        xqblk = [sb("xqblk0", [P, NG * P], BF16), sb("xqblk1", [P, NG * P], BF16)]

        qv = sb("qv", [P, 1024], F32)
        qi = sb("qi", [P, 1024], I32)
        qi2 = sb("qi2", [P, IQ], I32)
        qc1 = sb("qc1", [P, 1024], BF16)
        qc2 = sb("qc2", [P, 1024], BF16)
        qc3 = sb("qc3", [P, 1024], BF16)
        qa1 = sb("qa1", [P, 1024], BF16)
        qsg = sb("qsg", [P, 1024], BF16)
        qrse = sb("qrse", [P, 1024], F32)
        qav = qrse
        qsse = sb("qsse", [P, 1024], BF16)
        qbm = sb("qbm", [P, 192], F32)
        qsc = sb("qsc", [P, 192], F32)
        qrs = sb("qrs", [P, 192], F32)
        qsb = sb("qsb", [P, 192], BF16)
        qsb2 = sb("qsb2", [P, 192], BF16)

        xb0 = sb("xb0", [P, 1024], F32)
        xb1 = sb("xb1", [P, 1024], F32)
        xnb = sb("xnb", [P, 1024], F32)
        xn16t = [sb("xn16t0", [P, 1024], BF16), sb("xn16t1", [P, 1024], BF16)]
        xqo = [sb("xqo0", [P, 1024], BF16), sb("xqo1", [P, 1024], BF16)]
        lnscr = sb("lnscr", [P, 1024], F32)
        lnsx = [[sb(f"lnsx{pp}{s}", [P, 1], F32) for s in range(2)] for pp in range(2)]
        lnsq = [[sb(f"lnsq{pp}{s}", [P, 1], F32) for s in range(2)] for pp in range(2)]
        lnm = [[sb(f"lnm{pp}{s}", [P, 1], F32) for s in range(2)] for pp in range(2)]
        lnr1 = [[sb(f"lnr1{pp}{s}", [P, 1], F32) for s in range(2)] for pp in range(2)]
        lnrs = [[sb(f"lnrs{pp}{s}", [P, 1], F32) for s in range(2)] for pp in range(2)]
        lnm2 = [sb(f"lnm2{s}", [P, 1], F32) for s in range(2)]
        lnv0 = [sb(f"lnv0{s}", [P, 1], F32) for s in range(2)]
        lnv1 = [sb(f"lnv1{s}", [P, 1], F32) for s in range(2)]

        wbuf = sb("wbuf", [P, 1024], F32)
        wqo = xqblk[1]

        logaM = sb("logaM", [P, NG * ASL], F32)
        aM = sb("aM", [P, NG * ASL], F32)
        gxM = sb("gxM", [P, NG * ASL], F32)
        multM = sb("multM", [P, NG * ASL], F32)
        xnslM = sb("xnslM", [P, NG * ASL], BF16)
        bb1 = [sb("bb1a", [P, ASL], F32), sb("bb1b", [P, ASL], F32)]
        bb2 = [sb("bb2a", [P, ASL], F32), sb("bb2b", [P, ASL], F32)]
        yslb = [sb(f"ysl{i}", [P, ASL], F32) for i in range(4)]
        pslb = [sb(f"psl{i}", [P, ASL], F32) for i in range(4)]
        zasl = sb("zasl", [P, ASL], F32)
        ycol = sb("ycol", [P, NG], F32)
        pcol = sb("pcol", [P, NG], F32)
        winb = sb("winb", [P, NG * P], BF16)
        wab = sb("wab", [P, NG * P], BF16)
        spu = sb("spu", [P, NG], F32)
        spt1 = sb("spt1", [P, NG], F32)
        spt2 = sb("spt2", [P, NG], F32)
        spneg = sb("spneg", [P, NG], F32)
        selc = sb("selc", [P, NG], F32)
        idf = sb("idf", [P, P], F32)
        ccs = sb("ccs", [P, 16], F32)
        carr = sb("carr", [P, NCORE * 16], F32)
        cc0 = sb("cc0", [P, NG], F32)
        cc1 = sb("cc1", [P, NG], F32)
        cct = sb("cct", [P, NG], F32)
        cs0 = sb("cs0", [P, NG], F32)
        cs1 = sb("cs1", [P, NG], F32)
        selq = sb("selq", [P, 1], F32)
        cmq = [sb("cmqa", [P, 1], F32), sb("cmqb", [P, 1], F32)]

        hsil = xnb
        hful = lnscr
        hqo = xqo

        otc = [lambda: lnscr[:, 0:512], lambda: lnscr[:, 512:1024]]
        res1 = [lambda: qrse[:, 0:512], lambda: qrse[:, 512:1024]]
        res2 = [lambda: qv[:, 0:512], lambda: qv[:, 512:1024]]

        PS = [ctx.enter_context(nc.psum_tensor(f"PS{i}", [P, 1024], F32)) for i in range(4)]

        sem_names = [
            "xld0", "xld1", "mld", "mld2", "wld", "spv", "tms", "tmv", "tms2", "tmv2",
            "xnst0", "xnst1",
            "wqv", "wqst", "ccw", "xcl0", "xcl1", "att", "ats", "atsB", "atsC", "atf",
            "atv", "atvB", "yb0", "yb1", "yb2", "yb3",
            "ccv", "ccc", "ccl", "ccc2", "crv", "xqcl0", "xqcl1", "wqtl", "wdl", "gut", "gus",
            "guh", "hqv", "hst0", "hst1", "dcl0", "dcl1", "rld", "dnt", "dnv", "trt", "otv",
            "ost", "qvs", "qss", "xnv", "xqv", "ttr", "tev", "vfn",
        ]
        sems = {n: ctx.enter_context(nc.semaphore(n)) for n in sem_names}

        ENG = ("sync", "vector", "scalar", "tensor", "gpsimd")
        sched = {e: [] for e in ENG}
        cnt = {n: 0 for n in sem_names}

        def op(eng, build, sem=None, k=1, drain=False, fence=False):
            if fence and sem is None:
                sem, k, drain = "vfn", 1, True
            if sem is not None:
                cnt[sem] += k

            def fn(E, build=build, sem=sem, k=k, drain=drain):
                ins = build(E)
                if drain:
                    d = E.drain()
                    if sem is not None:
                        d.then_inc(sems[sem], k)
                elif sem is not None:
                    ins.then_inc(sems[sem], k)
            sched[eng].append(fn)
            return None

        WAITLOG = []

        def wait(eng, sem, val=None):
            v = cnt[sem] if val is None else val
            if v <= 0:
                return
            WAITLOG.append((eng, sem, v, len(sched[eng])))
            sched[eng].append(lambda E, sem=sem, v=v: E.wait_ge(sems[sem], v))

        def cur(sem):
            return cnt[sem]

        def emit_quant2(rows, width, src, dst, dst_sem=None, dst_k=1, soff=True):
            nb = width // 16
            R = rows
            V = lambda fn, dr=False, sem=None, k=1: op("vector", fn, drain=dr, sem=sem, k=k)
            S = lambda fn, dr=False, sem=None: op("scalar", fn, drain=dr, sem=sem)
            V(lambda E: E.reduce_max(
                qbm[0:R, 0:nb].unsqueeze(2).unsqueeze(3),
                src(0, width).rearrange("p (a b c) -> p a b c", b=4, c=4),
                axis=AX.XY, apply_absolute_value=True), dr=True)
            V(lambda E: E.tensor_scalar(qsc[0:R, 0:nb], qbm[0:R, 0:nb], C6TH, 1e-12, OP.mult, OP.max), dr=True)
            V(lambda E: E.reciprocal(qrs[0:R, 0:nb], qsc[0:R, 0:nb]))
            V(lambda E: E.tensor_copy(qsb[0:R, 0:nb], qsc[0:R, 0:nb]), dr=True)
            V(lambda E: E.tensor_copy(
                qrse[0:R, 0:width].rearrange("p (a b) -> p a b", a=nb),
                qrs[0:R, 0:nb].unsqueeze(2).broadcast_to([R, nb, 16])))
            V(lambda E: E.tensor_copy(
                qsse[0:R, 0:width].rearrange("p (a b) -> p a b", a=nb),
                qsb[0:R, 0:nb].unsqueeze(2).broadcast_to([R, nb, 16])), dr=True)
            V(lambda E: E.tensor_tensor(qv[0:R, 0:width], src(0, width), qrse[0:R, 0:width], OP.mult),
              dr=True, sem="qvs")
            wait("scalar", "qvs")
            if soff:
                S(lambda E: E.activation(qav[0:R, 0:width], qv[0:R, 0:width], AF.Abs), dr=True)
                S(lambda E: E.activation(qsg[0:R, 0:width], qv[0:R, 0:width], AF.Sign))
                S(lambda E: E.activation(qi[0:R, 0:width], qav[0:R, 0:width], AF.Copy,
                                         scale=af32(2.0)), dr=True)
                S(lambda E: E.activation(qa1[0:R, 0:width], qi[0:R, 0:width], AF.Copy,
                                         scale=af32(0.5)), dr=True)
                S(lambda E: E.activation(qi[0:R, 0:width], qav[0:R, 0:width], AF.Copy,
                                         scale=af32(1.0)), dr=True)
                S(lambda E: E.activation(qc2[0:R, 0:width], qi[0:R, 0:width], AF.Copy,
                                         scale=af32(1.0)), dr=True)
                S(lambda E: E.activation(qi[0:R, 0:width], qav[0:R, 0:width], AF.Copy,
                                         scale=af32(0.125), bias=float(-0.125)), dr=True)
                S(lambda E: E.activation(qc3[0:R, 0:width], qi[0:R, 0:width], AF.Copy,
                                         scale=af32(2.0)), dr=True, sem="qss")
                wait("vector", "qss")
                V(lambda E: E.tensor_scalar(qc1[0:R, 0:width], qa1[0:R, 0:width], f32(2.0), None, OP.min), dr=True)
                V(lambda E: E.tensor_scalar(qa1[0:R, 0:width], qc2[0:R, 0:width], f32(2.0), f32(4.0), OP.max, OP.min), dr=True)
                V(lambda E: E.tensor_scalar(qc2[0:R, 0:width], qc3[0:R, 0:width], f32(-2.0), None, OP.add), dr=True)
                V(lambda E: E.tensor_tensor(qc3[0:R, 0:width], qc1[0:R, 0:width], qa1[0:R, 0:width], OP.add), dr=True)
                V(lambda E: E.tensor_tensor(qc1[0:R, 0:width], qc3[0:R, 0:width], qc2[0:R, 0:width], OP.add), dr=True)
                V(lambda E: E.tensor_tensor(qa1[0:R, 0:width], qc1[0:R, 0:width], qsg[0:R, 0:width], OP.mult), dr=True)
                V(lambda E: E.tensor_tensor(dst(0, width), qa1[0:R, 0:width], qsse[0:R, 0:width], OP.mult),
                  dr=True, sem=dst_sem, k=dst_k)
            else:
                S(lambda E: E.activation(qav[0:R, 0:width], qv[0:R, 0:width], AF.Abs))
                S(lambda E: E.activation(qsg[0:R, 0:width], qv[0:R, 0:width], AF.Sign),
                  dr=True, sem="qss")
                wait("vector", "qss")
                V(lambda E: E.tensor_scalar(qi[0:R, 0:width], qav[0:R, 0:width], f32(2.0), None, OP.mult), dr=True)
                V(lambda E: E.tensor_copy(qa1[0:R, 0:width], qi[0:R, 0:width]), dr=True)
                V(lambda E: E.tensor_scalar(qi[0:R, 0:width], qav[0:R, 0:width], f32(1.0), None, OP.mult))
                V(lambda E: E.tensor_scalar(qc1[0:R, 0:width], qa1[0:R, 0:width], f32(4.0), f32(0.5), OP.min, OP.mult), dr=True)
                V(lambda E: E.tensor_copy(qa1[0:R, 0:width], qi[0:R, 0:width]), dr=True)
                V(lambda E: E.tensor_scalar(qi[0:R, 0:width], qav[0:R, 0:width], f32(0.125), f32(-0.125), OP.mult, OP.add))
                V(lambda E: E.tensor_scalar(qc2[0:R, 0:width], qa1[0:R, 0:width], f32(2.0), f32(4.0), OP.max, OP.min), dr=True)
                V(lambda E: E.tensor_copy(qa1[0:R, 0:width], qi[0:R, 0:width]), dr=True)
                V(lambda E: E.tensor_scalar(qc3[0:R, 0:width], qa1[0:R, 0:width], f32(2.0), f32(-2.0), OP.mult, OP.add), dr=True)
                V(lambda E: E.tensor_tensor(qa1[0:R, 0:width], qc1[0:R, 0:width], qc2[0:R, 0:width], OP.add), dr=True)
                V(lambda E: E.tensor_tensor(qc1[0:R, 0:width], qa1[0:R, 0:width], qc3[0:R, 0:width], OP.add), dr=True)
                V(lambda E: E.tensor_tensor(qc2[0:R, 0:width], qc1[0:R, 0:width], qsg[0:R, 0:width], OP.mult), dr=True)
                V(lambda E: E.tensor_tensor(dst(0, width), qc2[0:R, 0:width], qsse[0:R, 0:width], OP.mult),
                  dr=True, sem=dst_sem, k=dst_k)

        def qp_head(T, R, W, srcf):
            V = lambda fn, dr=False, sem=None: op("vector", fn, drain=dr, sem=sem)
            nb = W // 16
            V(lambda E: E.reduce_max(
                qbm[0:R, 0:nb].unsqueeze(2).unsqueeze(3),
                srcf().rearrange("p (a b c) -> p a b c", b=4, c=4),
                axis=AX.XY, apply_absolute_value=True), dr=True)
            V(lambda E: E.tensor_scalar(qsc[0:R, 0:nb], qbm[0:R, 0:nb], C6TH, 1e-12, OP.mult, OP.max), dr=True)
            V(lambda E: E.reciprocal(qrs[0:R, 0:nb], qsc[0:R, 0:nb]))
            V(lambda E: E.tensor_copy(T["qsb"](R, nb), qsc[0:R, 0:nb]), dr=True)
            V(lambda E: E.tensor_copy(
                qrse[0:R, 0:W].rearrange("p (a b) -> p a b", a=nb),
                qrs[0:R, 0:nb].unsqueeze(2).broadcast_to([R, nb, 16])), dr=True)
            V(lambda E: E.tensor_tensor(T["qv"](R, W), srcf(), qrse[0:R, 0:W], OP.mult),
              dr=True, sem="qvs")
            return cur("qvs")

        def qp_schain_gu(T, R, W, vqs):
            S = lambda fn, dr=False, sem=None: op("scalar", fn, drain=dr, sem=sem)
            wait("scalar", "qvs", vqs)
            S(lambda E: E.activation(T["qav"](R, W), T["qv"](R, W), AF.Abs))
            S(lambda E: E.activation(T["qsg"](R, W), T["qv"](R, W), AF.Sign))
            S(lambda E: E.activation(qi[0:R, 0:W], T["qv"](R, W), AF.Abs, scale=af32(2.0)))
            S(lambda E: E.activation(qi2[0:R, 0:W], T["qv"](R, W), AF.Abs, scale=af32(1.0)), dr=True)
            S(lambda E: E.activation(T["qa1"](R, W), qi[0:R, 0:W], AF.Copy, scale=af32(0.5)))
            S(lambda E: E.activation(T["qc2"](R, W), qi2[0:R, 0:W], AF.Copy, scale=af32(1.0)), dr=True)
            S(lambda E: E.activation(qi[0:R, 0:W], T["qav"](R, W), AF.Copy,
                                     scale=af32(0.125), bias=float(-0.125)), dr=True)
            S(lambda E: E.activation(T["qc3"](R, W), qi[0:R, 0:W], AF.Copy, scale=af32(2.0)),
              dr=True, sem="qss")
            return cur("qss")

        def qp_schain(T, R, W, vqs):
            S = lambda fn, dr=False, sem=None: op("scalar", fn, drain=dr, sem=sem)
            wait("scalar", "qvs", vqs)
            S(lambda E: E.activation(T["qav"](R, W), T["qv"](R, W), AF.Abs), dr=True)
            S(lambda E: E.activation(T["qsg"](R, W), T["qv"](R, W), AF.Sign))
            S(lambda E: E.activation(qi[0:R, 0:W], T["qav"](R, W), AF.Copy, scale=af32(2.0)), dr=True)
            S(lambda E: E.activation(T["qa1"](R, W), qi[0:R, 0:W], AF.Copy, scale=af32(0.5)), dr=True)
            S(lambda E: E.activation(qi[0:R, 0:W], T["qav"](R, W), AF.Copy, scale=af32(1.0)), dr=True)
            S(lambda E: E.activation(T["qc2"](R, W), qi[0:R, 0:W], AF.Copy, scale=af32(1.0)), dr=True)
            S(lambda E: E.activation(qi[0:R, 0:W], T["qav"](R, W), AF.Copy,
                                     scale=af32(0.125), bias=float(-0.125)), dr=True)
            S(lambda E: E.activation(T["qc3"](R, W), qi[0:R, 0:W], AF.Copy, scale=af32(2.0)),
              dr=True, sem="qss")
            return cur("qss")

        def qp_tail(T, R, W, dstf, dst_sem, dst_k, sqs):
            V = lambda fn, dr=False, sem=None, k=1: op("vector", fn, drain=dr, sem=sem, k=k)
            nb = W // 16
            wait("vector", "qss", sqs)
            V(lambda E: E.tensor_scalar(qc1[0:R, 0:W], T["qa1"](R, W), f32(2.0), None, OP.min), dr=True)
            V(lambda E: E.tensor_scalar(T["qa1"](R, W), T["qc2"](R, W), f32(2.0), f32(4.0), OP.max, OP.min), dr=True)
            V(lambda E: E.tensor_scalar(T["qc2"](R, W), T["qc3"](R, W), f32(-2.0), None, OP.add), dr=True)
            V(lambda E: E.tensor_tensor(T["qc3"](R, W), qc1[0:R, 0:W], T["qa1"](R, W), OP.add), dr=True)
            V(lambda E: E.tensor_copy(
                qsse[0:R, 0:W].rearrange("p (a b) -> p a b", a=nb),
                T["qsb"](R, nb).unsqueeze(2).broadcast_to([R, nb, 16])))
            V(lambda E: E.tensor_tensor(qc1[0:R, 0:W], T["qc3"](R, W), T["qc2"](R, W), OP.add), dr=True)
            V(lambda E: E.tensor_tensor(T["qa1"](R, W), qc1[0:R, 0:W], T["qsg"](R, W), OP.mult), dr=True)
            V(lambda E: E.tensor_tensor(dstf(), T["qa1"](R, W), qsse[0:R, 0:W], OP.mult),
              dr=True, sem=dst_sem, k=dst_k)

        def qp_schain_lite(T, R, W, vqs):
            S = lambda fn, dr=False, sem=None: op("scalar", fn, drain=dr, sem=sem)
            wait("scalar", "qvs", vqs)
            S(lambda E: E.activation(T["qav"](R, W), T["qv"](R, W), AF.Abs))
            S(lambda E: E.activation(T["qsg"](R, W), T["qv"](R, W), AF.Sign),
              dr=True, sem="qss")
            return cur("qss")

        def qp_tail_v(T, R, W, dstf, dst_sem, dst_k, sqs):
            V = lambda fn, dr=False, sem=None, k=1: op("vector", fn, drain=dr, sem=sem, k=k)
            nb = W // 16
            wait("vector", "qss", sqs)
            V(lambda E: E.tensor_scalar(qi[0:R, 0:W], T["qav"](R, W), f32(2.0), None, OP.mult), dr=True)
            V(lambda E: E.tensor_copy(qa1[0:R, 0:W], qi[0:R, 0:W]), dr=True)
            V(lambda E: E.tensor_scalar(qi[0:R, 0:W], T["qav"](R, W), f32(1.0), None, OP.mult))
            V(lambda E: E.tensor_scalar(qc1[0:R, 0:W], qa1[0:R, 0:W], f32(4.0), f32(0.5), OP.min, OP.mult), dr=True)
            V(lambda E: E.tensor_copy(qa1[0:R, 0:W], qi[0:R, 0:W]), dr=True)
            V(lambda E: E.tensor_scalar(qi[0:R, 0:W], T["qav"](R, W), f32(0.125), f32(-0.125), OP.mult, OP.add))
            V(lambda E: E.tensor_scalar(qc2[0:R, 0:W], qa1[0:R, 0:W], f32(2.0), f32(4.0), OP.max, OP.min), dr=True)
            V(lambda E: E.tensor_copy(qa1[0:R, 0:W], qi[0:R, 0:W]), dr=True)
            V(lambda E: E.tensor_scalar(qc3[0:R, 0:W], qa1[0:R, 0:W], f32(2.0), f32(-2.0), OP.mult, OP.add), dr=True)
            V(lambda E: E.tensor_tensor(qa1[0:R, 0:W], qc1[0:R, 0:W], qc2[0:R, 0:W], OP.add), dr=True)
            V(lambda E: E.tensor_copy(
                qsse[0:R, 0:W].rearrange("p (a b) -> p a b", a=nb),
                T["qsb"](R, nb).unsqueeze(2).broadcast_to([R, nb, 16])))
            V(lambda E: E.tensor_tensor(qc1[0:R, 0:W], qa1[0:R, 0:W], qc3[0:R, 0:W], OP.add), dr=True)
            V(lambda E: E.tensor_tensor(qc2[0:R, 0:W], qc1[0:R, 0:W], T["qsg"](R, W), OP.mult), dr=True)
            V(lambda E: E.tensor_tensor(dstf(), qc2[0:R, 0:W], qsse[0:R, 0:W], OP.mult),
              dr=True, sem=dst_sem, k=dst_k)

        TL_TM = [
            {"qv": lambda R, W: qv[0:R, 0:W], "qav": lambda R, W: aM[0:R, 0:W],
             "qa1": lambda R, W: qa1[0:R, 0:W], "qc2": lambda R, W: qc2[0:R, 0:W],
             "qc3": lambda R, W: qc3[0:R, 0:W], "qsg": lambda R, W: qsg[0:R, 0:W],
             "qsb": lambda R, nb: qsb[0:R, 0:nb]},
            {"qv": lambda R, W: logaM[0:R, 0:W], "qav": lambda R, W: aM[0:R, 1024:1024 + W],
             "qa1": lambda R, W: xqblk[0][0:R, 0:W], "qc2": lambda R, W: xqblk[1][0:R, 0:W],
             "qc3": lambda R, W: xnslM[0:R, 1024:1024 + W], "qsg": lambda R, W: winb[0:R, 0:W],
             "qsb": lambda R, nb: qsb2[0:R, 0:nb]},
        ]
        TL_W = [
            {"qv": lambda R, W: qv[0:R, 0:W], "qav": lambda R, W: aM[0:R, 0:W],
             "qsg": lambda R, W: qsg[0:R, 0:W], "qsb": lambda R, nb: qsb[0:R, 0:nb]},
            {"qv": lambda R, W: xb0[0:R, 0:W], "qav": lambda R, W: xb1[0:R, 0:W],
             "qsg": lambda R, W: xqblk[0][0:R, 0:W], "qsb": lambda R, nb: qsb2[0:R, 0:nb]},
        ]
        TL_GU = [
            TL_TM[0],
            {"qv": lambda R, W: logaM[0:R, 0:W], "qav": lambda R, W: aM[0:R, 1024:1024 + W],
             "qa1": lambda R, W: winb[0:R, 0:W], "qc2": lambda R, W: wab[0:R, 0:W],
             "qc3": lambda R, W: xn16t[0][0:R, 0:W], "qsg": lambda R, W: xn16t[1][0:R, 0:W],
             "qsb": lambda R, nb: qsb2[0:R, 0:nb]},
        ]

        op("sync", lambda E: E.dma_start(out=idf[:], in_=identf), sem="mld", k=16)
        op("sync", lambda E: E.dma_start(out=selc[:], in_=selcol), sem="mld", k=16)
        op("sync", lambda E: E.dma_start(out=spu[:], in_=apg), sem="mld", k=16)
        def emit_mld2():
            wait("gpsimd", "xnst0", 256)
            wait("gpsimd", "xnst1", 256)
            op("gpsimd", lambda E: E.dma_start(
                out=winb[:].rearrange("d (h e) -> d h e", h=NG),
                in_=w_in.rearrange("h d e -> d h e")), sem="mld2", k=16)
            op("gpsimd", lambda E: E.dma_start(
                out=wab[:].rearrange("d (h e) -> d h e", h=NG),
                in_=w_a.rearrange("h d e -> d h e")), sem="mld2", k=16)
        def emit_xloads(pr):
            op("sync", lambda E, pr=pr: E.dma_start(
                out=xb0[:], in_=x_tm[2 * pr * P:(2 * pr + 1) * P, :]), sem="xld0", k=16)
            op("sync", lambda E, pr=pr: E.dma_start(
                out=xb1[:], in_=x_tm[(2 * pr + 1) * P:(2 * pr + 2) * P, :]), sem="xld1", k=16)
        emit_xloads(0)

        wait("scalar", "mld", 48)
        op("scalar", lambda E: E.activation(spt1[:], spu[:], AF.Exp), sem="spv", drain=True)
        wait("vector", "spv")
        op("vector", lambda E: E.tensor_scalar(spt2[:], spt1[:], f32(-0.25), f32(1.0 / 3.0), OP.mult, OP.add), fence=True)
        op("vector", lambda E: E.scalar_tensor_tensor(spneg[:], spt2[:], f32(-1.0), spt1[:], OP.mult, OP.mult), fence=True)
        op("vector", lambda E: E.tensor_scalar(spt2[:], spneg[:], f32(0.5), None, OP.add), fence=True)
        op("vector", lambda E: E.scalar_tensor_tensor(spneg[:], spt2[:], f32(-1.0), spt1[:], OP.mult, OP.mult), fence=True)
        op("vector", lambda E: E.tensor_scalar(spt2[:], spneg[:], f32(1.0), None, OP.add), fence=True)
        op("vector", lambda E: E.tensor_mul(spu[:], spt2[:], spt1[:]), fence=True)
        op("vector", lambda E: E.tensor_scalar(spneg[:], spu[:], f32(-8.0), None, OP.mult), fence=True)
        op("vector", lambda E: E.memset(zasl[:], 0.0), fence=True)
        op("vector", lambda E: E.memset(xnslM[:, 0:1024], 1.0), sem="spv", drain=True)

        XB = [xb0, xb1]
        SCR2 = [lnscr, xnb]
        SSC = [qv, qrse]
        QPEND = []
        TTRXN, TEVXN, TTRXQ, TEVXQ = {}, {}, {}, {}

        def pend_pop():
            ent = QPEND.pop(0)
            if ent[0] == "tm":
                _, tc2, s2, vqs2 = ent
                sqs2 = qp_schain(TL_TM[s2], P, D, vqs2)
                qp_tail(TL_TM[s2], P, D, lambda s2=s2: gxM[:, s2 * 1024:(s2 + 1) * 1024],
                        "xqv", 1, sqs2)
                wait("tensor", "xqv", tc2 + 1)
                if tc2 >= 1:
                    wait("tensor", "tev", TEVXQ[tc2 - 1])
                for b in range(8):
                    op("tensor", lambda E, s2=s2, b=b: E.transpose(
                        PS[3][:, b * P:(b + 1) * P], gxM[:, s2 * 1024 + b * P:s2 * 1024 + (b + 1) * P],
                        idf[:]), sem=("ttr" if b == 7 else None))
                TTRXQ[tc2] = cur("ttr")
                wait("scalar", "ttr", TTRXQ[tc2])
                wait("scalar", f"xnst{s2}", 32 * (tc2 // 2))
                op("scalar", lambda E, s2=s2: E.activation(
                    xqo[s2][:, 0:D], PS[3][:, 0:D], AF.Copy), sem="tev", drain=True)
                TEVXQ[tc2] = cur("tev")
                wait("sync", "tev", TEVXQ[tc2])
                op("sync", lambda E, tc2=tc2, s2=s2: E.dma_start(
                    out=xq16_d[:, tc2 * P:(tc2 + 1) * P].rearrange("(b d) t -> d b t", b=8),
                    in_=xqo[s2][:].rearrange("p (b t) -> p b t", b=8)), sem=f"xnst{s2}", k=16)
            else:
                _, k2, rr2, ww2, dst2, vqs2 = ent
                p2 = k2 % 2
                sqs2 = qp_schain_lite(TL_W[p2], rr2, ww2, vqs2)
                wait("vector", "wqst", 16 * k2)
                qp_tail_v(TL_W[p2], rr2, ww2,
                          lambda rr2=rr2, ww2=ww2: wqo[0:rr2, 0:ww2], "wqv", 1, sqs2)
                wait("gpsimd", "wqv")
                op("gpsimd", lambda E, dst2=dst2, rr2=rr2, ww2=ww2: E.dma_start(
                    out=dst2, in_=wqo[0:rr2, 0:ww2]), sem="wqst", k=16)

        for it in range(NPR):
            if True:
                pr = it
                pp = pr % 2
                wait("vector", "xld0", 16 * (pr + 1))
                wait("vector", "xld1", 16 * (pr + 1))
                if pr >= 2:
                    wait("vector", "tms2", pr - 1)
                for s in range(2):
                    op("vector", lambda E, s=s: E.tensor_tensor(
                        SCR2[s][:], XB[s][:], XB[s][:], OP.mult))
                for s in range(2):
                    op("vector", lambda E, s=s: E.tensor_tensor_scan(
                        SSC[s][:], xnslM[:, 0:1024], XB[s][:], 0.0, OP.mult, OP.add), drain=(s == 1))
                for s in range(2):
                    op("vector", lambda E, s=s, pp=pp: E.tensor_copy(
                        lnsx[pp][s][:], SSC[s][:, 1023:1024]), drain=(s == 1))
                for s in range(2):
                    op("vector", lambda E, s=s: E.tensor_tensor_scan(
                        SSC[s][:], xnslM[:, 0:1024], SCR2[s][:], 0.0, OP.mult, OP.add), drain=(s == 1))
                for s in range(2):
                    op("vector", lambda E, s=s, pp=pp: E.tensor_copy(
                        lnsq[pp][s][:], SSC[s][:, 1023:1024]))
                for s in range(2):
                    op("vector", lambda E, s=s, pp=pp: E.tensor_scalar(
                        lnm[pp][s][:], lnsx[pp][s][:], f32(1.0 / D), None, OP.mult), drain=(s == 1))
                for s in range(2):
                    op("vector", lambda E, s=s, pp=pp: E.tensor_mul(
                        lnm2[s][:], lnm[pp][s][:], lnm[pp][s][:]), drain=(s == 1))
                for s in range(2):
                    op("vector", lambda E, s=s, pp=pp: E.scalar_tensor_tensor(
                        lnv0[s][:], lnsq[pp][s][:], f32(1.0 / D), lnm2[s][:], OP.mult, OP.subtract), drain=(s == 1))
                for s in range(2):
                    op("vector", lambda E, s=s: E.tensor_scalar(
                        lnv1[s][:], lnv0[s][:], f32(1e-5), None, OP.add), drain=(s == 1))
                for s in range(2):
                    op("vector", lambda E, s=s, pp=pp: E.reciprocal(
                        lnr1[pp][s][:], lnv1[s][:]),
                       sem=("tmv" if s == 1 else None), drain=(s == 1))
                wait("scalar", "tmv")
                for s in range(2):
                    op("scalar", lambda E, s=s, pp=pp: E.activation(
                        lnrs[pp][s][:], lnr1[pp][s][:], AF.Sqrt),
                       sem=("tms2" if s == 1 else None), drain=(s == 1))
            if True:
                pr = it
                pp = pr % 2
                wait("vector", "tms2", pr + 1)

                for s in range(2):
                    tc = 2 * pr + s
                    if tc >= 1:
                        wait("vector", "ttr", TTRXN[tc - 1])
                    for h in range(2):
                        op("vector", lambda E, s=s, pp=pp, h=h: E.tensor_scalar(
                            xnb[:, h * 512:(h + 1) * 512], XB[s][:, h * 512:(h + 1) * 512],
                            lnm[pp][s][:], lnrs[pp][s][:], OP.subtract, OP.mult),
                           sem=("xnv" if h == 1 else None), k=1, drain=(h == 1))
                    vqs = qp_head(TL_TM[s], P, D, lambda: xnb[:, 0:D])
                    if tc == 0:
                        wait("tensor", "mld", 48)
                    wait("tensor", "xnv", tc + 1)
                    if tc >= 1:
                        wait("tensor", "tev", TEVXN[tc - 1])
                    for b in range(8):
                        op("tensor", lambda E, b=b: E.transpose(
                            PS[2][:, b * P:(b + 1) * P], xnb[:, b * P:(b + 1) * P],
                            idf[:]), sem=("ttr" if b == 7 else None))
                    TTRXN[tc] = cur("ttr")
                    wait("scalar", "ttr", TTRXN[tc])
                    wait("scalar", f"xnst{s}", 32 * pr)
                    op("scalar", lambda E, s=s: E.activation(
                        xn16t[s][:, 0:D], PS[2][:, 0:D], AF.Copy), sem="tev", drain=True)
                    TEVXN[tc] = cur("tev")
                    wait("sync", "tev", TEVXN[tc])
                    op("sync", lambda E, tc=tc, s=s: E.dma_start(
                        out=xn16_d[:, tc * P:(tc + 1) * P].rearrange("(b d) t -> d b t", b=8),
                        in_=xn16t[s][:].rearrange("p (b t) -> p b t", b=8)), sem=f"xnst{s}", k=16)
                    if s == 1 and pr + 1 < NPR:
                        wait("sync", "xnv", cur("xnv"))
                        emit_xloads(pr + 1)
                    if QPEND:
                        pend_pop()
                    QPEND.append(("tm", tc, s, vqs))

        wsubs = []
        for j in range(3):
            rr = min(P, WGU - j * P)
            wsubs.append((wg_sl[j * P:j * P + rr, :], rr, 1024,
                          wgq_sl_d.ap()[j * P:j * P + rr, :]))
        for j in range(3):
            rr = min(P, WGU - j * P)
            wsubs.append((wu_sl[j * P:j * P + rr, :], rr, 1024,
                          wuq_sl_d.ap()[j * P:j * P + rr, :]))
        for j in range(NQ):
            wsubs.append((wd_sl[:, j * IQ:(j + 1) * IQ], P, IQ,
                          wdq_sl_d.ap()[:, j * IQ:(j + 1) * IQ]))
        wvqs_prev = None
        for k, (wsrc, rr, ww, dst) in enumerate(wsubs):
            if wvqs_prev is not None:
                wait("sync", "qvs", wvqs_prev)
            op("sync", lambda E, wsrc=wsrc, rr=rr, ww=ww: E.dma_start(
                out=wbuf[0:rr, 0:ww], in_=wsrc), sem="wld", k=16)
            wait("vector", "wld", 16 * (k + 1))
            wvqs_prev = qp_head(TL_W[k % 2], rr, ww, lambda rr=rr, ww=ww: wbuf[0:rr, 0:ww])
            if QPEND:
                pend_pop()
            QPEND.append(("w", k, rr, ww, dst, wvqs_prev))
        while QPEND:
            pend_pop()
        emit_mld2()
        wait("gpsimd", "wqst")
        op("gpsimd", lambda E: E.collective_compute(
            "AllGather", OP.bypass, replica_groups=[list(range(NCORE))],
            ins=[wgq_sl_d.ap().opt()], outs=[wgq_d.ap().opt()]), sem="ccw")
        wait("gpsimd", "ccw", 1)
        op("gpsimd", lambda E: E.collective_compute(
            "AllGather", OP.bypass, replica_groups=[list(range(NCORE))],
            ins=[wuq_sl_d.ap().opt()], outs=[wuq_d.ap().opt()]), sem="ccw")
        wait("gpsimd", "ccw", 2)
        op("gpsimd", lambda E: E.collective_compute(
            "AllGather", OP.bypass, replica_groups=[list(range(NCORE))],
            ins=[wdq_sl_d.ap().opt()], outs=[wdq_d.ap().opt()]), sem="ccw")

        if dbg:
            wait("gpsimd", "ccw", 2)
            op("gpsimd", lambda E: E.dma_start(out=dbgwg_d, in_=wgq_d.ap()), sem="ccv", k=16)
            op("gpsimd", lambda E: E.dma_start(out=dbgwu_d, in_=wuq_d.ap()), sem="ccv", k=16)

        ybcnt = [0, 0, 0, 0]

        def xnsl(sl, g):
            if sl % 2 == 0:
                return xnslM[:, g * ASL:(g + 1) * ASL]
            if g < 4:
                return xn16t[0][:, g * ASL:(g + 1) * ASL]
            return xn16t[1][:, (g - 4) * ASL:(g - 3) * ASL]

        def emit_xcl(sl):
            if sl >= NASL:
                return
            wait("sync", "xnst0")
            wait("sync", "xnst1")
            if sl > 1:
                wait("sync", "atv", 8 * (sl - 1))
            if sl % 2 == 0:
                op("sync", lambda E, sl=sl: E.dma_start(
                    out=xnslM[:].rearrange("p (g t) -> p g t", g=NG),
                    in_=xn16_d[:, sl * ASL:(sl + 1) * ASL].rearrange("(g p) t -> p g t", g=NG)),
                   sem="xcl0", k=16)
            else:
                for hh in range(2):
                    op("sync", lambda E, sl=sl, hh=hh: E.dma_start(
                        out=xn16t[hh][:].rearrange("p (g t) -> p g t", g=4),
                        in_=xn16_d[hh * 512:(hh + 1) * 512,
                                   sl * ASL:(sl + 1) * ASL].rearrange("(g p) t -> p g t", g=4)),
                       sem="xcl1", k=16)
        emit_xcl(0)
        emit_xcl(1)

        def emit_wqtl():
            wait("sync", "ccw", 1)
            for d in range(NG):
                op("sync", lambda E, d=d: E.dma_start_transpose(
                    AR_W[:, d * DI:(d + 1) * DI], wgq_d.ap()[:, d * P:(d + 1) * P]), sem="wqtl", k=16)
            wait("sync", "ccw", 2)
            for d in range(NG):
                op("sync", lambda E, d=d: E.dma_start_transpose(
                    AR_W[:, (NG + d) * DI:(NG + d + 1) * DI], wuq_d.ap()[:, d * P:(d + 1) * P]), sem="wqtl", k=16)

        for sl in range(NASL):
            if sl == 4:
                emit_wqtl()
            if sl == 0:
                wait("tensor", "tev", 32)
            wait("tensor", "mld", 48)
            wait("tensor", "mld2", 32)
            wait("tensor", f"xcl{sl % 2}", (16 if sl % 2 == 0 else 32) * (sl // 2 + 1))
            if sl > 0:
                wait("tensor", "ats", 8 * sl)
            aps = lambda g: PS[g // 2][:, (g % 2) * 512:(g % 2) * 512 + ASL]
            xps = lambda g: PS[g // 2][:, (g % 2) * 512 + ASL:(g % 2) * 512 + 2 * ASL]
            for g in range(NG):
                op("tensor", lambda E, sl=sl, g=g: E.matmul(
                    aps(g), wab[:, g * P:(g + 1) * P],
                    xnsl(sl, g), start=True, stop=True))
                op("tensor", lambda E, sl=sl, g=g: E.matmul(
                    xps(g), winb[:, g * P:(g + 1) * P],
                    xnsl(sl, g), start=True, stop=True),
                   sem=("att" if g == NG - 1 else None))
            wait("scalar", "att", sl + 1)
            if sl > 0:
                wait("scalar", "atf", sl)
            for g in range(NG):
                op("scalar", lambda E, g=g: E.activation(bb1[g % 2][:], aps(g), AF.Sigmoid))
                op("scalar", lambda E, g=g: E.activation(
                    gxM[:, g * ASL:(g + 1) * ASL], xps(g), AF.Sigmoid), drain=True)
                op("scalar", lambda E, g=g: E.activation(
                    logaM[:, g * ASL:(g + 1) * ASL], bb1[g % 2][:], AF.Copy,
                    scale=spneg[:, g:g + 1]), sem="ats", drain=(g == NG - 1))
            for g in range(NG):
                op("scalar", lambda E, g=g: E.activation(
                    multM[:, g * ASL:(g + 1) * ASL], logaM[:, g * ASL:(g + 1) * ASL], AF.Exp),
                   sem=("atsB" if g == NG - 1 else None), drain=(g == NG - 1))
            wait("vector", "atsB")
            op("vector", lambda E: E.tensor_scalar(aM[:], multM[:], 1.0, None, OP.min),
               drain=True)
            op("vector", lambda E: E.tensor_mul(logaM[:], aM[:], aM[:]),
               sem="atvB", drain=True)
            wait("scalar", "atvB")
            for g in range(NG):
                op("scalar", lambda E, g=g: E.activation(
                    multM[:, g * ASL:(g + 1) * ASL], logaM[:, g * ASL:(g + 1) * ASL],
                    AF.Sqrt, bias=af32(1.0), scale=af32(-1.0)),
                   sem=("atsC" if g == NG - 1 else None), drain=(g == NG - 1))
            wait("vector", "atsC")
            if sl == 0:
                wait("vector", "ttr", TTRXN[NTC - 1])
            for hh in range(2):
                op("vector", lambda E, hh=hh: E.tensor_mul(
                    [lnscr, xnb][hh][:], multM[:, hh * 1024:(hh + 1) * 1024],
                    gxM[:, hh * 1024:(hh + 1) * 1024], ), sem=("atf" if hh == 1 else None),
                   drain=(hh == 1))
            for gp in range(NG // 2):
                gg = [2 * gp, 2 * gp + 1]
                bidx = [2 * (gp % 2), 2 * (gp % 2) + 1]
                for s in range(2):
                    b = bidx[s]
                    if ybcnt[b] > 0:
                        wait("vector", f"yb{b}", 32 * ybcnt[b])
                for s in range(2):
                    op("vector", lambda E, sl=sl, g=gg[s], s=s: E.tensor_tensor(
                        bb2[s][:], [lnscr, xnb][g // 4][:, (g % 4) * ASL:(g % 4 + 1) * ASL],
                        xnsl(sl, g), OP.mult), drain=(s == 1))
                for s in range(2):
                    if sl == 0:
                        op("vector", lambda E, g=gg[s], s=s, b=bidx[s]: E.tensor_tensor_scan(
                            yslb[b][:], aM[:, g * ASL:(g + 1) * ASL], bb2[s][:], 0.0, OP.mult, OP.add))
                    else:
                        op("vector", lambda E, g=gg[s], s=s, b=bidx[s]: E.tensor_tensor_scan(
                            yslb[b][:], aM[:, g * ASL:(g + 1) * ASL], bb2[s][:],
                            ycol[:, g:g + 1], OP.mult, OP.add))
                for s in range(2):
                    if sl == 0:
                        op("vector", lambda E, g=gg[s], b=bidx[s]: E.tensor_tensor_scan(
                            pslb[b][:], aM[:, g * ASL:(g + 1) * ASL], zasl[:], 1.0, OP.mult, OP.add),
                           drain=(s == 1))
                    else:
                        op("vector", lambda E, g=gg[s], b=bidx[s]: E.tensor_tensor_scan(
                            pslb[b][:], aM[:, g * ASL:(g + 1) * ASL], zasl[:],
                            pcol[:, g:g + 1], OP.mult, OP.add), drain=(s == 1))
                for s in range(2):
                    op("vector", lambda E, g=gg[s], b=bidx[s]: E.tensor_copy(
                        ycol[:, g:g + 1], yslb[b][:, ASL - 1:ASL]))
                for s in range(2):
                    op("vector", lambda E, g=gg[s], b=bidx[s]: E.tensor_copy(
                        pcol[:, g:g + 1], pslb[b][:, ASL - 1:ASL]),
                       sem=("atv" if s == 1 else None), k=2, drain=(s == 1))
                wait("sync", "atv", cur("atv"))
                for s in range(2):
                    b = bidx[s]
                    op("sync", lambda E, sl=sl, g=gg[s], b=b: E.dma_start(
                        out=y_d[g, :, sl * ASL:(sl + 1) * ASL], in_=yslb[b][:]), sem=f"yb{b}", k=16)
                    op("sync", lambda E, sl=sl, g=gg[s], b=b: E.dma_start(
                        out=p_d[g, :, sl * ASL:(sl + 1) * ASL], in_=pslb[b][:]), sem=f"yb{b}", k=16)
                    ybcnt[b] += 1
            if gp == NG // 2 - 1:
                emit_xcl(sl + 2)

        op("vector", lambda E: E.tensor_copy(ccs[:, 0:NG], pcol[:]), fence=True)
        op("vector", lambda E: E.tensor_copy(ccs[:, NG:16], ycol[:]), sem="ccv", drain=True)
        wait("gpsimd", "ccv")
        op("gpsimd", lambda E: E.dma_start(out=cc_in[:, :], in_=ccs[:]), sem="ccc", k=16)
        wait("gpsimd", "ccc", 16)
        op("gpsimd", lambda E: E.collective_compute(
            "AllGather", OP.bypass, replica_groups=[list(range(NCORE))],
            ins=[cc_in.ap().opt()], outs=[cc_out.ap().opt()]), sem="ccl")
        wait("gpsimd", "ccl", 1)
        op("gpsimd", lambda E: E.dma_start(
            out=carr[:].rearrange("p (r c) -> p r c", r=NCORE),
            in_=cc_out.ap().rearrange("(r p) c -> p r c", p=P)), sem="ccc2", k=16)
        wait("vector", "ccc2", 16)
        op("vector", lambda E: E.memset(cc0[:], 0.0), fence=True)
        op("vector", lambda E: E.memset(cs0[:], 0.0), fence=True)
        ccur, cnxt = cc0, cc1
        scur, snxt = cs0, cs1
        for r in range(NCORE):
            op("vector", lambda E, r=r: E.tensor_copy(selq[:], selc[:, r:r + 1]), fence=True)
            op("vector", lambda E, r=r, ccur=ccur, scur=scur, snxt=snxt: E.scalar_tensor_tensor(
                snxt[:], ccur[:], selq[:], scur[:], OP.mult, OP.add), fence=True)
            scur, snxt = snxt, scur
            if r < NCORE - 1:
                if r + 1 == 4:
                    op("vector", lambda E, cnxt=cnxt: E.memset(cnxt[:], 0.0), fence=True)
                else:
                    op("vector", lambda E, r=r, ccur=ccur: E.tensor_tensor(
                        cct[:], carr[:, r * 16:r * 16 + NG], ccur[:], OP.mult), fence=True)
                    op("vector", lambda E, r=r, cnxt=cnxt: E.tensor_tensor(
                        cnxt[:], cct[:], carr[:, r * 16 + NG:r * 16 + 16], OP.add), fence=True)
                ccur, cnxt = cnxt, ccur
        cmine = scur
        if dbg:
            op("vector", lambda E, cmine=cmine: E.tensor_copy(spt2[:], cmine[:]), fence=True)
            wait("gpsimd", "vfn")
            op("gpsimd", lambda E: E.dma_start(out=dbgcc_d[:, 0:128], in_=carr[:]), sem="ccv", k=16)
            op("gpsimd", lambda E: E.dma_start(out=dbgcc_d[:, 128:136], in_=spt2[:]), sem="ccv", k=16)
            op("gpsimd", lambda E: E.dma_start(out=dbgcc_d[:, 152:160], in_=selc[:]), sem="ccv", k=16)
            op("gpsimd", lambda E: E.dma_start(out=dbgcc_d[:, 136:144], in_=pcol[:]), sem="ccv", k=16)
            op("gpsimd", lambda E: E.dma_start(out=dbgcc_d[:, 144:152], in_=ycol[:]), sem="ccv", k=16)
        op("vector", lambda E, cmine=cmine: E.tensor_scalar(
            cct[:], cmine[:], f32(1.0), None, OP.mult), sem="crv", drain=True)

        IH = IQ // 2
        assert not QPEND

        def gu_tail(ent):
            tc2, q2, it2, vqs2 = ent
            par2 = it2 % 2
            sqs2 = qp_schain_gu(TL_GU[par2], P, IQ, vqs2)
            if it2 >= 2:
                wait("vector", f"hst{par2}", 16 * (it2 // 2))
            qp_tail(TL_GU[par2], P, IQ, lambda par2=par2: hqo[par2][0:P, 0:IQ],
                    "hqv", 1, sqs2)
            wait("gpsimd", "hqv")
            op("gpsimd", lambda E, tc2=tc2, q2=q2, par2=par2: E.dma_start(
                out=hq_d[tc2 * P:(tc2 + 1) * P, q2 * IQ:(q2 + 1) * IQ],
                in_=hqo[par2][0:P, 0:IQ]), sem=f"hst{par2}", k=16)
        if dbg:
            wait("gpsimd", "wqtl", 256)
            op("gpsimd", lambda E: E.dma_start(out=dbgar1_d, in_=AR_W[:, :]), sem="ccv", k=16)
        wait("tensor", "wqtl")
        wait("tensor", "ats")
        for tc in range(NTC):
            wait("sync", "xnst0")
            wait("sync", "xnst1")
            if tc == 0:
                wait("sync", "wqst", cur("wqst"))
            if tc >= 2:
                wait("sync", "gut", 4 * (tc - 1))
            op("sync", lambda E, tc=tc: E.dma_start(
                out=xqblk[tc % 2][:].rearrange("p (b t) -> p b t", b=NG),
                in_=xq16_d[:, tc * P:(tc + 1) * P].rearrange("(b d) t -> d b t", b=NG)),
               sem=f"xqcl{tc % 2}", k=16)
            for q in range(NQ):
                it = tc * NQ + q
                psG = PS[0 if it % 2 == 0 else 2]
                psU = PS[1 if it % 2 == 0 else 3]
                if it >= 2:
                    wait("tensor", "gus", it - 1)
                    wait("tensor", "guh", it - 1)
                if q == 0:
                    wait("tensor", f"xqcl{tc % 2}", 16 * (tc // 2 + 1))
                for d in range(NG):
                    lhs = lambda d=d, tc=tc: xqblk[tc % 2][:, d * P:(d + 1) * P]
                    rg = lambda d=d, q=q: AR_W[:, d * DI + q * IQ: d * DI + (q + 1) * IQ]
                    ru = lambda d=d, q=q: AR_W[:, (NG + d) * DI + q * IQ: (NG + d) * DI + (q + 1) * IQ]
                    st0, st1 = (d == 0), (d == NG - 1)
                    op("tensor", lambda E, lhs=lhs, rg=rg, psG=psG, st0=st0, st1=st1: E.matmul(
                        psG[:, 0:512], lhs(), rg()[:, 0:512], start=st0, stop=st1))
                    op("tensor", lambda E, lhs=lhs, rg=rg, psG=psG, st0=st0, st1=st1: E.matmul(
                        psG[:, 512:IQ], lhs(), rg()[:, 512:IQ], start=st0, stop=st1))
                    op("tensor", lambda E, lhs=lhs, ru=ru, psU=psU, st0=st0, st1=st1: E.matmul(
                        psU[:, 0:512], lhs(), ru()[:, 0:512], start=st0, stop=st1))
                    op("tensor", lambda E, lhs=lhs, ru=ru, psU=psU, st0=st0, st1=st1: E.matmul(
                        psU[:, 512:IQ], lhs(), ru()[:, 512:IQ], start=st0, stop=st1),
                       sem=("gut" if st1 else None))
                wait("scalar", "gut")
                if it >= 1:
                    wait("scalar", "guh", it)
                for s in range(2):
                    op("scalar", lambda E, psG=psG, s=s: E.activation(
                        hsil[:, s * IH:(s + 1) * IH], psG[:, s * IH:(s + 1) * IH], AF.Silu),
                       sem=("gus" if s == 1 else None), drain=(s == 1))
                wait("vector", "gus")
                for s in range(2):
                    op("vector", lambda E, psU=psU, s=s: E.tensor_tensor(
                        hful[:, s * IH:(s + 1) * IH], hsil[:, s * IH:(s + 1) * IH],
                        psU[:, s * IH:(s + 1) * IH], OP.mult),
                       sem=("guh" if s == 1 else None), drain=(s == 1))
                if dbg:
                    wait("gpsimd", "guh", it + 1)
                    op("gpsimd", lambda E, it=it: E.dma_start(
                        out=dbgh_d[it, :, :], in_=hful[:, 0:IQ]), sem="ccv", k=16)
                par = it % 2
                vqs = qp_head(TL_GU[par], P, IQ, lambda: hful[:, 0:IQ])
                if QPEND:
                    gu_tail(QPEND.pop(0))
                QPEND.append((tc, q, it, vqs))

        if dbg:
            wait("gpsimd", "gut", 64)
            op("gpsimd", lambda E: E.dma_start(out=dbgar2_d, in_=AR_W[:, :]), sem="ccv", k=16)

        while QPEND:
            gu_tail(QPEND.pop(0))

        wait("sync", "gut")
        wait("sync", "ccw", 3)
        for i in range(NID):
            op("sync", lambda E, i=i: E.dma_start_transpose(
                AR_W[:, i * 1024:(i + 1) * 1024], wdq_d.ap()[:, i * P:(i + 1) * P]), sem="wdl", k=16)
        HQOFF = NID * 1024
        wait("sync", "hst0")
        wait("sync", "hst1")
        for b in range(4):
            wait("sync", f"yb{b}")

        def hqsl(i, par):
            off = HQOFF + par * NID * 512 + i * 512
            return AR_W[:, off:off + 512]

        wait("vector", "crv")
        MT = [logaM, aM, gxM, multM]

        def emit_loads(j):
            tt, k = divmod(j, 4)
            tpar = tt % 2
            hc = [2 * k, 2 * k + 1]
            if k == 0:
                if tt >= 2:
                    wait("sync", "dnt", 8 * tt - 8)
                for i in range(NID):
                    op("sync", lambda E, tt=tt, i=i, tpar=tpar: E.dma_start_transpose(
                        hqsl(i, tpar), hq_d[tt * 512:(tt + 1) * 512, i * P:(i + 1) * P]), sem=f"dcl{tpar}", k=16)
            wait("sync", "dnv", 2 * j)
            for s in range(2):
                op("sync", lambda E, tt=tt, hc=hc[s], s=s: E.dma_start(
                    out=xb0[:, s * 512:(s + 1) * 512], in_=y_d[hc, :, tt * 512:(tt + 1) * 512]), sem="rld", k=16)
                op("sync", lambda E, tt=tt, hc=hc[s], s=s: E.dma_start(
                    out=xb1[:, s * 512:(s + 1) * 512], in_=p_d[hc, :, tt * 512:(tt + 1) * 512]), sem="rld", k=16)
                op("sync", lambda E, tt=tt, hc=hc[s], s=s: E.dma_start(
                    out=xnb[:, s * 512:(s + 1) * 512], in_=xT[hc * P:(hc + 1) * P, tt * 512:(tt + 1) * 512]), sem="rld", k=16)

        def emit_mm(j):
            tt = j // 4
            tpar = tt % 2
            PD = PS[j % 2]
            if j == 0:
                wait("tensor", "wdl", 16 * NID)
            if j % 4 == 0:
                wait("tensor", f"dcl{tpar}", 16 * NID * (tt // 2 + 1))
            if j >= 2:
                wait("tensor", "dnv", 2 * (j - 1))
            for s in range(2):
                for i in range(NID):
                    st0, st1 = (i == 0), (i == NID - 1)
                    op("tensor", lambda E, i=i, j=j, s=s, tpar=tpar, PD=PD, st0=st0, st1=st1: E.matmul(
                        PD[:, s * 512:(s + 1) * 512],
                        AR_W[:, i * 1024 + (2 * (j % 4) + s) * P: i * 1024 + (2 * (j % 4) + s + 1) * P],
                        hqsl(i, tpar), start=st0, stop=st1), sem=("dnt" if st1 else None))

        def emit_res(j):
            PD = PS[j % 2]
            hc = [2 * (j % 4), 2 * (j % 4) + 1]
            wait("vector", "dnt", 2 * (j + 1))
            wait("vector", "rld", 16 * 6 * (j + 1))
            if j >= 1:
                wait("vector", "trt", 8 * j)
            for s in range(2):
                op("vector", lambda E, s=s, hc=hc[s], cmine=cmine: E.tensor_copy(
                    cmq[s][:], cmine[:, hc:hc + 1]), drain=(s == 1))
            for s in range(2):
                op("vector", lambda E, s=s: E.scalar_tensor_tensor(
                    res1[s](), xb1[:, s * 512:(s + 1) * 512], cmq[s][:],
                    xb0[:, s * 512:(s + 1) * 512], OP.mult, OP.add), drain=(s == 1))
            for s in range(2):
                op("vector", lambda E, s=s, PD=PD: E.scalar_tensor_tensor(
                    res2[s](), PD[:, s * 512:(s + 1) * 512], f32(1.0), res1[s](), OP.mult, OP.add), drain=(s == 1))
            for s in range(2):
                op("vector", lambda E, s=s: E.tensor_tensor(
                    otc[s](), res2[s](), xnb[:, s * 512:(s + 1) * 512], OP.add),
                   sem=("dnv" if s == 1 else None), k=2, drain=(s == 1))

        def emit_tr(j):
            wait("tensor", "dnv", 2 * (j + 1))
            if j >= 1:
                wait("tensor", "otv", 8 * j)
            for s in range(2):
                for blk in range(4):
                    op("tensor", lambda E, s=s, blk=blk: E.transpose(
                        PS[2][:, s * 512 + blk * P:s * 512 + (blk + 1) * P],
                        lnscr[:, s * 512 + blk * P:s * 512 + (blk + 1) * P], idf[:]),
                       sem=("trt" if (s == 1 and blk == 3) else None), k=8)

        def emit_otv(j):
            tt = j // 4
            hc = [2 * (j % 4), 2 * (j % 4) + 1]
            wait("vector", "trt", 8 * (j + 1))
            if j % 4 == 0 and tt >= 1:
                wait("vector", "ost", 64 * tt)
            for blk in range(4):
                for s in range(2):
                    op("vector", lambda E, blk=blk, s=s, hc=hc[s]: E.tensor_copy(
                        MT[blk][:, hc * P:(hc + 1) * P], PS[2][:, s * 512 + blk * P:s * 512 + (blk + 1) * P]),
                       sem=("otv" if (blk == 3 and s == 1) else None), k=8,
                       drain=(blk == 3 and s == 1))
            if j % 4 == 3:
                wait("gpsimd", "otv", cur("otv"))
                for blk in range(4):
                    op("gpsimd", lambda E, tt=tt, blk=blk: E.dma_start(
                        out=out[tt * 512 + blk * P: tt * 512 + (blk + 1) * P, :],
                        in_=MT[blk][:, 0:1024]), sem="ost", k=16)

        NJP = 16
        for j in range(NJP):
            emit_loads(j)
        for j in range(NJP):
            emit_mm(j)
            if j >= 1:
                emit_tr(j - 1)
            emit_res(j)
            if j >= 1:
                emit_otv(j - 1)
        emit_tr(NJP - 1)
        emit_otv(NJP - 1)

        wait("sync", "ost")
        wait("gpsimd", "ost")

        for (weng, wsem, wv, pos) in WAITLOG:
            assert wv <= cnt[wsem], (
                f"wait({weng}, {wsem}, {wv}) exceeds final total {cnt[wsem]} (hang)")

        block = ctx.enter_context(nc.Block())

        @block.sync
        def _(E):
            for fn in sched["sync"]:
                fn(E)

        @block.vector
        def _(E):
            for fn in sched["vector"]:
                fn(E)

        @block.scalar
        def _(E):
            for fn in sched["scalar"]:
                fn(E)

        @block.tensor
        def _(E):
            for fn in sched["tensor"]:
                fn(E)

        @block.gpsimd
        def _(E):
            for fn in sched["gpsimd"]:
                fn(E)

    return nc


_NC_CACHE = [None]


def kernel(L0_attn_x, L0_attn_a_param, L0_attn_w_in, L0_attn_w_a,
           L0_ffn_w_gate, L0_ffn_w_up, L0_ffn_w_down):
    x = np.asarray(L0_attn_x, dtype=np.float32)
    a_param = np.asarray(L0_attn_a_param, dtype=np.float32)
    w_in = np.ascontiguousarray(np.asarray(L0_attn_w_in, dtype=np.float32))
    w_a = np.ascontiguousarray(np.asarray(L0_attn_w_a, dtype=np.float32))
    wg = np.asarray(L0_ffn_w_gate, dtype=np.float32)
    wu = np.asarray(L0_ffn_w_up, dtype=np.float32)
    wd = np.asarray(L0_ffn_w_down, dtype=np.float32)

    if _NC_CACHE[0] is None:
        _NC_CACHE[0] = build_nc()
    nc = _NC_CACHE[0]

    apg = np.ascontiguousarray(a_param.reshape(NG, P).T)
    identf = np.eye(P, dtype=np.float32)
    in_maps = []
    for c in range(NCORE):
        s, k = divmod(c, 4)
        xs = np.ascontiguousarray(x[s, k * TOK:(k + 1) * TOK, :])
        sel = np.zeros((P, NG), np.float32)
        sel[:, c] = 1.0
        in_maps.append({
            "x_tm": xs,
            "xT": np.ascontiguousarray(xs.T),
            "wg_sl": np.ascontiguousarray(wg[c * WGU:(c + 1) * WGU, :]),
            "wu_sl": np.ascontiguousarray(wu[c * WGU:(c + 1) * WGU, :]),
            "wd_sl": np.ascontiguousarray(wd[c * P:(c + 1) * P, :]),
            "w_in": w_in,
            "w_a": w_a,
            "apg": apg,
            "selcol": sel,
            "identf": identf,
        })
    res = run_bass_kernel_spmd(nc, in_maps, core_ids=list(range(NCORE)))
    outp = np.empty((2, 8192, D), np.float32)
    for c in range(NCORE):
        s, k = divmod(c, 4)
        outp[s, k * TOK:(k + 1) * TOK, :] = res.results[c]["out"]
    return outp


# version 33
# speedup vs baseline: 1.0395x; 1.0323x over previous
import numpy as np
from contextlib import ExitStack
import concourse.bass as bass
import concourse.mybir as mybir
from concourse.bass_utils import run_bass_kernel_spmd

F32 = mybir.dt.float32
BF16 = mybir.dt.bfloat16
I32 = mybir.dt.int32
AF = mybir.ActivationFunctionType
OP = mybir.AluOpType
AX = mybir.AxisListType

NCORE = 8
P = 128
TOK = 2048
D = 1024
NG = 8
DI = 2816
IQ = 704
NQ = 4
ASL = 256
NASL = TOK // ASL
NTC = TOK // P
NPR = NTC // 2
NID = DI // P
WGU = 352
C6TH = float(np.float32(1.0 / 6.0))
f32 = lambda v: float(np.float32(v))
af32 = np.float32


def build_nc(dbg=False):
    dk = (lambda: {"kind": "ExternalOutput"}) if dbg else (lambda: {})
    nc = bass.Bass("TRN2", target_bir_lowering=False, debug=False)

    x_tm = nc.dram_tensor("x_tm", [TOK, D], F32, kind="ExternalInput").ap()
    xT = nc.dram_tensor("xT", [D, TOK], F32, kind="ExternalInput").ap()
    wg_sl = nc.dram_tensor("wg_sl", [WGU, D], F32, kind="ExternalInput").ap()
    wu_sl = nc.dram_tensor("wu_sl", [WGU, D], F32, kind="ExternalInput").ap()
    wd_sl = nc.dram_tensor("wd_sl", [P, DI], F32, kind="ExternalInput").ap()
    w_in = nc.dram_tensor("w_in", [NG, P, P], F32, kind="ExternalInput").ap()
    w_a = nc.dram_tensor("w_a", [NG, P, P], F32, kind="ExternalInput").ap()
    apg = nc.dram_tensor("apg", [P, NG], F32, kind="ExternalInput").ap()
    selcol = nc.dram_tensor("selcol", [P, NG], F32, kind="ExternalInput").ap()
    identf = nc.dram_tensor("identf", [P, P], F32, kind="ExternalInput").ap()
    out = nc.dram_tensor("out", [TOK, D], F32, kind="ExternalOutput").ap()

    xn16_d = nc.dram_tensor("xn16_d", [D, TOK], BF16, **dk()).ap()
    xq16_d = nc.dram_tensor("xq16_d", [D, TOK], BF16, **dk()).ap()
    wgq_sl_d = nc.dram_tensor("wgq_sl_d", [WGU, D], BF16)
    wuq_sl_d = nc.dram_tensor("wuq_sl_d", [WGU, D], BF16)
    wdq_sl_d = nc.dram_tensor("wdq_sl_d", [P, DI], BF16)
    wgq_d = nc.dram_tensor("wgq_d", [DI, D], BF16)
    wuq_d = nc.dram_tensor("wuq_d", [DI, D], BF16)
    wdq_d = nc.dram_tensor("wdq_d", [D, DI], BF16)
    y_d = nc.dram_tensor("y_d", [NG, P, TOK], F32, **dk()).ap()
    p_d = nc.dram_tensor("p_d", [NG, P, TOK], F32, **dk()).ap()
    hq_d = nc.dram_tensor("hq_d", [TOK, DI], BF16, **dk()).ap()
    dbgcc_d = nc.dram_tensor("dbgcc_d", [P, 160], F32, **dk()).ap() if True else None
    dbgwg_d = nc.dram_tensor("dbgwg_d", [DI, D], BF16, **dk()).ap() if dbg else None
    dbgwu_d = nc.dram_tensor("dbgwu_d", [DI, D], BF16, **dk()).ap() if dbg else None
    dbgh_d = nc.dram_tensor("dbgh_d", [NTC * NQ, P, IQ], F32, **dk()).ap() if dbg else None
    dbgar1_d = nc.dram_tensor("dbgar1_d", [P, 16 * DI], BF16, **dk()).ap() if dbg else None
    dbgar2_d = nc.dram_tensor("dbgar2_d", [P, 16 * DI], BF16, **dk()).ap() if dbg else None
    cc_in = nc.dram_tensor("cc_in", [P, 16], F32)
    cc_out = nc.dram_tensor("cc_out", [NCORE * P, 16], F32)

    ctx = ExitStack()
    with ctx:
        def sb(name, shape, dt):
            return ctx.enter_context(nc.sbuf_tensor(name, shape, dt))

        AR_W = sb("AR_W", [P, 16 * DI], BF16)
        xqblk = [sb("xqblk0", [P, NG * P], BF16), sb("xqblk1", [P, NG * P], BF16)]

        qv = sb("qv", [P, 1024], F32)
        qi = sb("qi", [P, 1024], I32)
        qi2 = sb("qi2", [P, IQ], I32)
        qc1 = sb("qc1", [P, 1024], BF16)
        qc2 = sb("qc2", [P, 1024], BF16)
        qc3 = sb("qc3", [P, 1024], BF16)
        qa1 = sb("qa1", [P, 1024], BF16)
        qsg = sb("qsg", [P, 1024], BF16)
        qrse = sb("qrse", [P, 1024], F32)
        qav = qrse
        qsse = sb("qsse", [P, 1024], BF16)
        qbm = sb("qbm", [P, 192], F32)
        qsc = sb("qsc", [P, 192], F32)
        qrs = sb("qrs", [P, 192], F32)
        qsb = sb("qsb", [P, 192], BF16)
        qsb2 = sb("qsb2", [P, 192], BF16)

        xb0 = sb("xb0", [P, 1024], F32)
        xb1 = sb("xb1", [P, 1024], F32)
        xnb = sb("xnb", [P, 1024], F32)
        xn16t = [sb("xn16t0", [P, 1024], BF16), sb("xn16t1", [P, 1024], BF16)]
        xqo = [sb("xqo0", [P, 1024], BF16), sb("xqo1", [P, 1024], BF16)]
        lnscr = sb("lnscr", [P, 1024], F32)
        lnsx = [[sb(f"lnsx{pp}{s}", [P, 1], F32) for s in range(2)] for pp in range(2)]
        lnsq = [[sb(f"lnsq{pp}{s}", [P, 1], F32) for s in range(2)] for pp in range(2)]
        lnm = [[sb(f"lnm{pp}{s}", [P, 1], F32) for s in range(2)] for pp in range(2)]
        lnr1 = [[sb(f"lnr1{pp}{s}", [P, 1], F32) for s in range(2)] for pp in range(2)]
        lnrs = [[sb(f"lnrs{pp}{s}", [P, 1], F32) for s in range(2)] for pp in range(2)]
        lnm2 = [sb(f"lnm2{s}", [P, 1], F32) for s in range(2)]
        lnv0 = [sb(f"lnv0{s}", [P, 1], F32) for s in range(2)]
        lnv1 = [sb(f"lnv1{s}", [P, 1], F32) for s in range(2)]

        wbuf = sb("wbuf", [P, 1024], F32)
        wqo = xqblk[1]

        logaM = sb("logaM", [P, NG * ASL], F32)
        aM = sb("aM", [P, NG * ASL], F32)
        gxM = sb("gxM", [P, NG * ASL], F32)
        multM = sb("multM", [P, NG * ASL], F32)
        xnslM = sb("xnslM", [P, NG * ASL], BF16)
        bb1 = [sb("bb1a", [P, ASL], F32), sb("bb1b", [P, ASL], F32)]
        bb2 = [sb("bb2a", [P, ASL], F32), sb("bb2b", [P, ASL], F32)]
        yslb = [sb(f"ysl{i}", [P, ASL], F32) for i in range(4)]
        pslb = [sb(f"psl{i}", [P, ASL], F32) for i in range(4)]
        zasl = sb("zasl", [P, ASL], F32)
        ycol = sb("ycol", [P, NG], F32)
        pcol = sb("pcol", [P, NG], F32)
        winb = sb("winb", [P, NG * P], BF16)
        wab = sb("wab", [P, NG * P], BF16)
        spu = sb("spu", [P, NG], F32)
        spt1 = sb("spt1", [P, NG], F32)
        spt2 = sb("spt2", [P, NG], F32)
        spneg = sb("spneg", [P, NG], F32)
        selc = sb("selc", [P, NG], F32)
        idf = sb("idf", [P, P], F32)
        ccs = sb("ccs", [P, 16], F32)
        carr = sb("carr", [P, NCORE * 16], F32)
        cc0 = sb("cc0", [P, NG], F32)
        cc1 = sb("cc1", [P, NG], F32)
        cct = sb("cct", [P, NG], F32)
        cs0 = sb("cs0", [P, NG], F32)
        cs1 = sb("cs1", [P, NG], F32)
        selq = sb("selq", [P, 1], F32)
        cmq = [sb("cmqa", [P, 1], F32), sb("cmqb", [P, 1], F32)]

        hsil = xnb
        hful = lnscr
        hqo = xqo

        otc = [lambda: lnscr[:, 0:512], lambda: lnscr[:, 512:1024]]
        res1 = [lambda: qrse[:, 0:512], lambda: qrse[:, 512:1024]]
        res2 = [lambda: qv[:, 0:512], lambda: qv[:, 512:1024]]

        PS = [ctx.enter_context(nc.psum_tensor(f"PS{i}", [P, 1024], F32)) for i in range(4)]

        sem_names = [
            "xld0", "xld1", "mld", "mld2", "wld", "spv", "tms", "tmv", "tms2", "tmv2",
            "xnst0", "xnst1",
            "wqv", "wqst", "ccw", "xcl0", "xcl1", "att", "ats", "atsB", "atsC", "atf",
            "atv", "atvB", "yb0", "yb1", "yb2", "yb3",
            "ccv", "ccc", "ccl", "ccc2", "crv", "xqcl0", "xqcl1", "wqtl", "wdl", "gut", "gus",
            "guh", "hqv", "hst0", "hst1", "dcl0", "dcl1", "rld", "dnt", "dnv", "trt", "otv",
            "ost", "qvs", "qss", "xnv", "xqv", "ttr", "tev", "vfn",
        ]
        sems = {n: ctx.enter_context(nc.semaphore(n)) for n in sem_names}

        ENG = ("sync", "vector", "scalar", "tensor", "gpsimd")
        sched = {e: [] for e in ENG}
        cnt = {n: 0 for n in sem_names}

        def op(eng, build, sem=None, k=1, drain=False, fence=False):
            if fence and sem is None:
                sem, k, drain = "vfn", 1, True
            if sem is not None:
                cnt[sem] += k

            def fn(E, build=build, sem=sem, k=k, drain=drain):
                ins = build(E)
                if drain:
                    d = E.drain()
                    if sem is not None:
                        d.then_inc(sems[sem], k)
                elif sem is not None:
                    ins.then_inc(sems[sem], k)
            sched[eng].append(fn)
            return None

        WAITLOG = []

        def wait(eng, sem, val=None):
            v = cnt[sem] if val is None else val
            if v <= 0:
                return
            WAITLOG.append((eng, sem, v, len(sched[eng])))
            sched[eng].append(lambda E, sem=sem, v=v: E.wait_ge(sems[sem], v))

        def cur(sem):
            return cnt[sem]

        def emit_quant2(rows, width, src, dst, dst_sem=None, dst_k=1, soff=True):
            nb = width // 16
            R = rows
            V = lambda fn, dr=False, sem=None, k=1: op("vector", fn, drain=dr, sem=sem, k=k)
            S = lambda fn, dr=False, sem=None: op("scalar", fn, drain=dr, sem=sem)
            V(lambda E: E.reduce_max(
                qbm[0:R, 0:nb].unsqueeze(2).unsqueeze(3),
                src(0, width).rearrange("p (a b c) -> p a b c", b=4, c=4),
                axis=AX.XY, apply_absolute_value=True), dr=True)
            V(lambda E: E.tensor_scalar(qsc[0:R, 0:nb], qbm[0:R, 0:nb], C6TH, 1e-12, OP.mult, OP.max), dr=True)
            V(lambda E: E.reciprocal(qrs[0:R, 0:nb], qsc[0:R, 0:nb]))
            V(lambda E: E.tensor_copy(qsb[0:R, 0:nb], qsc[0:R, 0:nb]), dr=True)
            V(lambda E: E.tensor_copy(
                qrse[0:R, 0:width].rearrange("p (a b) -> p a b", a=nb),
                qrs[0:R, 0:nb].unsqueeze(2).broadcast_to([R, nb, 16])))
            V(lambda E: E.tensor_copy(
                qsse[0:R, 0:width].rearrange("p (a b) -> p a b", a=nb),
                qsb[0:R, 0:nb].unsqueeze(2).broadcast_to([R, nb, 16])), dr=True)
            V(lambda E: E.tensor_tensor(qv[0:R, 0:width], src(0, width), qrse[0:R, 0:width], OP.mult),
              dr=True, sem="qvs")
            wait("scalar", "qvs")
            if soff:
                S(lambda E: E.activation(qav[0:R, 0:width], qv[0:R, 0:width], AF.Abs), dr=True)
                S(lambda E: E.activation(qsg[0:R, 0:width], qv[0:R, 0:width], AF.Sign))
                S(lambda E: E.activation(qi[0:R, 0:width], qav[0:R, 0:width], AF.Copy,
                                         scale=af32(2.0)), dr=True)
                S(lambda E: E.activation(qa1[0:R, 0:width], qi[0:R, 0:width], AF.Copy,
                                         scale=af32(0.5)), dr=True)
                S(lambda E: E.activation(qi[0:R, 0:width], qav[0:R, 0:width], AF.Copy,
                                         scale=af32(1.0)), dr=True)
                S(lambda E: E.activation(qc2[0:R, 0:width], qi[0:R, 0:width], AF.Copy,
                                         scale=af32(1.0)), dr=True)
                S(lambda E: E.activation(qi[0:R, 0:width], qav[0:R, 0:width], AF.Copy,
                                         scale=af32(0.125), bias=float(-0.125)), dr=True)
                S(lambda E: E.activation(qc3[0:R, 0:width], qi[0:R, 0:width], AF.Copy,
                                         scale=af32(2.0)), dr=True, sem="qss")
                wait("vector", "qss")
                V(lambda E: E.tensor_scalar(qc1[0:R, 0:width], qa1[0:R, 0:width], f32(2.0), None, OP.min), dr=True)
                V(lambda E: E.tensor_scalar(qa1[0:R, 0:width], qc2[0:R, 0:width], f32(2.0), f32(4.0), OP.max, OP.min), dr=True)
                V(lambda E: E.tensor_scalar(qc2[0:R, 0:width], qc3[0:R, 0:width], f32(-2.0), None, OP.add), dr=True)
                V(lambda E: E.tensor_tensor(qc3[0:R, 0:width], qc1[0:R, 0:width], qa1[0:R, 0:width], OP.add), dr=True)
                V(lambda E: E.tensor_tensor(qc1[0:R, 0:width], qc3[0:R, 0:width], qc2[0:R, 0:width], OP.add), dr=True)
                V(lambda E: E.tensor_tensor(qa1[0:R, 0:width], qc1[0:R, 0:width], qsg[0:R, 0:width], OP.mult), dr=True)
                V(lambda E: E.tensor_tensor(dst(0, width), qa1[0:R, 0:width], qsse[0:R, 0:width], OP.mult),
                  dr=True, sem=dst_sem, k=dst_k)
            else:
                S(lambda E: E.activation(qav[0:R, 0:width], qv[0:R, 0:width], AF.Abs))
                S(lambda E: E.activation(qsg[0:R, 0:width], qv[0:R, 0:width], AF.Sign),
                  dr=True, sem="qss")
                wait("vector", "qss")
                V(lambda E: E.tensor_scalar(qi[0:R, 0:width], qav[0:R, 0:width], f32(2.0), None, OP.mult), dr=True)
                V(lambda E: E.tensor_copy(qa1[0:R, 0:width], qi[0:R, 0:width]), dr=True)
                V(lambda E: E.tensor_scalar(qi[0:R, 0:width], qav[0:R, 0:width], f32(1.0), None, OP.mult))
                V(lambda E: E.tensor_scalar(qc1[0:R, 0:width], qa1[0:R, 0:width], f32(4.0), f32(0.5), OP.min, OP.mult), dr=True)
                V(lambda E: E.tensor_copy(qa1[0:R, 0:width], qi[0:R, 0:width]), dr=True)
                V(lambda E: E.tensor_scalar(qi[0:R, 0:width], qav[0:R, 0:width], f32(0.125), f32(-0.125), OP.mult, OP.add))
                V(lambda E: E.tensor_scalar(qc2[0:R, 0:width], qa1[0:R, 0:width], f32(2.0), f32(4.0), OP.max, OP.min), dr=True)
                V(lambda E: E.tensor_copy(qa1[0:R, 0:width], qi[0:R, 0:width]), dr=True)
                V(lambda E: E.tensor_scalar(qc3[0:R, 0:width], qa1[0:R, 0:width], f32(2.0), f32(-2.0), OP.mult, OP.add), dr=True)
                V(lambda E: E.tensor_tensor(qa1[0:R, 0:width], qc1[0:R, 0:width], qc2[0:R, 0:width], OP.add), dr=True)
                V(lambda E: E.tensor_tensor(qc1[0:R, 0:width], qa1[0:R, 0:width], qc3[0:R, 0:width], OP.add), dr=True)
                V(lambda E: E.tensor_tensor(qc2[0:R, 0:width], qc1[0:R, 0:width], qsg[0:R, 0:width], OP.mult), dr=True)
                V(lambda E: E.tensor_tensor(dst(0, width), qc2[0:R, 0:width], qsse[0:R, 0:width], OP.mult),
                  dr=True, sem=dst_sem, k=dst_k)

        def qp_head(T, R, W, srcf):
            V = lambda fn, dr=False, sem=None: op("vector", fn, drain=dr, sem=sem)
            nb = W // 16
            V(lambda E: E.reduce_max(
                qbm[0:R, 0:nb].unsqueeze(2).unsqueeze(3),
                srcf().rearrange("p (a b c) -> p a b c", b=4, c=4),
                axis=AX.XY, apply_absolute_value=True), dr=True)
            V(lambda E: E.tensor_scalar(qsc[0:R, 0:nb], qbm[0:R, 0:nb], C6TH, 1e-12, OP.mult, OP.max), dr=True)
            V(lambda E: E.reciprocal(qrs[0:R, 0:nb], qsc[0:R, 0:nb]))
            V(lambda E: E.tensor_copy(T["qsb"](R, nb), qsc[0:R, 0:nb]), dr=True)
            V(lambda E: E.tensor_copy(
                qrse[0:R, 0:W].rearrange("p (a b) -> p a b", a=nb),
                qrs[0:R, 0:nb].unsqueeze(2).broadcast_to([R, nb, 16])), dr=True)
            V(lambda E: E.tensor_tensor(T["qv"](R, W), srcf(), qrse[0:R, 0:W], OP.mult),
              dr=True, sem="qvs")
            return cur("qvs")

        def qp_schain_gu(T, R, W, vqs):
            S = lambda fn, dr=False, sem=None: op("scalar", fn, drain=dr, sem=sem)
            wait("scalar", "qvs", vqs)
            S(lambda E: E.activation(T["qav"](R, W), T["qv"](R, W), AF.Abs))
            S(lambda E: E.activation(T["qsg"](R, W), T["qv"](R, W), AF.Sign))
            S(lambda E: E.activation(qi[0:R, 0:W], T["qv"](R, W), AF.Abs, scale=af32(2.0)))
            S(lambda E: E.activation(qi2[0:R, 0:W], T["qv"](R, W), AF.Abs, scale=af32(1.0)), dr=True)
            S(lambda E: E.activation(T["qa1"](R, W), qi[0:R, 0:W], AF.Copy, scale=af32(0.5)))
            S(lambda E: E.activation(T["qc2"](R, W), qi2[0:R, 0:W], AF.Copy, scale=af32(1.0)), dr=True)
            S(lambda E: E.activation(qi[0:R, 0:W], T["qav"](R, W), AF.Copy,
                                     scale=af32(0.125), bias=float(-0.125)), dr=True)
            S(lambda E: E.activation(T["qc3"](R, W), qi[0:R, 0:W], AF.Copy, scale=af32(2.0)),
              dr=True, sem="qss")
            return cur("qss")

        def qp_schain(T, R, W, vqs):
            S = lambda fn, dr=False, sem=None: op("scalar", fn, drain=dr, sem=sem)
            wait("scalar", "qvs", vqs)
            S(lambda E: E.activation(T["qav"](R, W), T["qv"](R, W), AF.Abs), dr=True)
            S(lambda E: E.activation(T["qsg"](R, W), T["qv"](R, W), AF.Sign))
            S(lambda E: E.activation(qi[0:R, 0:W], T["qav"](R, W), AF.Copy, scale=af32(2.0)), dr=True)
            S(lambda E: E.activation(T["qa1"](R, W), qi[0:R, 0:W], AF.Copy, scale=af32(0.5)), dr=True)
            S(lambda E: E.activation(qi[0:R, 0:W], T["qav"](R, W), AF.Copy, scale=af32(1.0)), dr=True)
            S(lambda E: E.activation(T["qc2"](R, W), qi[0:R, 0:W], AF.Copy, scale=af32(1.0)), dr=True)
            S(lambda E: E.activation(qi[0:R, 0:W], T["qav"](R, W), AF.Copy,
                                     scale=af32(0.125), bias=float(-0.125)), dr=True)
            S(lambda E: E.activation(T["qc3"](R, W), qi[0:R, 0:W], AF.Copy, scale=af32(2.0)),
              dr=True, sem="qss")
            return cur("qss")

        def qp_tail(T, R, W, dstf, dst_sem, dst_k, sqs):
            V = lambda fn, dr=False, sem=None, k=1: op("vector", fn, drain=dr, sem=sem, k=k)
            nb = W // 16
            wait("vector", "qss", sqs)
            V(lambda E: E.tensor_scalar(qc1[0:R, 0:W], T["qa1"](R, W), f32(2.0), None, OP.min), dr=True)
            V(lambda E: E.tensor_scalar(T["qa1"](R, W), T["qc2"](R, W), f32(2.0), f32(4.0), OP.max, OP.min), dr=True)
            V(lambda E: E.tensor_scalar(T["qc2"](R, W), T["qc3"](R, W), f32(-2.0), None, OP.add), dr=True)
            V(lambda E: E.tensor_tensor(T["qc3"](R, W), qc1[0:R, 0:W], T["qa1"](R, W), OP.add), dr=True)
            V(lambda E: E.tensor_copy(
                qsse[0:R, 0:W].rearrange("p (a b) -> p a b", a=nb),
                T["qsb"](R, nb).unsqueeze(2).broadcast_to([R, nb, 16])))
            V(lambda E: E.tensor_tensor(qc1[0:R, 0:W], T["qc3"](R, W), T["qc2"](R, W), OP.add), dr=True)
            V(lambda E: E.tensor_tensor(T["qa1"](R, W), qc1[0:R, 0:W], T["qsg"](R, W), OP.mult), dr=True)
            V(lambda E: E.tensor_tensor(dstf(), T["qa1"](R, W), qsse[0:R, 0:W], OP.mult),
              dr=True, sem=dst_sem, k=dst_k)

        def qp_schain_lite(T, R, W, vqs):
            S = lambda fn, dr=False, sem=None: op("scalar", fn, drain=dr, sem=sem)
            wait("scalar", "qvs", vqs)
            S(lambda E: E.activation(T["qav"](R, W), T["qv"](R, W), AF.Abs))
            S(lambda E: E.activation(T["qsg"](R, W), T["qv"](R, W), AF.Sign),
              dr=True, sem="qss")
            return cur("qss")

        def qp_tail_v(T, R, W, dstf, dst_sem, dst_k, sqs):
            V = lambda fn, dr=False, sem=None, k=1: op("vector", fn, drain=dr, sem=sem, k=k)
            nb = W // 16
            wait("vector", "qss", sqs)
            V(lambda E: E.tensor_scalar(qi[0:R, 0:W], T["qav"](R, W), f32(2.0), None, OP.mult), dr=True)
            V(lambda E: E.tensor_copy(qa1[0:R, 0:W], qi[0:R, 0:W]), dr=True)
            V(lambda E: E.tensor_scalar(qi[0:R, 0:W], T["qav"](R, W), f32(1.0), None, OP.mult))
            V(lambda E: E.tensor_scalar(qc1[0:R, 0:W], qa1[0:R, 0:W], f32(4.0), f32(0.5), OP.min, OP.mult), dr=True)
            V(lambda E: E.tensor_copy(qa1[0:R, 0:W], qi[0:R, 0:W]), dr=True)
            V(lambda E: E.tensor_scalar(qi[0:R, 0:W], T["qav"](R, W), f32(0.125), f32(-0.125), OP.mult, OP.add))
            V(lambda E: E.tensor_scalar(qc2[0:R, 0:W], qa1[0:R, 0:W], f32(2.0), f32(4.0), OP.max, OP.min), dr=True)
            V(lambda E: E.tensor_copy(qa1[0:R, 0:W], qi[0:R, 0:W]), dr=True)
            V(lambda E: E.tensor_scalar(qc3[0:R, 0:W], qa1[0:R, 0:W], f32(2.0), f32(-2.0), OP.mult, OP.add), dr=True)
            V(lambda E: E.tensor_tensor(qa1[0:R, 0:W], qc1[0:R, 0:W], qc2[0:R, 0:W], OP.add), dr=True)
            V(lambda E: E.tensor_copy(
                qsse[0:R, 0:W].rearrange("p (a b) -> p a b", a=nb),
                T["qsb"](R, nb).unsqueeze(2).broadcast_to([R, nb, 16])))
            V(lambda E: E.tensor_tensor(qc1[0:R, 0:W], qa1[0:R, 0:W], qc3[0:R, 0:W], OP.add), dr=True)
            V(lambda E: E.tensor_tensor(qc2[0:R, 0:W], qc1[0:R, 0:W], T["qsg"](R, W), OP.mult), dr=True)
            V(lambda E: E.tensor_tensor(dstf(), qc2[0:R, 0:W], qsse[0:R, 0:W], OP.mult),
              dr=True, sem=dst_sem, k=dst_k)

        TL_TM = [
            {"qv": lambda R, W: qv[0:R, 0:W], "qav": lambda R, W: aM[0:R, 0:W],
             "qa1": lambda R, W: qa1[0:R, 0:W], "qc2": lambda R, W: qc2[0:R, 0:W],
             "qc3": lambda R, W: qc3[0:R, 0:W], "qsg": lambda R, W: qsg[0:R, 0:W],
             "qsb": lambda R, nb: qsb[0:R, 0:nb]},
            {"qv": lambda R, W: logaM[0:R, 0:W], "qav": lambda R, W: aM[0:R, 1024:1024 + W],
             "qa1": lambda R, W: xqblk[0][0:R, 0:W], "qc2": lambda R, W: xqblk[1][0:R, 0:W],
             "qc3": lambda R, W: xnslM[0:R, 1024:1024 + W], "qsg": lambda R, W: winb[0:R, 0:W],
             "qsb": lambda R, nb: qsb2[0:R, 0:nb]},
        ]
        TL_W = [
            {"qv": lambda R, W: qv[0:R, 0:W], "qav": lambda R, W: aM[0:R, 0:W],
             "qsg": lambda R, W: qsg[0:R, 0:W], "qsb": lambda R, nb: qsb[0:R, 0:nb]},
            {"qv": lambda R, W: xb0[0:R, 0:W], "qav": lambda R, W: xb1[0:R, 0:W],
             "qsg": lambda R, W: xqblk[0][0:R, 0:W], "qsb": lambda R, nb: qsb2[0:R, 0:nb]},
        ]
        TL_GU = [
            TL_TM[0],
            {"qv": lambda R, W: logaM[0:R, 0:W], "qav": lambda R, W: aM[0:R, 1024:1024 + W],
             "qa1": lambda R, W: winb[0:R, 0:W], "qc2": lambda R, W: wab[0:R, 0:W],
             "qc3": lambda R, W: xn16t[0][0:R, 0:W], "qsg": lambda R, W: xn16t[1][0:R, 0:W],
             "qsb": lambda R, nb: qsb2[0:R, 0:nb]},
        ]

        op("sync", lambda E: E.dma_start(out=idf[:], in_=identf), sem="mld", k=16)
        op("sync", lambda E: E.dma_start(out=selc[:], in_=selcol), sem="mld", k=16)
        op("sync", lambda E: E.dma_start(out=spu[:], in_=apg), sem="mld", k=16)
        def emit_mld2():
            wait("gpsimd", "xnst0", 256)
            wait("gpsimd", "xnst1", 256)
            op("gpsimd", lambda E: E.dma_start(
                out=winb[:].rearrange("d (h e) -> d h e", h=NG),
                in_=w_in.rearrange("h d e -> d h e")), sem="mld2", k=16)
            op("gpsimd", lambda E: E.dma_start(
                out=wab[:].rearrange("d (h e) -> d h e", h=NG),
                in_=w_a.rearrange("h d e -> d h e")), sem="mld2", k=16)
        def emit_xloads(pr):
            op("sync", lambda E, pr=pr: E.dma_start(
                out=xb0[:], in_=x_tm[2 * pr * P:(2 * pr + 1) * P, :]), sem="xld0", k=16)
            op("sync", lambda E, pr=pr: E.dma_start(
                out=xb1[:], in_=x_tm[(2 * pr + 1) * P:(2 * pr + 2) * P, :]), sem="xld1", k=16)
        emit_xloads(0)

        wait("scalar", "mld", 48)
        op("scalar", lambda E: E.activation(spt1[:], spu[:], AF.Exp), sem="spv", drain=True)
        wait("vector", "spv")
        op("vector", lambda E: E.tensor_scalar(spt2[:], spt1[:], f32(-0.25), f32(1.0 / 3.0), OP.mult, OP.add), fence=True)
        op("vector", lambda E: E.scalar_tensor_tensor(spneg[:], spt2[:], f32(-1.0), spt1[:], OP.mult, OP.mult), fence=True)
        op("vector", lambda E: E.tensor_scalar(spt2[:], spneg[:], f32(0.5), None, OP.add), fence=True)
        op("vector", lambda E: E.scalar_tensor_tensor(spneg[:], spt2[:], f32(-1.0), spt1[:], OP.mult, OP.mult), fence=True)
        op("vector", lambda E: E.tensor_scalar(spt2[:], spneg[:], f32(1.0), None, OP.add), fence=True)
        op("vector", lambda E: E.tensor_mul(spu[:], spt2[:], spt1[:]), fence=True)
        op("vector", lambda E: E.tensor_scalar(spneg[:], spu[:], f32(-8.0), None, OP.mult), fence=True)
        op("vector", lambda E: E.memset(zasl[:], 0.0), fence=True)
        op("vector", lambda E: E.memset(xnslM[:, 0:1024], 1.0), sem="spv", drain=True)

        XB = [xb0, xb1]
        SCR2 = [lnscr, xnb]
        SSC = [qv, qrse]
        QPEND = []
        TTRXN, TEVXN, TTRXQ, TEVXQ = {}, {}, {}, {}

        def pend_pop():
            ent = QPEND.pop(0)
            if ent[0] == "tm":
                _, tc2, s2, vqs2 = ent
                sqs2 = qp_schain(TL_TM[s2], P, D, vqs2)
                qp_tail(TL_TM[s2], P, D, lambda s2=s2: gxM[:, s2 * 1024:(s2 + 1) * 1024],
                        "xqv", 1, sqs2)
                wait("tensor", "xqv", tc2 + 1)
                if tc2 >= 1:
                    wait("tensor", "tev", TEVXQ[tc2 - 1])
                for b in range(8):
                    op("tensor", lambda E, s2=s2, b=b: E.transpose(
                        PS[3][:, b * P:(b + 1) * P], gxM[:, s2 * 1024 + b * P:s2 * 1024 + (b + 1) * P],
                        idf[:]), sem=("ttr" if b == 7 else None))
                TTRXQ[tc2] = cur("ttr")
                wait("scalar", "ttr", TTRXQ[tc2])
                wait("scalar", f"xnst{s2}", 32 * (tc2 // 2))
                op("scalar", lambda E, s2=s2: E.activation(
                    xqo[s2][:, 0:D], PS[3][:, 0:D], AF.Copy), sem="tev", drain=True)
                TEVXQ[tc2] = cur("tev")
                wait("sync", "tev", TEVXQ[tc2])
                op("sync", lambda E, tc2=tc2, s2=s2: E.dma_start(
                    out=xq16_d[:, tc2 * P:(tc2 + 1) * P].rearrange("(b d) t -> d b t", b=8),
                    in_=xqo[s2][:].rearrange("p (b t) -> p b t", b=8)), sem=f"xnst{s2}", k=16)
            else:
                _, k2, rr2, ww2, dst2, vqs2 = ent
                p2 = k2 % 2
                sqs2 = qp_schain_lite(TL_W[p2], rr2, ww2, vqs2)
                wait("vector", "wqst", 16 * k2)
                qp_tail_v(TL_W[p2], rr2, ww2,
                          lambda rr2=rr2, ww2=ww2: wqo[0:rr2, 0:ww2], "wqv", 1, sqs2)
                wait("gpsimd", "wqv")
                op("gpsimd", lambda E, dst2=dst2, rr2=rr2, ww2=ww2: E.dma_start(
                    out=dst2, in_=wqo[0:rr2, 0:ww2]), sem="wqst", k=16)

        for it in range(NPR):
            if True:
                pr = it
                pp = pr % 2
                wait("vector", "xld0", 16 * (pr + 1))
                wait("vector", "xld1", 16 * (pr + 1))
                if pr >= 2:
                    wait("vector", "tms2", pr - 1)
                for s in range(2):
                    op("vector", lambda E, s=s: E.tensor_tensor(
                        SCR2[s][:], XB[s][:], XB[s][:], OP.mult))
                for s in range(2):
                    op("vector", lambda E, s=s: E.tensor_tensor_scan(
                        SSC[s][:], xnslM[:, 0:1024], XB[s][:], 0.0, OP.mult, OP.add), drain=(s == 1))
                for s in range(2):
                    op("vector", lambda E, s=s, pp=pp: E.tensor_copy(
                        lnsx[pp][s][:], SSC[s][:, 1023:1024]), drain=(s == 1))
                for s in range(2):
                    op("vector", lambda E, s=s: E.tensor_tensor_scan(
                        SSC[s][:], xnslM[:, 0:1024], SCR2[s][:], 0.0, OP.mult, OP.add), drain=(s == 1))
                for s in range(2):
                    op("vector", lambda E, s=s, pp=pp: E.tensor_copy(
                        lnsq[pp][s][:], SSC[s][:, 1023:1024]))
                for s in range(2):
                    op("vector", lambda E, s=s, pp=pp: E.tensor_scalar(
                        lnm[pp][s][:], lnsx[pp][s][:], f32(1.0 / D), None, OP.mult), drain=(s == 1))
                for s in range(2):
                    op("vector", lambda E, s=s, pp=pp: E.tensor_mul(
                        lnm2[s][:], lnm[pp][s][:], lnm[pp][s][:]), drain=(s == 1))
                for s in range(2):
                    op("vector", lambda E, s=s, pp=pp: E.scalar_tensor_tensor(
                        lnv0[s][:], lnsq[pp][s][:], f32(1.0 / D), lnm2[s][:], OP.mult, OP.subtract), drain=(s == 1))
                for s in range(2):
                    op("vector", lambda E, s=s: E.tensor_scalar(
                        lnv1[s][:], lnv0[s][:], f32(1e-5), None, OP.add), drain=(s == 1))
                for s in range(2):
                    op("vector", lambda E, s=s, pp=pp: E.reciprocal(
                        lnr1[pp][s][:], lnv1[s][:]),
                       sem=("tmv" if s == 1 else None), drain=(s == 1))
                wait("scalar", "tmv")
                for s in range(2):
                    op("scalar", lambda E, s=s, pp=pp: E.activation(
                        lnrs[pp][s][:], lnr1[pp][s][:], AF.Sqrt),
                       sem=("tms2" if s == 1 else None), drain=(s == 1))
            if True:
                pr = it
                pp = pr % 2
                wait("vector", "tms2", pr + 1)

                for s in range(2):
                    tc = 2 * pr + s
                    if tc >= 1:
                        wait("vector", "ttr", TTRXN[tc - 1])
                    for h in range(2):
                        op("vector", lambda E, s=s, pp=pp, h=h: E.tensor_scalar(
                            xnb[:, h * 512:(h + 1) * 512], XB[s][:, h * 512:(h + 1) * 512],
                            lnm[pp][s][:], lnrs[pp][s][:], OP.subtract, OP.mult),
                           sem=("xnv" if h == 1 else None), k=1, drain=(h == 1))
                    vqs = qp_head(TL_TM[s], P, D, lambda: xnb[:, 0:D])
                    if tc == 0:
                        wait("tensor", "mld", 48)
                    wait("tensor", "xnv", tc + 1)
                    if tc >= 1:
                        wait("tensor", "tev", TEVXN[tc - 1])
                    for b in range(8):
                        op("tensor", lambda E, b=b: E.transpose(
                            PS[2][:, b * P:(b + 1) * P], xnb[:, b * P:(b + 1) * P],
                            idf[:]), sem=("ttr" if b == 7 else None))
                    TTRXN[tc] = cur("ttr")
                    wait("scalar", "ttr", TTRXN[tc])
                    wait("scalar", f"xnst{s}", 32 * pr)
                    op("scalar", lambda E, s=s: E.activation(
                        xn16t[s][:, 0:D], PS[2][:, 0:D], AF.Copy), sem="tev", drain=True)
                    TEVXN[tc] = cur("tev")
                    wait("sync", "tev", TEVXN[tc])
                    op("sync", lambda E, tc=tc, s=s: E.dma_start(
                        out=xn16_d[:, tc * P:(tc + 1) * P].rearrange("(b d) t -> d b t", b=8),
                        in_=xn16t[s][:].rearrange("p (b t) -> p b t", b=8)), sem=f"xnst{s}", k=16)
                    if s == 1 and pr + 1 < NPR:
                        wait("sync", "xnv", cur("xnv"))
                        emit_xloads(pr + 1)
                    if QPEND:
                        pend_pop()
                    QPEND.append(("tm", tc, s, vqs))

        wsubs = []
        for j in range(3):
            rr = min(P, WGU - j * P)
            wsubs.append((wg_sl[j * P:j * P + rr, :], rr, 1024,
                          wgq_sl_d.ap()[j * P:j * P + rr, :]))
        for j in range(3):
            rr = min(P, WGU - j * P)
            wsubs.append((wu_sl[j * P:j * P + rr, :], rr, 1024,
                          wuq_sl_d.ap()[j * P:j * P + rr, :]))
        for j in range(NQ):
            wsubs.append((wd_sl[:, j * IQ:(j + 1) * IQ], P, IQ,
                          wdq_sl_d.ap()[:, j * IQ:(j + 1) * IQ]))
        wvqs_prev = None
        for k, (wsrc, rr, ww, dst) in enumerate(wsubs):
            if wvqs_prev is not None:
                wait("sync", "qvs", wvqs_prev)
            op("sync", lambda E, wsrc=wsrc, rr=rr, ww=ww: E.dma_start(
                out=wbuf[0:rr, 0:ww], in_=wsrc), sem="wld", k=16)
            wait("vector", "wld", 16 * (k + 1))
            wvqs_prev = qp_head(TL_W[k % 2], rr, ww, lambda rr=rr, ww=ww: wbuf[0:rr, 0:ww])
            if QPEND:
                pend_pop()
            QPEND.append(("w", k, rr, ww, dst, wvqs_prev))
        while QPEND:
            pend_pop()
        emit_mld2()
        wait("gpsimd", "wqst")
        op("gpsimd", lambda E: E.collective_compute(
            "AllGather", OP.bypass, replica_groups=[list(range(NCORE))],
            ins=[wgq_sl_d.ap().opt()], outs=[wgq_d.ap().opt()]), sem="ccw")
        wait("gpsimd", "ccw", 1)
        op("gpsimd", lambda E: E.collective_compute(
            "AllGather", OP.bypass, replica_groups=[list(range(NCORE))],
            ins=[wuq_sl_d.ap().opt()], outs=[wuq_d.ap().opt()]), sem="ccw")
        wait("gpsimd", "ccw", 2)
        op("gpsimd", lambda E: E.collective_compute(
            "AllGather", OP.bypass, replica_groups=[list(range(NCORE))],
            ins=[wdq_sl_d.ap().opt()], outs=[wdq_d.ap().opt()]), sem="ccw")

        if dbg:
            wait("gpsimd", "ccw", 2)
            op("gpsimd", lambda E: E.dma_start(out=dbgwg_d, in_=wgq_d.ap()), sem="ccv", k=16)
            op("gpsimd", lambda E: E.dma_start(out=dbgwu_d, in_=wuq_d.ap()), sem="ccv", k=16)

        ybcnt = [0, 0, 0, 0]

        def xnsl(sl, g):
            if sl % 2 == 0:
                return xnslM[:, g * ASL:(g + 1) * ASL]
            if g < 4:
                return xn16t[0][:, g * ASL:(g + 1) * ASL]
            return xn16t[1][:, (g - 4) * ASL:(g - 3) * ASL]

        def emit_xcl(sl):
            if sl >= NASL:
                return
            wait("sync", "xnst0")
            wait("sync", "xnst1")
            if sl > 1:
                wait("sync", "atv", 8 * (sl - 1))
            if sl % 2 == 0:
                op("sync", lambda E, sl=sl: E.dma_start(
                    out=xnslM[:].rearrange("p (g t) -> p g t", g=NG),
                    in_=xn16_d[:, sl * ASL:(sl + 1) * ASL].rearrange("(g p) t -> p g t", g=NG)),
                   sem="xcl0", k=16)
            else:
                for hh in range(2):
                    op("sync", lambda E, sl=sl, hh=hh: E.dma_start(
                        out=xn16t[hh][:].rearrange("p (g t) -> p g t", g=4),
                        in_=xn16_d[hh * 512:(hh + 1) * 512,
                                   sl * ASL:(sl + 1) * ASL].rearrange("(g p) t -> p g t", g=4)),
                       sem="xcl1", k=16)
        emit_xcl(0)
        emit_xcl(1)

        def emit_wqtl():
            wait("sync", "ccw", 1)
            for d in range(NG):
                op("sync", lambda E, d=d: E.dma_start_transpose(
                    AR_W[:, d * DI:(d + 1) * DI], wgq_d.ap()[:, d * P:(d + 1) * P]), sem="wqtl", k=16)
            wait("sync", "ccw", 2)
            for d in range(NG):
                op("sync", lambda E, d=d: E.dma_start_transpose(
                    AR_W[:, (NG + d) * DI:(NG + d + 1) * DI], wuq_d.ap()[:, d * P:(d + 1) * P]), sem="wqtl", k=16)

        for sl in range(NASL):
            if sl == 4:
                emit_wqtl()
            if sl == 0:
                wait("tensor", "tev", 32)
            wait("tensor", "mld", 48)
            wait("tensor", "mld2", 32)
            wait("tensor", f"xcl{sl % 2}", (16 if sl % 2 == 0 else 32) * (sl // 2 + 1))
            if sl > 0:
                wait("tensor", "ats", 8 * sl)
            aps = lambda g: PS[g // 2][:, (g % 2) * 512:(g % 2) * 512 + ASL]
            xps = lambda g: PS[g // 2][:, (g % 2) * 512 + ASL:(g % 2) * 512 + 2 * ASL]
            for g in range(NG):
                op("tensor", lambda E, sl=sl, g=g: E.matmul(
                    aps(g), wab[:, g * P:(g + 1) * P],
                    xnsl(sl, g), start=True, stop=True))
                op("tensor", lambda E, sl=sl, g=g: E.matmul(
                    xps(g), winb[:, g * P:(g + 1) * P],
                    xnsl(sl, g), start=True, stop=True),
                   sem=("att" if g == NG - 1 else None))
            wait("scalar", "att", sl + 1)
            if sl > 0:
                wait("scalar", "atf", sl)
            for g in range(NG):
                op("scalar", lambda E, g=g: E.activation(bb1[g % 2][:], aps(g), AF.Sigmoid))
                op("scalar", lambda E, g=g: E.activation(
                    gxM[:, g * ASL:(g + 1) * ASL], xps(g), AF.Sigmoid), drain=True)
                op("scalar", lambda E, g=g: E.activation(
                    logaM[:, g * ASL:(g + 1) * ASL], bb1[g % 2][:], AF.Copy,
                    scale=spneg[:, g:g + 1]), sem="ats", drain=(g == NG - 1))
            for g in range(NG):
                op("scalar", lambda E, g=g: E.activation(
                    multM[:, g * ASL:(g + 1) * ASL], logaM[:, g * ASL:(g + 1) * ASL], AF.Exp),
                   sem=("atsB" if g == NG - 1 else None), drain=(g == NG - 1))
            wait("vector", "atsB")
            op("vector", lambda E: E.tensor_scalar(aM[:], multM[:], 1.0, None, OP.min),
               sem="atvB", drain=True)
            wait("scalar", "atvB")
            for g in range(NG):
                op("scalar", lambda E, g=g: E.activation(
                    logaM[:, g * ASL:(g + 1) * ASL], aM[:, g * ASL:(g + 1) * ASL], AF.Square),
                   drain=(g == NG - 1))
            for g in range(NG):
                op("scalar", lambda E, g=g: E.activation(
                    multM[:, g * ASL:(g + 1) * ASL], logaM[:, g * ASL:(g + 1) * ASL],
                    AF.Sqrt, bias=af32(1.0), scale=af32(-1.0)),
                   sem=("atsC" if g == NG - 1 else None), drain=(g == NG - 1))
            wait("vector", "atsC")
            if sl == 0:
                wait("vector", "ttr", TTRXN[NTC - 1])
            for hh in range(2):
                op("vector", lambda E, hh=hh: E.tensor_mul(
                    [lnscr, xnb][hh][:], multM[:, hh * 1024:(hh + 1) * 1024],
                    gxM[:, hh * 1024:(hh + 1) * 1024], ), sem=("atf" if hh == 1 else None),
                   drain=(hh == 1))
            for gp in range(NG // 2):
                gg = [2 * gp, 2 * gp + 1]
                bidx = [2 * (gp % 2), 2 * (gp % 2) + 1]
                for s in range(2):
                    b = bidx[s]
                    if ybcnt[b] > 0:
                        wait("vector", f"yb{b}", 32 * ybcnt[b])
                for s in range(2):
                    op("vector", lambda E, sl=sl, g=gg[s], s=s: E.tensor_tensor(
                        bb2[s][:], [lnscr, xnb][g // 4][:, (g % 4) * ASL:(g % 4 + 1) * ASL],
                        xnsl(sl, g), OP.mult), drain=(s == 1))
                for s in range(2):
                    if sl == 0:
                        op("vector", lambda E, g=gg[s], s=s, b=bidx[s]: E.tensor_tensor_scan(
                            yslb[b][:], aM[:, g * ASL:(g + 1) * ASL], bb2[s][:], 0.0, OP.mult, OP.add))
                    else:
                        op("vector", lambda E, g=gg[s], s=s, b=bidx[s]: E.tensor_tensor_scan(
                            yslb[b][:], aM[:, g * ASL:(g + 1) * ASL], bb2[s][:],
                            ycol[:, g:g + 1], OP.mult, OP.add))
                for s in range(2):
                    if sl == 0:
                        op("vector", lambda E, g=gg[s], b=bidx[s]: E.tensor_tensor_scan(
                            pslb[b][:], aM[:, g * ASL:(g + 1) * ASL], zasl[:], 1.0, OP.mult, OP.add),
                           drain=(s == 1))
                    else:
                        op("vector", lambda E, g=gg[s], b=bidx[s]: E.tensor_tensor_scan(
                            pslb[b][:], aM[:, g * ASL:(g + 1) * ASL], zasl[:],
                            pcol[:, g:g + 1], OP.mult, OP.add), drain=(s == 1))
                for s in range(2):
                    op("vector", lambda E, g=gg[s], b=bidx[s]: E.tensor_copy(
                        ycol[:, g:g + 1], yslb[b][:, ASL - 1:ASL]))
                for s in range(2):
                    op("vector", lambda E, g=gg[s], b=bidx[s]: E.tensor_copy(
                        pcol[:, g:g + 1], pslb[b][:, ASL - 1:ASL]),
                       sem=("atv" if s == 1 else None), k=2, drain=(s == 1))
                wait("sync", "atv", cur("atv"))
                for s in range(2):
                    b = bidx[s]
                    op("sync", lambda E, sl=sl, g=gg[s], b=b: E.dma_start(
                        out=y_d[g, :, sl * ASL:(sl + 1) * ASL], in_=yslb[b][:]), sem=f"yb{b}", k=16)
                    op("sync", lambda E, sl=sl, g=gg[s], b=b: E.dma_start(
                        out=p_d[g, :, sl * ASL:(sl + 1) * ASL], in_=pslb[b][:]), sem=f"yb{b}", k=16)
                    ybcnt[b] += 1
            if gp == NG // 2 - 1:
                emit_xcl(sl + 2)

        op("vector", lambda E: E.tensor_copy(ccs[:, 0:NG], pcol[:]), fence=True)
        op("vector", lambda E: E.tensor_copy(ccs[:, NG:16], ycol[:]), sem="ccv", drain=True)
        wait("gpsimd", "ccv")
        op("gpsimd", lambda E: E.dma_start(out=cc_in[:, :], in_=ccs[:]), sem="ccc", k=16)
        wait("gpsimd", "ccc", 16)
        op("gpsimd", lambda E: E.collective_compute(
            "AllGather", OP.bypass, replica_groups=[list(range(NCORE))],
            ins=[cc_in.ap().opt()], outs=[cc_out.ap().opt()]), sem="ccl")
        wait("gpsimd", "ccl", 1)
        op("gpsimd", lambda E: E.dma_start(
            out=carr[:].rearrange("p (r c) -> p r c", r=NCORE),
            in_=cc_out.ap().rearrange("(r p) c -> p r c", p=P)), sem="ccc2", k=16)
        wait("vector", "ccc2", 16)
        op("vector", lambda E: E.memset(cc0[:], 0.0), fence=True)
        op("vector", lambda E: E.memset(cs0[:], 0.0), fence=True)
        ccur, cnxt = cc0, cc1
        scur, snxt = cs0, cs1
        for r in range(NCORE):
            op("vector", lambda E, r=r: E.tensor_copy(selq[:], selc[:, r:r + 1]), fence=True)
            op("vector", lambda E, r=r, ccur=ccur, scur=scur, snxt=snxt: E.scalar_tensor_tensor(
                snxt[:], ccur[:], selq[:], scur[:], OP.mult, OP.add), fence=True)
            scur, snxt = snxt, scur
            if r < NCORE - 1:
                if r + 1 == 4:
                    op("vector", lambda E, cnxt=cnxt: E.memset(cnxt[:], 0.0), fence=True)
                else:
                    op("vector", lambda E, r=r, ccur=ccur: E.tensor_tensor(
                        cct[:], carr[:, r * 16:r * 16 + NG], ccur[:], OP.mult), fence=True)
                    op("vector", lambda E, r=r, cnxt=cnxt: E.tensor_tensor(
                        cnxt[:], cct[:], carr[:, r * 16 + NG:r * 16 + 16], OP.add), fence=True)
                ccur, cnxt = cnxt, ccur
        cmine = scur
        if dbg:
            op("vector", lambda E, cmine=cmine: E.tensor_copy(spt2[:], cmine[:]), fence=True)
            wait("gpsimd", "vfn")
            op("gpsimd", lambda E: E.dma_start(out=dbgcc_d[:, 0:128], in_=carr[:]), sem="ccv", k=16)
            op("gpsimd", lambda E: E.dma_start(out=dbgcc_d[:, 128:136], in_=spt2[:]), sem="ccv", k=16)
            op("gpsimd", lambda E: E.dma_start(out=dbgcc_d[:, 152:160], in_=selc[:]), sem="ccv", k=16)
            op("gpsimd", lambda E: E.dma_start(out=dbgcc_d[:, 136:144], in_=pcol[:]), sem="ccv", k=16)
            op("gpsimd", lambda E: E.dma_start(out=dbgcc_d[:, 144:152], in_=ycol[:]), sem="ccv", k=16)
        op("vector", lambda E, cmine=cmine: E.tensor_scalar(
            cct[:], cmine[:], f32(1.0), None, OP.mult), sem="crv", drain=True)

        IH = IQ // 2
        assert not QPEND

        def gu_tail(ent):
            tc2, q2, it2, vqs2 = ent
            par2 = it2 % 2
            sqs2 = qp_schain_gu(TL_GU[par2], P, IQ, vqs2)
            if it2 >= 2:
                wait("vector", f"hst{par2}", 16 * (it2 // 2))
            qp_tail(TL_GU[par2], P, IQ, lambda par2=par2: hqo[par2][0:P, 0:IQ],
                    "hqv", 1, sqs2)
            wait("gpsimd", "hqv")
            op("gpsimd", lambda E, tc2=tc2, q2=q2, par2=par2: E.dma_start(
                out=hq_d[tc2 * P:(tc2 + 1) * P, q2 * IQ:(q2 + 1) * IQ],
                in_=hqo[par2][0:P, 0:IQ]), sem=f"hst{par2}", k=16)
        if dbg:
            wait("gpsimd", "wqtl", 256)
            op("gpsimd", lambda E: E.dma_start(out=dbgar1_d, in_=AR_W[:, :]), sem="ccv", k=16)
        wait("tensor", "wqtl")
        wait("tensor", "ats")
        for tc in range(NTC):
            wait("sync", "xnst0")
            wait("sync", "xnst1")
            if tc == 0:
                wait("sync", "wqst", cur("wqst"))
            if tc >= 2:
                wait("sync", "gut", 4 * (tc - 1))
            op("sync", lambda E, tc=tc: E.dma_start(
                out=xqblk[tc % 2][:].rearrange("p (b t) -> p b t", b=NG),
                in_=xq16_d[:, tc * P:(tc + 1) * P].rearrange("(b d) t -> d b t", b=NG)),
               sem=f"xqcl{tc % 2}", k=16)
            for q in range(NQ):
                it = tc * NQ + q
                psG = PS[0 if it % 2 == 0 else 2]
                psU = PS[1 if it % 2 == 0 else 3]
                if it >= 2:
                    wait("tensor", "gus", it - 1)
                    wait("tensor", "guh", it - 1)
                if q == 0:
                    wait("tensor", f"xqcl{tc % 2}", 16 * (tc // 2 + 1))
                for d in range(NG):
                    lhs = lambda d=d, tc=tc: xqblk[tc % 2][:, d * P:(d + 1) * P]
                    rg = lambda d=d, q=q: AR_W[:, d * DI + q * IQ: d * DI + (q + 1) * IQ]
                    ru = lambda d=d, q=q: AR_W[:, (NG + d) * DI + q * IQ: (NG + d) * DI + (q + 1) * IQ]
                    st0, st1 = (d == 0), (d == NG - 1)
                    op("tensor", lambda E, lhs=lhs, rg=rg, psG=psG, st0=st0, st1=st1: E.matmul(
                        psG[:, 0:512], lhs(), rg()[:, 0:512], start=st0, stop=st1))
                    op("tensor", lambda E, lhs=lhs, rg=rg, psG=psG, st0=st0, st1=st1: E.matmul(
                        psG[:, 512:IQ], lhs(), rg()[:, 512:IQ], start=st0, stop=st1))
                    op("tensor", lambda E, lhs=lhs, ru=ru, psU=psU, st0=st0, st1=st1: E.matmul(
                        psU[:, 0:512], lhs(), ru()[:, 0:512], start=st0, stop=st1))
                    op("tensor", lambda E, lhs=lhs, ru=ru, psU=psU, st0=st0, st1=st1: E.matmul(
                        psU[:, 512:IQ], lhs(), ru()[:, 512:IQ], start=st0, stop=st1),
                       sem=("gut" if st1 else None))
                wait("scalar", "gut")
                if it >= 1:
                    wait("scalar", "guh", it)
                for s in range(2):
                    op("scalar", lambda E, psG=psG, s=s: E.activation(
                        hsil[:, s * IH:(s + 1) * IH], psG[:, s * IH:(s + 1) * IH], AF.Silu),
                       sem=("gus" if s == 1 else None), drain=(s == 1))
                wait("vector", "gus")
                for s in range(2):
                    op("vector", lambda E, psU=psU, s=s: E.tensor_tensor(
                        hful[:, s * IH:(s + 1) * IH], hsil[:, s * IH:(s + 1) * IH],
                        psU[:, s * IH:(s + 1) * IH], OP.mult),
                       sem=("guh" if s == 1 else None), drain=(s == 1))
                if dbg:
                    wait("gpsimd", "guh", it + 1)
                    op("gpsimd", lambda E, it=it: E.dma_start(
                        out=dbgh_d[it, :, :], in_=hful[:, 0:IQ]), sem="ccv", k=16)
                par = it % 2
                vqs = qp_head(TL_GU[par], P, IQ, lambda: hful[:, 0:IQ])
                if QPEND:
                    gu_tail(QPEND.pop(0))
                QPEND.append((tc, q, it, vqs))

        if dbg:
            wait("gpsimd", "gut", 64)
            op("gpsimd", lambda E: E.dma_start(out=dbgar2_d, in_=AR_W[:, :]), sem="ccv", k=16)

        while QPEND:
            gu_tail(QPEND.pop(0))

        wait("sync", "gut")
        wait("sync", "ccw", 3)
        for i in range(NID):
            op("sync", lambda E, i=i: E.dma_start_transpose(
                AR_W[:, i * 1024:(i + 1) * 1024], wdq_d.ap()[:, i * P:(i + 1) * P]), sem="wdl", k=16)
        HQOFF = NID * 1024
        wait("sync", "hst0")
        wait("sync", "hst1")
        for b in range(4):
            wait("sync", f"yb{b}")

        def hqsl(i, par):
            off = HQOFF + par * NID * 512 + i * 512
            return AR_W[:, off:off + 512]

        wait("vector", "crv")
        MT = [logaM, aM, gxM, multM]

        def emit_loads(j):
            tt, k = divmod(j, 4)
            tpar = tt % 2
            hc = [2 * k, 2 * k + 1]
            if k == 0:
                if tt >= 2:
                    wait("sync", "dnt", 8 * tt - 8)
                for i in range(NID):
                    op("sync", lambda E, tt=tt, i=i, tpar=tpar: E.dma_start_transpose(
                        hqsl(i, tpar), hq_d[tt * 512:(tt + 1) * 512, i * P:(i + 1) * P]), sem=f"dcl{tpar}", k=16)
            wait("sync", "dnv", 2 * j)
            for s in range(2):
                op("sync", lambda E, tt=tt, hc=hc[s], s=s: E.dma_start(
                    out=xb0[:, s * 512:(s + 1) * 512], in_=y_d[hc, :, tt * 512:(tt + 1) * 512]), sem="rld", k=16)
                op("sync", lambda E, tt=tt, hc=hc[s], s=s: E.dma_start(
                    out=xb1[:, s * 512:(s + 1) * 512], in_=p_d[hc, :, tt * 512:(tt + 1) * 512]), sem="rld", k=16)
                op("sync", lambda E, tt=tt, hc=hc[s], s=s: E.dma_start(
                    out=xnb[:, s * 512:(s + 1) * 512], in_=xT[hc * P:(hc + 1) * P, tt * 512:(tt + 1) * 512]), sem="rld", k=16)

        PDs = [PS[0], PS[1], PS[3]]

        def emit_mm(j):
            tt = j // 4
            tpar = tt % 2
            PD = PDs[j % 3]
            if j == 0:
                wait("tensor", "wdl", 16 * NID)
            if j % 4 == 0:
                wait("tensor", f"dcl{tpar}", 16 * NID * (tt // 2 + 1))
            if j >= 3:
                wait("tensor", "dnv", 2 * (j - 2))
            for s in range(2):
                for i in range(NID):
                    st0, st1 = (i == 0), (i == NID - 1)
                    op("tensor", lambda E, i=i, j=j, s=s, tpar=tpar, PD=PD, st0=st0, st1=st1: E.matmul(
                        PD[:, s * 512:(s + 1) * 512],
                        AR_W[:, i * 1024 + (2 * (j % 4) + s) * P: i * 1024 + (2 * (j % 4) + s + 1) * P],
                        hqsl(i, tpar), start=st0, stop=st1), sem=("dnt" if st1 else None))

        def emit_res(j):
            PD = PDs[j % 3]
            hc = [2 * (j % 4), 2 * (j % 4) + 1]
            wait("vector", "dnt", 2 * (j + 1))
            wait("vector", "rld", 16 * 6 * (j + 1))
            if j >= 1:
                wait("vector", "trt", 8 * j)
            for s in range(2):
                op("vector", lambda E, s=s, hc=hc[s], cmine=cmine: E.tensor_copy(
                    cmq[s][:], cmine[:, hc:hc + 1]), drain=(s == 1))
            for s in range(2):
                op("vector", lambda E, s=s: E.scalar_tensor_tensor(
                    res1[s](), xb1[:, s * 512:(s + 1) * 512], cmq[s][:],
                    xb0[:, s * 512:(s + 1) * 512], OP.mult, OP.add), drain=(s == 1))
            for s in range(2):
                op("vector", lambda E, s=s, PD=PD: E.scalar_tensor_tensor(
                    res2[s](), PD[:, s * 512:(s + 1) * 512], f32(1.0), res1[s](), OP.mult, OP.add), drain=(s == 1))
            for s in range(2):
                op("vector", lambda E, s=s: E.tensor_tensor(
                    otc[s](), res2[s](), xnb[:, s * 512:(s + 1) * 512], OP.add),
                   sem=("dnv" if s == 1 else None), k=2, drain=(s == 1))

        def emit_tr(j):
            wait("tensor", "dnv", 2 * (j + 1))
            if j >= 1:
                wait("tensor", "otv", 8 * j)
            for s in range(2):
                for blk in range(4):
                    op("tensor", lambda E, s=s, blk=blk: E.transpose(
                        PS[2][:, s * 512 + blk * P:s * 512 + (blk + 1) * P],
                        lnscr[:, s * 512 + blk * P:s * 512 + (blk + 1) * P], idf[:]),
                       sem=("trt" if (s == 1 and blk == 3) else None), k=8)

        def emit_otv(j):
            tt = j // 4
            hc = [2 * (j % 4), 2 * (j % 4) + 1]
            wait("vector", "trt", 8 * (j + 1))
            if j % 4 == 0 and tt >= 1:
                wait("vector", "ost", 64 * tt)
            for blk in range(4):
                for s in range(2):
                    op("vector", lambda E, blk=blk, s=s, hc=hc[s]: E.tensor_copy(
                        MT[blk][:, hc * P:(hc + 1) * P], PS[2][:, s * 512 + blk * P:s * 512 + (blk + 1) * P]),
                       sem=("otv" if (blk == 3 and s == 1) else None), k=8,
                       drain=(blk == 3 and s == 1))
            if j % 4 == 3:
                wait("gpsimd", "otv", cur("otv"))
                for blk in range(4):
                    op("gpsimd", lambda E, tt=tt, blk=blk: E.dma_start(
                        out=out[tt * 512 + blk * P: tt * 512 + (blk + 1) * P, :],
                        in_=MT[blk][:, 0:1024]), sem="ost", k=16)

        NJP = 16
        for j in range(NJP):
            emit_loads(j)
        for j in range(NJP):
            emit_mm(j)
            if j >= 1:
                emit_tr(j - 1)
            emit_res(j)
            if j >= 1:
                emit_otv(j - 1)
        emit_tr(NJP - 1)
        emit_otv(NJP - 1)

        wait("sync", "ost")
        wait("gpsimd", "ost")

        for (weng, wsem, wv, pos) in WAITLOG:
            assert wv <= cnt[wsem], (
                f"wait({weng}, {wsem}, {wv}) exceeds final total {cnt[wsem]} (hang)")

        block = ctx.enter_context(nc.Block())

        @block.sync
        def _(E):
            for fn in sched["sync"]:
                fn(E)

        @block.vector
        def _(E):
            for fn in sched["vector"]:
                fn(E)

        @block.scalar
        def _(E):
            for fn in sched["scalar"]:
                fn(E)

        @block.tensor
        def _(E):
            for fn in sched["tensor"]:
                fn(E)

        @block.gpsimd
        def _(E):
            for fn in sched["gpsimd"]:
                fn(E)

    return nc


_NC_CACHE = [None]


def kernel(L0_attn_x, L0_attn_a_param, L0_attn_w_in, L0_attn_w_a,
           L0_ffn_w_gate, L0_ffn_w_up, L0_ffn_w_down):
    x = np.asarray(L0_attn_x, dtype=np.float32)
    a_param = np.asarray(L0_attn_a_param, dtype=np.float32)
    w_in = np.ascontiguousarray(np.asarray(L0_attn_w_in, dtype=np.float32))
    w_a = np.ascontiguousarray(np.asarray(L0_attn_w_a, dtype=np.float32))
    wg = np.asarray(L0_ffn_w_gate, dtype=np.float32)
    wu = np.asarray(L0_ffn_w_up, dtype=np.float32)
    wd = np.asarray(L0_ffn_w_down, dtype=np.float32)

    if _NC_CACHE[0] is None:
        _NC_CACHE[0] = build_nc()
    nc = _NC_CACHE[0]

    apg = np.ascontiguousarray(a_param.reshape(NG, P).T)
    identf = np.eye(P, dtype=np.float32)
    in_maps = []
    for c in range(NCORE):
        s, k = divmod(c, 4)
        xs = np.ascontiguousarray(x[s, k * TOK:(k + 1) * TOK, :])
        sel = np.zeros((P, NG), np.float32)
        sel[:, c] = 1.0
        in_maps.append({
            "x_tm": xs,
            "xT": np.ascontiguousarray(xs.T),
            "wg_sl": np.ascontiguousarray(wg[c * WGU:(c + 1) * WGU, :]),
            "wu_sl": np.ascontiguousarray(wu[c * WGU:(c + 1) * WGU, :]),
            "wd_sl": np.ascontiguousarray(wd[c * P:(c + 1) * P, :]),
            "w_in": w_in,
            "w_a": w_a,
            "apg": apg,
            "selcol": sel,
            "identf": identf,
        })
    res = run_bass_kernel_spmd(nc, in_maps, core_ids=list(range(NCORE)))
    outp = np.empty((2, 8192, D), np.float32)
    for c in range(NCORE):
        s, k = divmod(c, 4)
        outp[s, k * TOK:(k + 1) * TOK, :] = res.results[c]["out"]
    return outp
